# version 09
# baseline (speedup 1.0000x reference)
import contextlib
import math

import numpy as np
import ml_dtypes

import concourse.bass as bass
import concourse.mybir as mybir
from concourse.bass import IndirectOffsetOnAxis
from concourse.bass_utils import run_bass_kernel_spmd

F32 = mybir.dt.float32
BF16 = mybir.dt.bfloat16
U32 = mybir.dt.uint32
AF = mybir.ActivationFunctionType
ALU = mybir.AluOpType
AX = mybir.AxisListType

T = 2048
D = 1024
DC = 8
CH = 1024
SUB = 512
HSH = 512
NE = 8
NPAIR = 16
M = 2048
ML = 256
EPS = 1e-5
LAMBDA_INIT = 0.2
MS = 180.0

IDC = 0
LIC = 128
MAC = 256
MBW = 384
ONC = 1280
SELC = 1408
OH8 = 1536
LABC = 1408
IOC = 0
EBC = 8
GBC = 16
LQ1, LK1, LQ2, LK2 = 32, 64, 96, 128
EPSC = 160
SB1, SB2, SB3, SLG, SLB, SELB, GAM, SELG = 0, 4, 8, 16, 20, 24, 28, 32
DMA_STEP = 16


class KB:
    ENGS = ("te", "act", "dve", "gp", "sy")
    CHS = ("te", "act", "dve", "gp", "sy", "gd", "sd", "yd")
    DMA_CHS = ("gd", "sd", "yd")
    SETOF = {AF.Exp: "nlexp", AF.Ln: "nlexp", AF.Sigmoid: "sig",
             AF.Identity: None, AF.Copy: None, AF.Square: None}

    def __init__(self):
        import os as _os
        self.nc = bass.Bass(
            "TRN2", target_bir_lowering=False,
            detect_race_conditions=bool(_os.environ.get("KERN_RACE")))
        self.cnt = {c: 0 for c in self.CHS}
        self.ops = {e: [] for e in self.ENGS}
        self.act_set = None

    def E(self, eng, fn, inc=None, wait=()):
        w = tuple((c, (self.cnt[c] if v is None else v)) for c, v in wait
                  if (self.cnt[c] if v is None else v) > 0)
        if inc is not None:
            self.cnt[inc] += DMA_STEP if inc in self.DMA_CHS else 1
        self.ops[eng].append((fn, w, inc))
        return self.cnt[inc] if inc is not None else None

    SETID = {"nlexp": 6, "sig": 2}

    def act(self, fn, func, inc="act", wait=()):
        need = self.SETOF[func]
        if need is not None and need != self.act_set:
            sid = self.SETID[need]

            def load(s, sid=sid):
                return s.add_instruction(mybir.InstLoadActFuncSet(
                    name=s.bass.get_next_instruction_name(),
                    act_func_set_id=sid, ins=[], outs=[]))
            self.E("act", load)
            self.act_set = need
        return self.E("act", fn, inc=inc, wait=wait)

    def c(self, ch):
        return self.cnt[ch]

    def runner(self, name, sems):
        ops = self.ops[name]

        def run(eng):
            waited = {}
            for fn, waits, inc in ops:
                for ch, val in waits:
                    if waited.get(ch, -1) < val:
                        eng.wait_ge(sems[ch], val)
                        waited[ch] = val
                if name == "dve":
                    eng.drain()
                inst = fn(eng)
                if inc is not None:
                    step = DMA_STEP if inc in self.DMA_CHS else 1
                    inst.then_inc(sems[inc], step)
        return run


def t_(i, n):
    return slice(i * n, (i + 1) * n)


def o_(off, i, n):
    return slice(off + i * n, off + (i + 1) * n)


def build_program():
    k = KB()
    nc = k.nc

    def din(name, shape, dt=F32):
        return nc.dram_tensor(name, shape, dt, kind="ExternalInput").ap()

    import os as _os
    _dbg = bool(_os.environ.get("KERN_DBG"))

    def dint(name, shape, dt=BF16):
        kind = ("ExternalOutput" if (_dbg and name in
                ("XNH", "SHH", "AGO", "E3E", "DBGA", "DBGB", "DBGC", "DBGD", "DBGE")) else "Internal")
        return nc.dram_tensor(name, shape, dt, kind=kind).ap()

    x_in = din("x_in", [D, T])
    w_attn = din("w_attn", [D, 3 * D])
    w_proj = din("w_proj", [D, D])
    s1t = din("s1t", [D, HSH])
    s2t = din("s2t", [D, HSH])
    s3t = din("s3t", [HSH, D])
    e1t = din("e1t", [NE, D, HSH])
    e2t = din("e2t", [NE, D, HSH])
    e3t = din("e3t", [NE, HSH, D])
    gwt = din("gwt", [D, 16])
    eb3_in = din("eb3_in", [NE, D])
    smalls = din("smalls", [128, 64])
    smalls2 = din("smalls2", [128, 64])
    constb = din("constb", [128, 1664], dt=BF16)
    constf = din("constf", [128, 168])
    constb2 = din("constb2", [8, 1024], dt=BF16)
    gidx = din("gidx", [128, 16], U32)
    ridx = din("ridx", [128, 16], U32)
    y_out = nc.dram_tensor("y_out", [D, T], F32, kind="ExternalOutput").ap()

    DBGA = dint("DBGA", [M, D]) if _dbg else None
    DBGD = dint("DBGD", [8, 4096]) if _dbg else None
    DBGE = dint("DBGE", [128, 128], dt=F32) if _dbg else None
    DBGB = dint("DBGB", [128, 8 * M]) if _dbg else None
    DBGC = dint("DBGC", [128, 16 * D]) if _dbg else None
    A_w = dint("A_w", [D, 3 * D])
    A_p = dint("A_p", [D, D])
    S1 = dint("S1", [D, HSH])
    S2 = dint("S2", [D, HSH])
    S3 = dint("S3", [HSH, D])
    E1 = dint("E1", [NE, D, HSH])
    E2 = dint("E2", [NE, D, HSH])
    E3 = dint("E3", [NE, HSH, D])
    GWb = dint("GWb", [D, 16])
    XNH = dint("XNH", [T + 1, D])
    SHH = dint("SHH", [T, D])
    RSI = dint("RSI", [M, D])
    RSO = dint("RSO", [ML, D])
    AGI = dint("AGI", [ML, 1032])
    AGO = dint("AGO", [M + 1, 1032])
    E3E = dint("E3E", [NE, D])

    es = contextlib.ExitStack()

    def sbuf(name, shape, dt=BF16):
        return es.enter_context(nc.sbuf_tensor(name, shape, dt))

    xf = sbuf("xf", [128, DC * T], F32)
    kbf = sbuf("kbf", [128, 8 * T])
    vbf = sbuf("vbf", [128, 16 * D])
    qbf = sbuf("qbf", [128, 8 * CH])
    attn = sbuf("attn", [128, 8 * CH])
    xnb = sbuf("xnb", [128, DC * SUB])
    pbuf = sbuf("pbuf", [128, 2048])
    wsb = sbuf("wsb", [128, 4096])
    tb = sbuf("tb", [128, 1024])
    tb2 = sbuf("tb2", [128, 1032])
    sfr = sbuf("sfr", [128, 1024], F32)
    sfr2 = sbuf("sfr2", [128, 520], F32)
    smal = sbuf("smal", [128, 64], F32)
    smal2 = sbuf("smal2", [128, 64], F32)
    cons = sbuf("cons", [128, 168], F32)
    conb = sbuf("conb", [128, 1664])
    cob2 = sbuf("cob2", [8, 1024])
    gidxs = sbuf("gidxs", [128, 16], U32)
    ridxs = sbuf("ridxs", [128, 16], U32)
    gat = sbuf("gat", [128, 48], F32)
    gat2 = sbuf("gat2", [128, 4], F32)
    idx8 = sbuf("idx8", [128, 8], U32)
    csb = sbuf("csb", [128, 128])
    e3es = sbuf("e3es", [8, 1024])
    svs = sbuf("svs", [8, ML])
    zrow = sbuf("zrow", [128, 1088])
    lamv = sbuf("lamv", [128, 8], F32)
    csc = sbuf("csc", [128, 8], F32)
    gpr = sbuf("gpr", [128, 2], F32)
    grow = sbuf("grow", [1, 64])
    sb3r = sbuf("sb3r", [1, 1024])
    km2 = sbuf("km2", [128, 24], F32)

    psA = es.enter_context(nc.psum_tensor("psA", [128, 3584], F32))
    psT = es.enter_context(nc.psum_tensor("psT", [128, 1024], BF16))

    def bank(i, w=512, p=128):
        return psA[0:p, i * 512: i * 512 + w]

    def ONE():
        return conb[:, ONC:ONC + 1]

    def ONER():
        return conb[0:1, ONC:ONC + 128]

    def IDT():
        return conb[:, IDC:IDC + 128]

    k.E("sy", lambda s: s.dma_start(out=cons[:], in_=constf[:]), inc="sd")
    k.E("sy", lambda s: s.dma_start(out=conb[:], in_=constb[:]), inc="sd")
    k.E("sy", lambda s: s.dma_start(out=smal[:], in_=smalls[:]), inc="sd")
    k.E("sy", lambda s: s.dma_start(out=smal2[:], in_=smalls2[:]), inc="sd")
    k.E("sy", lambda s: s.dma_start(out=gidxs[:], in_=gidx[:]), inc="sd")
    k.E("sy", lambda s: s.dma_start(out=ridxs[:], in_=ridx[:]), inc="sd")
    k.E("sy", lambda s: s.dma_start(out=cob2[:], in_=constb2[:]), inc="sd")
    for dc in range(DC):
        k.E("sy", lambda s, dc=dc: s.dma_start(
            out=xf[:, t_(dc, T)], in_=x_in[t_(dc, 128), :]), inc="sd")
    boot = k.c("sd")
    k.E("dve", lambda v: v.memset(tb2[:], 0.0), inc="dve",
        wait=[("sd", boot)])
    k.E("dve", lambda v: v.memset(tb[:], 0.0), inc="dve")
    k.E("dve", lambda v: v.memset(gpr[:], 0.0), inc="dve")
    k.E("dve", lambda v: v.memset(zrow[:], 0.0), inc="dve")
    z0 = k.c("dve")
    k.E("sy", lambda s: s.dma_start(out=XNH[T:T + 1, :], in_=tb2[0:1, 0:D]),
        inc="sd", wait=[("dve", z0)])
    k.E("sy", lambda s: s.dma_start(out=AGO[M:M + 1, :], in_=tb2[0:1, :]),
        inc="sd", wait=[("dve", z0)])

    k.E("dve", lambda v: v.tensor_tensor(
        out=sfr[0:1, 0:32], in0=cons[0:1, LQ1:LQ1 + 32],
        in1=cons[0:1, LK1:LK1 + 32], op=ALU.mult), inc="dve")
    k.E("dve", lambda v: v.tensor_tensor(
        out=sfr[0:1, 32:64], in0=cons[0:1, LQ2:LQ2 + 32],
        in1=cons[0:1, LK2:LK2 + 32], op=ALU.mult), inc="dve")
    k.E("dve", lambda v: v.reduce_sum(lamv[0:1, 0:1], sfr[0:1, 0:32],
                                      axis=AX.X), inc="dve")
    dv = k.E("dve", lambda v: v.reduce_sum(lamv[0:1, 1:2], sfr[0:1, 32:64],
                                           axis=AX.X), inc="dve")
    a = k.act(lambda s: s.activation(out=lamv[0:1, 2:4], in_=lamv[0:1, 0:2],
                                     func=AF.Exp), AF.Exp, wait=[("dve", dv)])
    k.E("dve", lambda v: v.tensor_tensor(
        out=lamv[0:1, 4:5], in0=lamv[0:1, 2:3], in1=lamv[0:1, 3:4],
        op=ALU.subtract), inc="dve", wait=[("act", a)])
    k.E("dve", lambda v: v.tensor_scalar(
        out=lamv[0:1, 5:6], in0=lamv[0:1, 4:5], scalar1=LAMBDA_INIT,
        scalar2=-1.0, op0=ALU.add, op1=ALU.mult), inc="dve")
    dv = k.E("dve", lambda v: v.tensor_copy(zrow[0:1, 1052:1053], lamv[0:1, 5:6]),
             inc="dve")
    mmn = k.E("te", lambda t: t.matmul(
        psA[:, 3072:3073], ONER(), zrow[0:1, 1052:1053], start=True, stop=True),
        inc="te", wait=[("dve", dv)])
    dv = k.E("dve", lambda v: v.tensor_copy(lamv[:, 6:7], psA[:, 3072:3073]),
             inc="dve", wait=[("te", mmn)])
    k.E("dve", lambda v: v.tensor_copy(csb[0:64, :],
                                       conb[0:64, IDC:IDC + 128]), inc="dve")
    k.E("dve", lambda v: v.tensor_scalar(
        out=csb[64:128, :], in0=conb[64:128, LIC:LIC + 128],
        scalar1=lamv[64:128, 6:7], scalar2=None, op0=ALU.mult), inc="dve")
    k.E("dve", lambda v: v.tensor_scalar(
        out=gpr[0:64, 0:1], in0=smal[0:64, GAM:GAM + 1], scalar1=0.8,
        scalar2=None, op0=ALU.mult), inc="dve")
    dv = k.E("dve", lambda v: v.tensor_copy(tb[:, 0:1], gpr[:, 0:1]),
             inc="dve")
    tr = k.E("te", lambda t: t.transpose(psT[:, 0:128], tb[:, 0:128], IDT()),
             inc="te", wait=[("dve", dv)])
    k.E("dve", lambda v: v.tensor_copy(grow[0:1, 0:64], psT[0:1, 0:64]),
        inc="dve", wait=[("te", tr)])
    for dc in range(DC):
        dv = k.E("dve", lambda v, dc=dc: v.tensor_copy(
            tb[:, 0:1], smal[:, SB3 + dc:SB3 + dc + 1]), inc="dve",
            wait=[("te", None)])
        tr = k.E("te", lambda t: t.transpose(psT[:, 0:128], tb[:, 0:128],
                                             IDT()),
                 inc="te", wait=[("dve", dv)])
        k.E("dve", lambda v, dc=dc: v.tensor_copy(
            sb3r[0:1, t_(dc, 128)], psT[0:1, 0:128]), inc="dve",
            wait=[("te", tr)])
    consts_ready_dve = k.c("dve")

    jobs = []

    def addj(src, dst, rows, cols, gfold=False):
        for r in range(rows // 128):
            for c0 in range(0, cols, 2048):
                cw = min(2048, cols - c0)
                jobs.append((src[t_(r, 128), c0:c0 + cw],
                             dst[t_(r, 128), c0:c0 + cw], cw,
                             (r % 4) if gfold else None))

    addj(s1t, S1, D, HSH)
    addj(s2t, S2, D, HSH)
    addj(s3t, S3, HSH, D)
    addj(gwt, GWb, D, 16)
    for e in range(NE):
        addj(e1t[e], E1[e], D, HSH)
        addj(e2t[e], E2[e], D, HSH)
    for e in range(NE):
        addj(e3t[e], E3[e], HSH, D)
    addj(w_attn, A_w, D, 3 * D)
    addj(w_proj, A_p, D, D)

    half = 0
    sd_prev = [0, 0]
    pcm = {}
    for j, (src, dst, cw, gf) in enumerate(jobs):
        h = half
        half ^= 1
        buf = wsb[:, h * 2048: h * 2048 + cw]
        g = k.E("gp", lambda gp, b=buf, s=src: gp.dma_start(out=b, in_=s),
                inc="gd", wait=[("gd", sd_prev[h])])
        if gf is not None:
            d = k.E("dve", lambda v, b=buf, hc=gf: v.tensor_scalar(
                out=b, in0=b, scalar1=smal[:, SELG + hc:SELG + hc + 1],
                scalar2=None, op0=ALU.mult), inc="dve", wait=[("gd", g)])
            sd_prev[h] = k.E("gp", lambda gp, b=buf, dd=dst: gp.dma_start(
                out=dd, in_=b), inc="gd", wait=[("dve", d)])
        else:
            sd_prev[h] = k.E("gp", lambda gp, b=buf, dd=dst: gp.dma_start(
                out=dd, in_=b), inc="gd", wait=[("gd", g)])
        pcm[dst.tensor.name] = k.c("gd")
    precast = k.c("gd")

    SCR = {"sd": boot, "gd": 0}

    def eb3_setup():
      selb_ok = k.E("dve", lambda v: v.tensor_copy(
        zrow[:, 1040:1044], smal[:, SELB:SELB + 4]), inc="dve")
      for e in range(NE):
          gws = []
          for hc in range(4):
              g = k.E("gp", lambda gp, e=e, hc=hc: gp.dma_start(
                  out=wsb[:, hc * 1024:(hc + 1) * 1024],
                  in_=e3t[e][t_(hc, 128), :]), inc="gd",
                  wait=[("gd", precast), ("te", None)])
              gws.append(g)
          mm = None
          for hc in range(4):
              for dh in range(2):
                  last = (hc == 3 and dh == 1)
                  mm = k.E("te", lambda t, hc=hc, dh=dh: t.matmul(
                      psA[0:1, dh * 512:(dh + 1) * 512],
                      zrow[:, 1040 + hc:1041 + hc],
                      wsb[:, hc * 1024 + dh * 512:hc * 1024 + (dh + 1) * 512],
                      start=(hc == 0), stop=(hc == 3)),
                      inc="te" if last else None,
                      wait=[("gd", gws[hc]), ("dve", selb_ok)]
                      + ([("dve", None)] if hc == 0 else []))
          g2 = k.E("sy", lambda s, e=e: s.dma_start(
              out=sfr[0:1, 0:1024], in_=eb3_in[e:e + 1, :]), inc="sd")
          dv = k.E("dve", lambda v: v.tensor_tensor(
              out=tb[0:1, 0:1024], in0=psA[0:1, 0:1024],
              in1=sfr[0:1, 0:1024], op=ALU.add), inc="dve",
              wait=[("te", mm), ("sd", g2)])
          k.E("sy", lambda s, e=e: s.dma_start(
              out=E3E[e:e + 1, :], in_=tb[0:1, 0:1024]), inc="sd",
              wait=[("dve", dv)])
      k.E("sy", lambda s: s.dma_start(out=e3es[:], in_=E3E[:, :]), inc="sd",
          wait=[("sd", None)])
      k.E("sy", lambda s: s.dma_start(
        out=gwsb[:].rearrange("p (a c) -> p a c", a=8),
        in_=GWb.rearrange("(a p) c -> p a c", p=128)),
        inc="sd", wait=[("gd", pcm["GWb"])])
      setup["sd"] = k.c("sd")
      SCR["sd"] = k.c("sd")

    gwsb = sbuf("gwsb", [128, 128])
    setup = {"sd": 0, "done": False}
    gb_ok = k.E("dve", lambda v: v.tensor_copy(
        zrow[0:1, 1024:1040], cons[0:1, GBC:GBC + 16]), inc="dve")

    def ln_sub(ci, si, out_xnh):
        t0 = ci * CH + si * SUB
        mmv = None
        for dc in range(DC):
            d1 = k.E("dve", lambda v, dc=dc: v.tensor_copy(
                tb[:, 0:512], xf[:, dc * T + t0:dc * T + t0 + 512]),
                inc="dve",
                wait=[("te", None), ("sd", SCR["sd"]), ("gd", SCR["gd"])])
            d2 = k.E("dve", lambda v: v.tensor_tensor(
                out=tb[:, 512:1024], in0=tb[:, 0:512], in1=tb[:, 0:512],
                op=ALU.mult), inc="dve")
            k.E("te", lambda t, dc=dc: t.matmul(
                bank(0, p=1), ONE(), tb[:, 0:512],
                start=(dc == 0), stop=(dc == DC - 1)), inc="te",
                wait=[("dve", d1)])
            mmv = k.E("te", lambda t, dc=dc: t.matmul(
                bank(1, p=1), ONE(), tb[:, 512:1024],
                start=(dc == 0), stop=(dc == DC - 1)), inc="te",
                wait=[("dve", d2)])
        k.E("dve", lambda v: v.tensor_scalar(
            out=sfr[0:1, 0:512], in0=bank(0, p=1), scalar1=1.0 / D,
            scalar2=None, op0=ALU.mult), inc="dve",
            wait=[("te", mmv), ("act", None)])
        k.E("dve", lambda v: v.tensor_scalar(
            out=sfr[0:1, 512:1024], in0=bank(1, p=1), scalar1=1.0 / D,
            scalar2=None, op0=ALU.mult), inc="dve")
        k.E("dve", lambda v: v.tensor_tensor(
            out=sfr2[0:1, 0:512], in0=sfr[0:1, 0:512],
            in1=sfr[0:1, 0:512], op=ALU.mult), inc="dve")
        dv = k.E("dve", lambda v: v.tensor_tensor(
            out=sfr[0:1, 512:1024], in0=sfr[0:1, 512:1024],
            in1=sfr2[0:1, 0:512], op=ALU.subtract), inc="dve")
        k.act(lambda s: s.activation(
            out=sfr2[0:1, 0:512], in_=sfr[0:1, 512:1024], func=AF.Ln,
            bias=cons[0:1, EPSC:EPSC + 1]), AF.Ln, wait=[("dve", dv)])
        a2 = k.act(lambda s: s.activation(
            out=sfr[0:1, 512:1024], in_=sfr2[0:1, 0:512], func=AF.Exp,
            scale=-0.5), AF.Exp)
        k.E("dve", lambda v: v.tensor_tensor(
            out=sfr[0:1, 0:512], in0=sfr[0:1, 0:512],
            in1=sfr[0:1, 512:1024], op=ALU.mult), inc="dve",
            wait=[("act", a2)])
        k.E("dve", lambda v: v.tensor_copy(zrow[0:1, 0:512],
                                           sfr[0:1, 512:1024]), inc="dve",
            wait=[("te", None)])
        dv = k.E("dve", lambda v: v.tensor_copy(zrow[0:1, 512:1024],
                                                sfr[0:1, 0:512]), inc="dve")
        k.E("te", lambda t: t.matmul(
            bank(2), ONER(), zrow[0:1, 0:512], start=True, stop=True),
            inc="te", wait=[("dve", dv)])
        b2 = k.E("te", lambda t: t.matmul(
            bank(3), ONER(), zrow[0:1, 512:1024], start=True, stop=True),
            inc="te")
        dlast = None
        for dc in range(DC):
            k.E("dve", lambda v, dc=dc: v.tensor_tensor(
                out=sfr2[:, 0:512], in0=xf[:, dc * T + t0:dc * T + t0 + 512],
                in1=bank(2), op=ALU.mult), inc="dve", wait=[("te", b2)])
            dlast = k.E("dve", lambda v, dc=dc: v.tensor_tensor(
                out=xnb[:, t_(dc, SUB)], in0=sfr2[:, 0:512], in1=bank(3),
                op=ALU.subtract), inc="dve")
        if out_xnh:
            for q4 in range(4):
                trs = None
                for dc in range(DC):
                    trs = k.E("te", lambda t, dc=dc, q4=q4: t.transpose(
                        psT[:, t_(dc, 128)],
                        xnb[:, dc * SUB + q4 * 128:dc * SUB + (q4 + 1) * 128],
                        IDT()), inc="te" if dc == DC - 1 else None,
                        wait=[("dve", None), ("gd", SCR["gd"])])
                dvt = k.E("dve", lambda v: v.tensor_copy(
                    tb[:, 0:1024], psT[:, :]), inc="dve",
                    wait=[("te", trs), ("gd", SCR["gd"])])
                k.E("gp", lambda gp, t0=t0, q4=q4: gp.dma_start(
                    out=XNH[t0 + q4 * 128:t0 + (q4 + 1) * 128, :],
                    in_=tb[:, 0:1024]), inc="gd", wait=[("dve", dvt)])
                SCR["gd"] = k.c("gd")
        return dlast

    def ffn_shared_sub(ci, si, ln_dve):
        t0 = ci * CH + si * SUB
        for hc in range(4):
            g1 = k.E("sy", lambda sy, hc=hc: sy.dma_start(
                out=wsb[:, 0:1024].rearrange("p (a c) -> p a c", a=8),
                in_=S1[:, t_(hc, 128)].rearrange("(a p) c -> p a c", p=128)),
                inc="sd", wait=[("te", None), ("gd", pcm["S2"])])
            g2 = k.E("sy", lambda sy, hc=hc: sy.dma_start(
                out=wsb[:, 1024:2048].rearrange("p (a c) -> p a c", a=8),
                in_=S2[:, t_(hc, 128)].rearrange("(a p) c -> p a c", p=128)),
                inc="sd")
            mm1 = mm2 = None
            for dc in range(DC):
                mm1 = k.E("te", lambda t, dc=dc: t.matmul(
                    bank(0), wsb[:, t_(dc, 128)], xnb[:, t_(dc, SUB)],
                    start=(dc == 0), stop=(dc == DC - 1)),
                    inc="te" if dc == DC - 1 else None,
                    wait=[("sd", g1), ("dve", ln_dve), ("act", None),
                          ("dve", None)])
                mm2 = k.E("te", lambda t, dc=dc: t.matmul(
                    bank(1), wsb[:, 1024 + dc * 128:1152 + dc * 128],
                    xnb[:, t_(dc, SUB)],
                    start=(dc == 0), stop=(dc == DC - 1)),
                    inc="te" if dc == DC - 1 else None,
                    wait=[("sd", g2)])
            a1 = k.act(lambda s, hc=hc: s.activation(
                out=tb[:, 0:512], in_=bank(0), func=AF.Identity,
                bias=smal[:, SB1 + hc:SB1 + hc + 1]), AF.Identity,
                wait=[("te", mm1), ("sd", SCR["sd"]), ("gd", SCR["gd"])])
            a2 = k.act(lambda s, hc=hc: s.activation(
                out=tb2[:, 0:512], in_=bank(0), func=AF.Sigmoid,
                bias=smal[:, SB1 + hc:SB1 + hc + 1]), AF.Sigmoid)
            a3 = k.act(lambda s, hc=hc: s.activation(
                out=tb[:, 512:1024], in_=bank(1), func=AF.Identity,
                bias=smal[:, SB2 + hc:SB2 + hc + 1]), AF.Identity,
                wait=[("te", mm2)])
            d1 = k.E("dve", lambda v: v.tensor_tensor(
                out=tb2[:, 512:1024], in0=tb[:, 0:512], in1=tb2[:, 0:512],
                op=ALU.mult), inc="dve", wait=[("act", a3)])
            d2 = k.E("dve", lambda v, hc=hc: v.tensor_tensor(
                out=attn[:, t_(hc, 512)], in0=tb2[:, 512:1024],
                in1=tb[:, 512:1024], op=ALU.mult), inc="dve")
            d3 = k.E("dve", lambda v, hc=hc: v.tensor_tensor(
                out=tb[:, 0:512], in0=attn[:, t_(hc, 512)],
                in1=attn[:, t_(hc, 512)], op=ALU.mult), inc="dve")
            k.E("te", lambda t, hc=hc: t.matmul(
                bank(2, p=1), ONE(), attn[:, t_(hc, 512)],
                start=(hc == 0), stop=(hc == 3)), inc="te",
                wait=[("dve", d2)])
            k.E("te", lambda t, hc=hc: t.matmul(
                bank(3, p=1), ONE(), tb[:, 0:512],
                start=(hc == 0), stop=(hc == 3)), inc="te",
                wait=[("dve", d3)])
        mmstat = k.c("te")
        k.E("dve", lambda v: v.tensor_scalar(
            out=sfr[0:1, 0:512], in0=bank(2, p=1), scalar1=1.0 / HSH,
            scalar2=None, op0=ALU.mult), inc="dve",
            wait=[("te", mmstat), ("act", None)])
        k.E("dve", lambda v: v.tensor_scalar(
            out=sfr[0:1, 512:1024], in0=bank(3, p=1), scalar1=1.0 / HSH,
            scalar2=None, op0=ALU.mult), inc="dve")
        k.E("dve", lambda v: v.tensor_tensor(
            out=sfr2[0:1, 0:512], in0=sfr[0:1, 0:512],
            in1=sfr[0:1, 0:512], op=ALU.mult), inc="dve")
        dv = k.E("dve", lambda v: v.tensor_tensor(
            out=sfr[0:1, 512:1024], in0=sfr[0:1, 512:1024],
            in1=sfr2[0:1, 0:512], op=ALU.subtract), inc="dve")
        k.act(lambda s: s.activation(
            out=sfr2[0:1, 0:512], in_=sfr[0:1, 512:1024], func=AF.Ln,
            bias=cons[0:1, EPSC:EPSC + 1]), AF.Ln, wait=[("dve", dv)])
        a2 = k.act(lambda s: s.activation(
            out=sfr[0:1, 512:1024], in_=sfr2[0:1, 0:512], func=AF.Exp,
            scale=-0.5), AF.Exp)
        k.E("dve", lambda v: v.tensor_tensor(
            out=sfr[0:1, 0:512], in0=sfr[0:1, 0:512],
            in1=sfr[0:1, 512:1024], op=ALU.mult), inc="dve",
            wait=[("act", a2)])
        k.E("dve", lambda v: v.tensor_copy(zrow[0:1, 0:512],
                                           sfr[0:1, 512:1024]), inc="dve",
            wait=[("te", None)])
        dv = k.E("dve", lambda v: v.tensor_copy(zrow[0:1, 512:1024],
                                                sfr[0:1, 0:512]), inc="dve")
        k.E("te", lambda t: t.matmul(
            bank(2), ONER(), zrow[0:1, 0:512], start=True, stop=True),
            inc="te", wait=[("dve", dv)])
        b2 = k.E("te", lambda t: t.matmul(
            bank(3), ONER(), zrow[0:1, 512:1024], start=True, stop=True),
            inc="te")
        dlast = None
        for hc in range(4):
            k.E("dve", lambda v, hc=hc: v.tensor_tensor(
                out=sfr2[:, 0:512], in0=attn[:, t_(hc, 512)],
                in1=bank(2), op=ALU.mult), inc="dve", wait=[("te", b2)])
            k.E("dve", lambda v: v.tensor_tensor(
                out=sfr2[:, 0:512], in0=sfr2[:, 0:512], in1=bank(3),
                op=ALU.subtract), inc="dve")
            dlast = k.E("dve", lambda v, hc=hc: v.tensor_scalar(
                out=attn[:, t_(hc, 512)], in0=sfr2[:, 0:512],
                scalar1=smal[:, SLG + hc:SLG + hc + 1],
                scalar2=smal[:, SLB + hc:SLB + hc + 1],
                op0=ALU.mult, op1=ALU.add), inc="dve")
        gws = []
        for hc in range(4):
            g = k.E("sy", lambda sy, hc=hc: sy.dma_start(
                out=wsb[:, t_(hc, 1024)], in_=S3[t_(hc, 128), :]),
                inc="sd", wait=[("te", None), ("gd", pcm["S3"])])
            gws.append(g)
        for tt4 in range(4):
            mm = None
            for hc in range(4):
                for dh in range(2):
                    mm = k.E("te", lambda t, hc=hc, tt4=tt4, dh=dh: t.matmul(
                        psA[:, 2048 + dh * 512:2048 + (dh + 1) * 512],
                        attn[:, o_(hc * 512, tt4, 128)],
                        wsb[:, hc * 1024 + dh * 512:
                            hc * 1024 + (dh + 1) * 512],
                        start=(hc == 0), stop=False),
                        inc=None,
                        wait=[("dve", dlast), ("sd", gws[hc]),
                              ("act", None)])
            for dh in range(2):
                mm = k.E("te", lambda t, dh=dh: t.matmul(
                    psA[:, 2048 + dh * 512:2048 + (dh + 1) * 512],
                    ONER(), sb3r[0:1, t_(dh, 512)],
                    start=False, stop=True), inc="te" if dh == 1 else None,
                    wait=[("dve", consts_ready_dve)])
            a1 = k.act(lambda s, tt4=tt4: s.activation(
                out=tb[:, 0:1024], in_=bank(4, w=1024), func=AF.Identity),
                AF.Identity, wait=[("te", mm), ("sd", SCR["sd"])])
            k.E("sy", lambda s, t0=t0, tt4=tt4: s.dma_start(
                out=SHH[t0 + tt4 * 128:t0 + (tt4 + 1) * 128, :],
                in_=tb[:, 0:1024]), inc="sd", wait=[("act", a1)])
            SCR["sd"] = k.c("sd")
        return k.c("te")

    def moe_phase():
        MTL = ML // 128
        gth = []
        for mt in range(16):
            g = k.E("gp", lambda gp, mt=mt: gp.indirect_dma_start(
                out=vbf[:, t_(mt, D)], out_offset=None, in_=XNH[:, :],
                in_offset=IndirectOffsetOnAxis(ap=gidxs[:, mt:mt + 1],
                                               axis=0)),
                inc="gd", wait=[("gd", None), ("te", None), ("dve", None)])
            gth.append(g)
        for mt in range(16):
            k.E("gp", lambda gp, mt=mt: gp.dma_start(
                out=RSI[t_(mt, 128), :], in_=vbf[:, t_(mt, D)]),
                inc="gd", wait=[("gd", gth[mt])])
        rs_w = k.c("gd")
        k.E("gp", lambda gp: gp.collective_compute(
            "ReduceScatter", ALU.add, ins=[RSI[:, :]], outs=[RSO[:, :]],
            replica_groups=[list(range(8))]), inc="gp",
            wait=[("gd", rs_w)])
        rs_done = k.c("gp")
        for mt in range(MTL):
            k.E("gp", lambda gp, mt=mt: gp.dma_start(
                out=vbf[:, t_(mt, D)], in_=RSO[t_(mt, 128), :]),
                inc="gd", wait=[("gp", rs_done)])
        slice_gd = k.c("gd")
        for dc in range(DC):
            trs = None
            for mt in range(MTL):
                trs = k.E("te", lambda t, dc=dc, mt=mt: t.transpose(
                    psT[:, t_(mt, 128)],
                    vbf[:, mt * D + dc * 128: mt * D + (dc + 1) * 128],
                    IDT()), inc="te" if mt == MTL - 1 else None,
                    wait=[("gd", slice_gd), ("dve", None)])
            k.E("dve", lambda v, dc=dc: v.tensor_copy(
                kbf[:, t_(dc * MTL, 128 * MTL)][:, 0:ML] if False else
                kbf[:, dc * ML: dc * ML + ML], psT[:, 0:ML]), inc="dve",
                wait=[("te", trs)])
        xm_ok = k.c("dve")
        for mt in range(MTL):
            mm = None
            for dc in range(DC):
                mm = k.E("te", lambda t, dc=dc, mt=mt: t.matmul(
                    bank(0, w=16), kbf[:, o_(dc * ML, mt, 128)],
                    gwsb[:, t_(dc, 16)], start=(dc == 0), stop=False),
                    inc=None,
                    wait=[("dve", xm_ok), ("sd", setup["sd"]), ("act", None),
                          ("dve", None)])
            mm = k.E("te", lambda t: t.matmul(
                bank(0, w=16), ONER(), zrow[0:1, 1024:1040],
                start=False, stop=True), inc="te", wait=[("dve", gb_ok)])
            a1 = k.act(lambda s: s.activation(
                out=gat[:, 0:16], in_=bank(0, w=16), func=AF.Sigmoid),
                AF.Sigmoid, wait=[("te", mm)])
            k.E("dve", lambda v: v.tensor_tensor(
                out=gat[:, 16:24], in0=gat[:, 1:9],
                in1=cons[:, EBC:EBC + 8], op=ALU.add), inc="dve",
                wait=[("act", a1)])
            k.E("dve", lambda v: v.reduce_max(gat[:, 24:25], gat[:, 16:24],
                                              axis=AX.X), inc="dve")
            k.E("dve", lambda v: v.tensor_scalar(
                out=gat[:, 32:40], in0=cons[:, IOC:IOC + 8], scalar1=0.0,
                scalar2=gat[:, 24:25], op0=ALU.mult, op1=ALU.add),
                inc="dve")
            k.E("dve", lambda v: v.max_index(idx8[:], gat[:, 32:40],
                                             gat[:, 16:24]), inc="dve")
            k.E("dve", lambda v: v.tensor_copy(gat[:, 25:26],
                                               idx8[:, 0:1]), inc="dve")
            k.E("dve", lambda v: v.tensor_scalar(
                out=gat[:, 32:40], in0=cons[:, IOC:IOC + 8],
                scalar1=gat[:, 25:26], scalar2=None, op0=ALU.is_equal),
                inc="dve")
            k.E("dve", lambda v: v.tensor_tensor(
                out=gat[:, 40:48], in0=gat[:, 32:40], in1=gat[:, 1:9],
                op=ALU.mult), inc="dve")
            k.E("dve", lambda v: v.reduce_sum(gat[:, 26:27], gat[:, 40:48],
                                              axis=AX.X), inc="dve")
            dv = k.E("dve", lambda v: v.tensor_tensor(
                out=gat[:, 27:28], in0=gat[:, 0:1], in1=gat[:, 26:27],
                op=ALU.subtract), inc="dve")
            a2 = k.act(lambda s: s.activation(
                out=gat[:, 28:29], in_=gat[:, 27:28], func=AF.Sigmoid),
                AF.Sigmoid, wait=[("dve", dv)])
            k.E("dve", lambda v: v.tensor_scalar(
                out=gat[:, 29:30], in0=gat[:, 28:29], scalar1=-1.0,
                scalar2=1.0, op0=ALU.mult, op1=ALU.add), inc="dve",
                wait=[("act", a2)])
            k.E("dve", lambda v, mt=mt: v.tensor_scalar(
                out=gat2[:, mt:mt + 1], in0=gat[:, 29:30], scalar1=-1.0,
                scalar2=None, op0=ALU.mult), inc="dve")
            dv = k.E("dve", lambda v: v.tensor_scalar(
                out=gat[:, 40:48], in0=gat[:, 32:40],
                scalar1=gat[:, 29:30], scalar2=None, op0=ALU.mult),
                inc="dve")
            dv = k.E("dve", lambda v: v.tensor_copy(tb[:, 0:8],
                                                    gat[:, 40:48]),
                     inc="dve",
                     wait=[("sd", SCR["sd"]), ("gd", SCR["gd"])])
            tr = k.E("te", lambda t: t.transpose(
                psT[:, 0:128], tb[:, 0:128], IDT()), inc="te",
                wait=[("dve", dv)])
            k.E("dve", lambda v, mt=mt: v.tensor_copy(
                svs[:, t_(mt, 128)], psT[0:8, 0:128]), inc="dve",
                wait=[("te", tr)])
        gating_done = k.c("dve")
        m0 = 0
        ZS = ML
        stat_mm = None
        for e in range(NE):
            for hc in range(4):
                hh = (hc % 2) * 2048
                g1 = k.E("sy", lambda sy, e=e, hc=hc, hh=hh: sy.dma_start(
                    out=wsb[:, hh:hh + 1024].rearrange(
                        "p (a c) -> p a c", a=8),
                    in_=E1[e][:, t_(hc, 128)].rearrange(
                        "(a p) c -> p a c", p=128)),
                    inc="sd", wait=[("te", None), ("gd", pcm["E2"])])
                g2 = k.E("sy", lambda sy, e=e, hc=hc, hh=hh: sy.dma_start(
                    out=wsb[:, hh + 1024:hh + 2048].rearrange(
                        "p (a c) -> p a c", a=8),
                    in_=E2[e][:, t_(hc, 128)].rearrange(
                        "(a p) c -> p a c", p=128)),
                    inc="sd")
                mm1 = mm2 = None
                for dc in range(DC):
                    mm1 = k.E("te", lambda t, dc=dc, hh=hh: t.matmul(
                        bank(0, w=ML), wsb[:, o_(hh, dc, 128)],
                        kbf[:, dc * ML:dc * ML + ML],
                        start=(dc == 0), stop=(dc == DC - 1)),
                        inc="te" if dc == DC - 1 else None,
                        wait=[("sd", g1), ("act", None), ("dve", None)])
                    mm2 = k.E("te", lambda t, dc=dc, hh=hh: t.matmul(
                        bank(1, w=ML), wsb[:, o_(hh + 1024, dc, 128)],
                        kbf[:, dc * ML:dc * ML + ML],
                        start=(dc == 0), stop=(dc == DC - 1)),
                        inc="te" if dc == DC - 1 else None,
                        wait=[("sd", g2)])
                a1 = k.act(lambda s, e=e, hc=hc: s.activation(
                    out=tb[:, 0:ML], in_=bank(0, w=ML), func=AF.Identity,
                    bias=smal2[:, e * 4 + hc:e * 4 + hc + 1]),
                    AF.Identity,
                    wait=[("te", mm1), ("sd", SCR["sd"]),
                          ("gd", SCR["gd"])])
                a2 = k.act(lambda s, e=e, hc=hc: s.activation(
                    out=tb2[:, 0:ML], in_=bank(0, w=ML), func=AF.Sigmoid,
                    bias=smal2[:, e * 4 + hc:e * 4 + hc + 1]),
                    AF.Sigmoid)
                a3 = k.act(lambda s, e=e, hc=hc: s.activation(
                    out=tb[:, 512:512 + ML], in_=bank(1, w=ML),
                    func=AF.Identity,
                    bias=smal2[:, 32 + e * 4 + hc:33 + e * 4 + hc]),
                    AF.Identity, wait=[("te", mm2)])
                zoff = (e * 4 + hc) * ZS
                k.E("dve", lambda v: v.tensor_tensor(
                    out=tb2[:, 512:512 + ML], in0=tb[:, 0:ML],
                    in1=tb2[:, 0:ML], op=ALU.mult), inc="dve",
                    wait=[("act", a3)])
                d2 = k.E("dve", lambda v, zoff=zoff: v.tensor_tensor(
                    out=vbf[:, zoff:zoff + ML], in0=tb2[:, 512:512 + ML],
                    in1=tb[:, 512:512 + ML], op=ALU.mult), inc="dve",
                    wait=[("te", None)])
                d3 = k.E("dve", lambda v, zoff=zoff: v.tensor_tensor(
                    out=tb[:, 0:ML], in0=vbf[:, zoff:zoff + ML],
                    in1=vbf[:, zoff:zoff + ML], op=ALU.mult), inc="dve")
                first = (e == 0 and hc == 0)
                last = (e == NE - 1 and hc == 3)
                k.E("te", lambda t, e=e, zoff=zoff, first=first,
                    last=last: t.matmul(
                        bank(2, w=ML, p=8),
                        conb[:, OH8 + e * 8:OH8 + e * 8 + 8],
                        vbf[:, zoff:zoff + ML],
                        start=first, stop=last), inc="te",
                    wait=[("dve", d2), ("act", None)])
                stat_mm = k.E("te", lambda t, e=e, first=first,
                              last=last: t.matmul(
                        bank(3, w=ML, p=8),
                        conb[:, OH8 + e * 8:OH8 + e * 8 + 8],
                        tb[:, 0:ML], start=first, stop=last), inc="te",
                    wait=[("dve", d3), ("act", None)])
        k.E("dve", lambda v: v.tensor_scalar(
            out=sfr[0:8, 0:ML], in0=bank(2, w=ML, p=8), scalar1=1.0 / HSH,
            scalar2=None, op0=ALU.mult), inc="dve",
            wait=[("te", stat_mm), ("act", None)])
        k.E("dve", lambda v: v.tensor_scalar(
            out=sfr[0:8, 512:512 + ML], in0=bank(3, w=ML, p=8),
            scalar1=1.0 / HSH, scalar2=None, op0=ALU.mult), inc="dve")
        k.E("dve", lambda v: v.tensor_tensor(
            out=sfr2[0:8, 0:ML], in0=sfr[0:8, 0:ML],
            in1=sfr[0:8, 0:ML], op=ALU.mult), inc="dve")
        dv = k.E("dve", lambda v: v.tensor_tensor(
            out=sfr[0:8, 512:512 + ML], in0=sfr[0:8, 512:512 + ML],
            in1=sfr2[0:8, 0:ML], op=ALU.subtract), inc="dve")
        k.act(lambda s: s.activation(
            out=sfr2[0:8, 0:ML], in_=sfr[0:8, 512:512 + ML],
            func=AF.Ln, bias=cons[0:8, EPSC:EPSC + 1]), AF.Ln,
            wait=[("dve", dv)])
        a2 = k.act(lambda s: s.activation(
            out=sfr[0:8, 512:512 + ML], in_=sfr2[0:8, 0:ML],
            func=AF.Exp, scale=-0.5), AF.Exp)
        k.E("dve", lambda v: v.tensor_tensor(
            out=sfr2[0:8, 0:ML], in0=sfr[0:8, 512:512 + ML],
            in1=svs[:, 0:ML], op=ALU.mult), inc="dve",
            wait=[("act", a2), ("dve", gating_done)])
        k.E("dve", lambda v: v.tensor_copy(zrow[0:8, 0:ML],
                                           sfr2[0:8, 0:ML]),
            inc="dve", wait=[("te", None)])
        k.E("dve", lambda v: v.tensor_tensor(
            out=sfr[0:8, 0:ML], in0=sfr[0:8, 0:ML],
            in1=sfr2[0:8, 0:ML], op=ALU.mult), inc="dve")
        dvr = k.E("dve", lambda v: v.tensor_copy(zrow[0:8, 512:512 + ML],
                                                 sfr[0:8, 0:ML]),
                  inc="dve")
        zn_done = None
        for e in range(NE):
            b1 = k.E("te", lambda t, e=e: t.matmul(
                bank(4, w=ML), cob2[:, t_(e, 128)], zrow[0:8, 0:ML],
                start=True, stop=True), inc="te",
                wait=[("dve", None), ("act", None)])
            b2 = k.E("te", lambda t, e=e: t.matmul(
                bank(5, w=ML), cob2[:, t_(e, 128)], zrow[0:8, 512:512 + ML],
                start=True, stop=True), inc="te")
            for hc in range(4):
                zoff = (e * 4 + hc) * ZS
                k.E("dve", lambda v, zoff=zoff: v.tensor_tensor(
                    out=sfr2[:, 0:ML], in0=vbf[:, zoff:zoff + ML],
                    in1=bank(4, w=ML), op=ALU.mult), inc="dve",
                    wait=[("te", b2)])
                k.E("dve", lambda v, zoff=zoff: v.tensor_tensor(
                    out=sfr2[:, 0:ML], in0=sfr2[:, 0:ML],
                    in1=bank(5, w=ML), op=ALU.subtract), inc="dve")
                zn_done = k.E("dve", lambda v, zoff=zoff, hc=hc:
                              v.tensor_scalar(
                    out=vbf[:, zoff:zoff + ML], in0=sfr2[:, 0:ML],
                    scalar1=smal[:, SELG + hc:SELG + hc + 1],
                    scalar2=None, op0=ALU.mult), inc="dve")
        for dh in range(2):
            evac_prev = k.c("act")
            mm_by_mt = [None] * (ML // 128)
            for e in range(NE):
                eh = (e % 2) * 2048
                g = k.E("sy", lambda sy, e=e, dh=dh, eh=eh: sy.dma_start(
                    out=wsb[:, eh:eh + 2048].rearrange(
                        "p (a c) -> p a c", a=4),
                    in_=E3[e][:, t_(dh, 512)].rearrange(
                        "(a p) c -> p a c", p=128)),
                    inc="sd", wait=[("te", None), ("gd", pcm["E3"])])
                for mt4 in range(ML // 128):
                    for hc in range(4):
                        first = (e == 0 and hc == 0)
                        mm_by_mt[mt4] = k.E(
                            "te", lambda t, e=e, mt4=mt4, hc=hc,
                            first=first, eh=eh: t.matmul(
                                bank(mt4),
                                vbf[:, o_((e * 4 + hc) * ZS, mt4, 128)],
                                wsb[:, o_(eh, hc, 512)],
                                start=first, stop=False),
                            inc="te" if (mt4 == ML // 128 - 1 and hc == 3)
                            else None,
                            wait=([("sd", g), ("dve", zn_done),
                                   ("act", evac_prev)] if first or hc == 0
                                  else [("sd", g)]))
            for mt4 in range(ML // 128):
                mm_by_mt[mt4] = k.E("te", lambda t, mt4=mt4, dh=dh:
                                    t.matmul(
                        bank(mt4), svs[:, t_(mt4, 128)],
                        e3es[:, t_(dh, 512)], start=False, stop=True),
                    inc="te", wait=[("sd", setup["sd"])])
            for mt4 in range(ML // 128):
                a1 = k.act(lambda s, mt4=mt4: s.activation(
                    out=tb[:, 0:512], in_=bank(mt4), func=AF.Identity),
                    AF.Identity,
                    wait=[("te", mm_by_mt[mt4]), ("sd", SCR["sd"])])
                k.E("sy", lambda s, mt4=mt4, dh=dh: s.dma_start(
                    out=AGI[t_(mt4, 128), t_(dh, 512)],
                    in_=tb[:, 0:512]), inc="sd",
                    wait=[("act", a1), ("gd", None)])
                SCR["sd"] = k.c("sd")
        for mt4 in range(ML // 128):
            dv = k.E("dve", lambda v, mt4=mt4: v.tensor_copy(
                tb2[:, 0:1], gat2[:, mt4:mt4 + 1]), inc="dve",
                wait=[("sd", SCR["sd"])])
            dv2 = k.E("dve", lambda v: v.memset(tb2[:, 1:8], 0.0),
                      inc="dve")
            k.E("sy", lambda s, mt4=mt4: s.dma_start(
                out=AGI[t_(mt4, 128), 1024:1032],
                in_=tb2[:, 0:8]), inc="sd", wait=[("dve", dv2)])
            SCR["sd"] = k.c("sd")
        ag_w = k.c("sd")
        k.E("gp", lambda gp: gp.collective_compute(
            "AllGather", ALU.bypass, ins=[AGI[:, :]],
            outs=[AGO[0:M, :]], replica_groups=[list(range(8))]),
            inc="gp", wait=[("sd", ag_w), ("gd", None)])

    def corrections():
        ag_done = k.c("gp")
        for tt in range(16):
            g1 = k.E("sy", lambda s, tt=tt: s.dma_start(
                out=tb[:, 0:1024], in_=SHH[t_(tt, 128), :]), inc="sd",
                wait=[("dve", None), ("act", None)])
            g2 = k.E("gp", lambda gp, tt=tt: gp.indirect_dma_start(
                out=tb2[:], out_offset=None, in_=AGO[:, :],
                in_offset=IndirectOffsetOnAxis(ap=ridxs[:, tt:tt + 1],
                                               axis=0)),
                inc="gd", wait=[("gp", ag_done), ("dve", None)])
            dv = k.E("dve", lambda v: v.tensor_scalar(
                out=sfr2[:, 0:1], in0=tb2[:, 1024:1025], scalar1=1.0,
                scalar2=None, op0=ALU.add), inc="dve",
                wait=[("gd", g2), ("sd", g1)])
            dv = k.E("dve", lambda v: v.scalar_tensor_tensor(
                out=tb2[:, 0:1024], in0=tb[:, 0:1024],
                scalar=sfr2[:, 0:1], in1=tb2[:, 0:1024],
                op0=ALU.mult, op1=ALU.add), inc="dve")
            trs = None
            for dc in range(DC):
                trs = k.E("te", lambda t, dc=dc: t.transpose(
                    psT[:, t_(dc, 128)], tb2[:, t_(dc, 128)], IDT()),
                    inc="te" if dc == DC - 1 else None,
                    wait=[("dve", dv)])
            xfv = xf[:, :].rearrange("p (dc t) -> p dc t", dc=DC)
            psv = psT[:, :].rearrange("p (dc t) -> p dc t", dc=DC)
            k.E("dve", lambda v, tt=tt, xfv=xfv, psv=psv:
                v.scalar_tensor_tensor(
                    out=xfv[:, :, t_(tt, 128)], in0=psv[:, :, :],
                    scalar=0.5, in1=xfv[:, :, t_(tt, 128)],
                    op0=ALU.mult, op1=ALU.add), inc="dve",
                wait=[("te", trs)])

    def qkv_sub(ci, si, ln_dve):
        t0 = ci * CH + si * SUB
        for fc in range(16):
            h = fc % 2
            g = k.E("sy", lambda sy, fc=fc, h=h: sy.dma_start(
                out=wsb[:, h * 2048:h * 2048 + 1024].rearrange(
                    "p (a c) -> p a c", a=8),
                in_=A_w[:, t_(fc, 128)].rearrange("(a p) c -> p a c",
                                                  p=128)),
                inc="sd", wait=[("te", None), ("gd", pcm["A_w"])])
            mm = None
            for dc in range(DC):
                mm = k.E("te", lambda t, dc=dc, fc=fc, h=h: t.matmul(
                    bank(h), wsb[:, o_(h * 2048, dc, 128)],
                    xnb[:, t_(dc, SUB)],
                    start=(dc == 0), stop=(dc == DC - 1)),
                    inc="te" if dc == DC - 1 else None,
                    wait=[("sd", g), ("dve", ln_dve), ("dve", None)])
            if fc < 8:
                dst = qbf[:, fc * CH + si * SUB: fc * CH + si * SUB + 512]
            else:
                c0 = (fc - 8) * T + t0
                dst = kbf[:, c0:c0 + 512]
            dv = k.E("dve", lambda v, dst=dst, h=h: v.tensor_copy(
                dst, bank(h)), inc="dve", wait=[("te", mm)])
            dv2 = k.E("dve", lambda v, dst=dst: v.tensor_tensor(
                out=tb[:, 0:512], in0=dst, in1=dst, op=ALU.mult),
                inc="dve")
            mm2 = k.E("te", lambda t: t.matmul(
                bank(2, p=1), ONE(), tb[:, 0:512], start=True, stop=True),
                inc="te", wait=[("dve", dv2)])
            k.E("dve", lambda v, fc=fc, si=si: v.reduce_max(
                km2[0:1, 16 + (fc % 2):17 + (fc % 2)], bank(2, w=512, p=1),
                axis=AX.X), inc="dve", wait=[("te", mm2)])
            slot = fc
            fresh = (si == 0) if fc < 8 else (ci == 0 and si == 0)
            if fresh:
                k.E("dve", lambda v, slot=slot, fc=fc: v.tensor_copy(
                    km2[0:1, slot:slot + 1],
                    km2[0:1, 16 + (fc % 2):17 + (fc % 2)]), inc="dve")
            else:
                k.E("dve", lambda v, slot=slot, fc=fc: v.tensor_tensor(
                    out=km2[0:1, slot:slot + 1],
                    in0=km2[0:1, slot:slot + 1],
                    in1=km2[0:1, 16 + (fc % 2):17 + (fc % 2)],
                    op=ALU.max), inc="dve")
        for dh in range(2):
            g1 = k.E("sy", lambda sy, dh=dh: sy.dma_start(
                out=wsb[:, 0:2048].rearrange("p (a c) -> p a c", a=4),
                in_=A_w[0:512, o_(2048, dh, 512)].rearrange(
                    "(a p) c -> p a c", p=128)),
                inc="sd", wait=[("te", None), ("gd", pcm["A_w"])])
            g2 = k.E("sy", lambda sy, dh=dh: sy.dma_start(
                out=wsb[:, 2048:4096].rearrange("p (a c) -> p a c", a=4),
                in_=A_w[512:1024, o_(2048, dh, 512)].rearrange(
                    "(a p) c -> p a c", p=128)),
                inc="sd")
            for tt4 in range(4):
                b = 3 + (tt4 % 2)
                mm = None
                for dc in range(DC):
                    mm = k.E("te", lambda t, dc=dc, tt4=tt4, b=b: t.matmul(
                        bank(b), xnb[:, o_(dc * SUB, tt4, 128)],
                        wsb[:, t_(dc, 512)],
                        start=(dc == 0), stop=(dc == DC - 1)),
                        inc="te" if dc == DC - 1 else None,
                        wait=[("sd", g1), ("sd", g2), ("dve", None)])
                tt = t0 // 128 + tt4
                k.E("dve", lambda v, tt=tt, dh=dh, b=b: v.tensor_copy(
                    vbf[:, o_(tt * D, dh, 512)], bank(b)), inc="dve",
                    wait=[("te", mm)])
        return k.c("dve")

    def stabilizer(ci):
        dv = k.E("dve", lambda v: v.tensor_tensor(
            out=km2[0:1, 16:24], in0=km2[0:1, 0:8], in1=km2[0:1, 8:16],
            op=ALU.mult), inc="dve", wait=[("te", None)])
        k.act(lambda s: s.activation(
            out=km2[0:1, 16:24], in_=km2[0:1, 16:24], func=AF.Ln,
            bias=cons[0:1, EPSC + 1:EPSC + 2]), AF.Ln, wait=[("dve", dv)])
        a = k.act(lambda s: s.activation(
            out=km2[0:1, 16:24], in_=km2[0:1, 16:24], func=AF.Exp,
            scale=0.5), AF.Exp)
        dv = k.E("dve", lambda v: v.tensor_scalar(
            out=km2[0:1, 16:24], in0=km2[0:1, 16:24], scalar1=-1.0 / 32.0,
            scalar2=None, op0=ALU.mult), inc="dve", wait=[("act", a)])
        dv = k.E("dve", lambda v: v.tensor_copy(zrow[0:1, 1044:1052],
                                                km2[0:1, 16:24]), inc="dve",
                 wait=[("te", None)])
        mm = k.E("te", lambda t: t.matmul(
            bank(2, w=8), ONER(), zrow[0:1, 1044:1052], start=True, stop=True),
            inc="te", wait=[("dve", dv)])
        k.E("dve", lambda v: v.tensor_copy(csc[:, 0:8], bank(2, w=8, p=128)),
            inc="dve", wait=[("te", mm), ("act", None)])
        return k.c("dve")

    def attention_chunk(ci, csc_dve):
        qc0 = ci * CH
        for pr in range(NPAIR):
            fc = pr // 2
            po0 = 64 * (pr % 2)
            grp = fc
            for qs in range(2):
                qoff = qc0 + qs * SUB
                qcol = qs * SUB
                kbmax = (qoff + SUB) // 128
                act_of = {}
                te_av = {}

                def emit_scores(kb):
                    par = kb % 2
                    sb0 = par * 1024
                    diag = (kb >= kbmax - 4)
                    for hi in range(2):
                        po = po0 + 32 * hi
                        sl = psA[:, sb0 + hi * 512: sb0 + (hi + 1) * 512]
                        w = [("act", act_of.get(kb - 2, 0)), ("dve", None)]
                        if kb == 0 and hi == 0:
                            w.append(("dve", csc_dve))
                        is_last = (hi == 1 and not diag)
                        k.E("te", lambda t, fc=fc, po=po, kb=kb,
                            qoff=qoff, sl=sl, diag=diag:
                            t.matmul(
                                sl,
                                kbf[po:po + 32,
                                    fc * T + kb * 128:
                                    fc * T + (kb + 1) * 128],
                                qbf[po:po + 32,
                                    fc * CH + qoff - qc0:
                                    fc * CH + qoff - qc0 + 512],
                                start=True, stop=not diag,
                                tile_position=(po, 0)),
                            inc="te" if is_last else None, wait=w)
                        if diag:
                            qw = kb * 128 - qoff
                            k.E("te", lambda t, sl=sl, qw=qw:
                                t.matmul(
                                    sl,
                                    conb[:, MAC:MAC + 128],
                                    conb[:, MBW + 384 - qw:
                                         MBW + 384 - qw + 512],
                                    start=False, stop=True,
                                    tile_position=(0, 0)),
                                inc="te" if hi == 1 else None)
                    return k.c("te")

                def emit_exp(kb, sc_done):
                    sb0 = (kb % 2) * 1024
                    a = k.act(lambda s, grp=grp, sb0=sb0: s.activation(
                        out=pbuf[:, sb0:sb0 + 1024],
                        in_=psA[:, sb0:sb0 + 1024],
                        func=AF.Exp, scale=1.0 / 32.0,
                        bias=csc[:, grp:grp + 1]), AF.Exp,
                        wait=[("te", sc_done),
                              ("te", te_av.get(kb - 2, 0))])
                    act_of[kb] = a

                def emit_av(kb):
                    sb0 = (kb % 2) * 1024
                    a = act_of[kb]
                    first = (kb == 0)
                    lastk = (kb == kbmax - 1)
                    k.E("te", lambda t, kb=kb, pr=pr, sb0=sb0,
                        first=first, lastk=lastk: t.matmul(
                            bank(4, p=64), vbf[:, kb * D + 64 * pr:
                                               kb * D + 64 * pr + 64],
                            pbuf[:, sb0:sb0 + 512],
                            start=first, stop=lastk,
                            tile_position=(0, 0)),
                        inc=None,
                        wait=([("act", a), ("dve", None)] if first
                              else [("act", a)]))
                    k.E("te", lambda t, kb=kb, pr=pr, sb0=sb0,
                        first=first, lastk=lastk: t.matmul(
                            bank(4)[64:128, :],
                            vbf[:, kb * D + 64 * pr:
                                 kb * D + 64 * pr + 64],
                            pbuf[:, sb0 + 512:sb0 + 1024],
                            start=first, stop=lastk,
                            tile_position=(0, 64),
                            skip_group_check=True),
                        inc=None)
                    k.E("te", lambda t, sb0=sb0, first=first,
                        lastk=lastk: t.matmul(
                        bank(5, p=1), ONE(), pbuf[:, sb0:sb0 + 512],
                        start=first, stop=lastk,
                        tile_position=(0, 0)), inc=None,
                        wait=([("dve", None)] if first else []))
                    te_av[kb] = k.E("te", lambda t, sb0=sb0, first=first,
                        lastk=lastk: t.matmul(
                        psA[32:33, 2560:3072], ONE(),
                        pbuf[:, sb0 + 512:sb0 + 1024],
                        start=first, stop=lastk,
                        tile_position=(0, 32),
                        skip_group_check=True), inc="te")

                for kb in range(kbmax + 1):
                    if kb < kbmax:
                        sc = emit_scores(kb)
                        emit_exp(kb, sc)
                    if kb >= 1:
                        emit_av(kb - 1)
                av_done = k.c("te")
                k.act(lambda s: s.activation(
                    out=sfr[0:1, 0:512], in_=psA[0:1, 2560:3072],
                    func=AF.Ln), AF.Ln,
                    wait=[("te", av_done), ("dve", None)])
                k.act(lambda s: s.activation(
                    out=zrow[0:1, 0:512], in_=sfr[0:1, 0:512],
                    func=AF.Exp, scale=-1.0), AF.Exp)
                k.act(lambda s: s.activation(
                    out=sfr[32:33, 0:512], in_=psA[32:33, 2560:3072],
                    func=AF.Ln), AF.Ln)
                az = k.act(lambda s: s.activation(
                    out=zrow[32:33, 0:512], in_=sfr[32:33, 0:512],
                    func=AF.Exp, scale=-1.0), AF.Exp)
                mm = k.E("te", lambda t: t.matmul(
                    bank(6), conb[0:64, LABC:LABC + 128], zrow[0:64, 0:512],
                    start=True, stop=True, tile_position=(0, 0)),
                    inc="te", wait=[("act", az), ("dve", None)])
                dab = k.E("dve", lambda v: v.tensor_copy(tb[:, 0:512],
                                                         bank(6)),
                          inc="dve", wait=[("te", mm)])
                dt = k.E("dve", lambda v: v.tensor_tensor(
                    out=tb2[:, 0:512], in0=bank(4), in1=tb[:, 0:512],
                    op=ALU.mult), inc="dve")
                mm = k.E("te", lambda t: t.matmul(
                    bank(6), csb[:, :], tb2[:, 0:512], start=True,
                    stop=True), inc="te", wait=[("dve", dt)])
                a1 = k.act(lambda s: s.activation(
                    out=tb[0:64, 0:512], in_=bank(6, p=64),
                    func=AF.Identity), AF.Identity, wait=[("te", mm)])
                dd = k.E("dve", lambda v: v.tensor_tensor(
                    out=tb[0:64, 512:1024], in0=tb[0:64, 0:512],
                    in1=tb[0:64, 0:512], op=ALU.mult), inc="dve",
                    wait=[("act", a1)])
                mm = k.E("te", lambda t: t.matmul(
                    psA[0:1, 2560:3072], conb[0:64, ONC:ONC + 1],
                    tb[0:64, 512:1024], start=True, stop=True), inc="te",
                    wait=[("dve", dd)])
                k.act(lambda s: s.activation(
                    out=sfr2[0:1, 0:512], in_=psA[0:1, 2560:3072],
                    func=AF.Ln, scale=1.0 / 64.0,
                    bias=cons[0:1, EPSC:EPSC + 1]), AF.Ln,
                    wait=[("te", mm)])
                ag = k.act(lambda s: s.activation(
                    out=sfr[0:1, 512:1024], in_=sfr2[0:1, 0:512],
                    func=AF.Exp, scale=-0.5), AF.Exp)
                dg = k.E("dve", lambda v: v.tensor_copy(
                    zrow[0:1, 512:1024], sfr[0:1, 512:1024]), inc="dve",
                    wait=[("act", ag)])
                mm = k.E("te", lambda t: t.matmul(
                    bank(6, p=64), grow[0:1, :], zrow[0:1, 512:1024],
                    start=True, stop=True), inc="te", wait=[("dve", dg)])
                k.E("dve", lambda v, pr=pr, qcol=qcol: v.tensor_tensor(
                    out=attn[64 * (pr % 2):64 * (pr % 2) + 64,
                             (pr // 2) * CH + qcol:
                             (pr // 2) * CH + qcol + 512],
                    in0=tb[0:64, 0:512], in1=bank(6, p=64), op=ALU.mult),
                    inc="dve", wait=[("te", mm)])
        return k.c("dve")

    def proj_chunk(ci, attn_dve):
        for si in range(2):
            for dc in range(DC):
                h = dc % 2
                g = k.E("sy", lambda sy, dc=dc, h=h: sy.dma_start(
                    out=wsb[:, h * 2048:h * 2048 + 1024].rearrange(
                        "p (a c) -> p a c", a=8),
                    in_=A_p[:, t_(dc, 128)].rearrange(
                        "(a p) c -> p a c", p=128)),
                    inc="sd", wait=[("te", None), ("gd", pcm["A_p"])])
                mm = None
                for ac in range(8):
                    mm = k.E("te", lambda t, ac=ac, si=si, h=h: t.matmul(
                        bank(h), wsb[:, o_(h * 2048, ac, 128)],
                        attn[:, o_(ac * CH, si, 512)],
                        start=(ac == 0), stop=(ac == 7)),
                        inc="te" if ac == 7 else None,
                        wait=[("sd", g), ("dve", attn_dve), ("dve", None)])
                k.E("dve", lambda v, dc=dc, ci=ci, si=si, h=h:
                    v.scalar_tensor_tensor(
                        out=xf[:, dc * T + ci * CH + si * 512:
                               dc * T + ci * CH + (si + 1) * 512],
                        in0=bank(h), scalar=1.0,
                        in1=xf[:, dc * T + ci * CH + si * 512:
                               dc * T + ci * CH + (si + 1) * 512],
                        op0=ALU.mult, op1=ALU.add), inc="dve",
                    wait=[("te", mm)])
        return k.c("dve")

    import os
    nphase = int(os.environ.get("KERN_PHASES", "3"))
    for ci in range(2):
        for si in range(2):
            lnd = ln_sub(ci, si, out_xnh=True)
            ffn_shared_sub(ci, si, lnd)
    if not setup["done"]:
        eb3_setup()
        setup["done"] = True
    moe_phase()
    corrections()

    if nphase >= 2:
        for ci in range(2):
            for si in range(2):
                lnd = ln_sub(ci, si, out_xnh=False)
                qkv_sub(ci, si, lnd)
            cdve = stabilizer(ci)
            adve = attention_chunk(ci, cdve)
            proj_chunk(ci, adve)

    if nphase >= 3:
        for ci in range(2):
            for si in range(2):
                lnd = ln_sub(ci, si, out_xnh=True)
                ffn_shared_sub(ci, si, lnd)
        moe_phase()
        corrections()

    for dc in range(DC):
        k.E("sy", lambda s, dc=dc: s.dma_start(
            out=y_out[t_(dc, 128), :], in_=xf[:, t_(dc, T)]), inc="yd",
            wait=[("dve", None), ("te", None), ("act", None)])
    ydone = k.c("yd")
    k.E("sy", lambda s: None, wait=[("yd", ydone)])

    sems = {}
    with contextlib.ExitStack() as ses:
        for ch in KB.CHS:
            sems[ch] = ses.enter_context(nc.semaphore(f"s_{ch}"))
        with nc.Block() as block:
            block.tensor(k.runner("te", sems))
            block.scalar(k.runner("act", sems))
            block.vector(k.runner("dve", sems))
            block.gpsimd(k.runner("gp", sems))
            block.sync(k.runner("sy", sems))
        es.close()
    return nc


def _consts_host():
    c = np.zeros((128, 1664), np.float32)
    p = np.arange(128)
    c[:, IDC:IDC + 128][p, p] = 1.0
    c[64 + np.arange(64), LIC + np.arange(64)] = 1.0
    kp = p[:, None]
    key = np.arange(128)[None, :]
    c[:, MAC:MAC + 128] = np.where(kp <= key, -MS, 0.0)
    r = np.arange(896)[None, :] - 384
    c[:, MBW:MBW + 896] = np.where(kp > r, MS, 0.0)
    c[:, ONC:ONC + 128] = 1.0

    for e in range(8):
        c[:, OH8 + e * 8 + e] = 1.0
    c[0, LABC:LABC + 64] = 1.0
    c[32, LABC + 64:LABC + 128] = 1.0
    return c.astype(ml_dtypes.bfloat16)


def _constb2_host():
    c = np.zeros((8, 1024), np.float32)
    for e in range(8):
        c[e, e * 128:(e + 1) * 128] = 1.0
    return c.astype(ml_dtypes.bfloat16)


def stage_inputs(inp):
    n = 8
    x = np.asarray(inp["L0_attn_x"])
    w_attn = np.ascontiguousarray(np.asarray(inp["L0_attn_w_attn"]))
    w_proj = np.ascontiguousarray(np.asarray(inp["L0_attn_w_proj"]))
    s1t = np.ascontiguousarray(np.asarray(inp["L0_ffn_sw1"]).T)
    s2t = np.ascontiguousarray(np.asarray(inp["L0_ffn_sw2"]).T)
    s3t = np.ascontiguousarray(np.asarray(inp["L0_ffn_sw3"]).T)
    e1t = np.ascontiguousarray(np.asarray(inp["L0_ffn_ew1"]).transpose(0, 2, 1))
    e2t = np.ascontiguousarray(np.asarray(inp["L0_ffn_ew2"]).transpose(0, 2, 1))
    e3t = np.ascontiguousarray(np.asarray(inp["L0_ffn_ew3"]).transpose(0, 2, 1))
    gwt = np.zeros((D, 16), np.float32)
    gwt[:, 0] = np.asarray(inp["L0_ffn_sg_w"])[0]
    gwt[:, 1:9] = np.asarray(inp["L0_ffn_eg_w"]).T
    eb3_in = np.ascontiguousarray(np.asarray(inp["L0_ffn_eb3"]))

    smalls = np.zeros((128, 64), np.float32)
    for hc in range(4):
        smalls[:, SB1 + hc] = np.asarray(inp["L0_ffn_sb1"])[t_(hc, 128)]
        smalls[:, SB2 + hc] = np.asarray(inp["L0_ffn_sb2"])[t_(hc, 128)]
        smalls[:, SLG + hc] = np.asarray(inp["L0_ffn_s_ln_g"])[t_(hc, 128)]
        smalls[:, SLB + hc] = np.asarray(inp["L0_ffn_s_ln_b"])[t_(hc, 128)]
        smalls[:, SELB + hc] = np.asarray(inp["L0_ffn_sel_ln_b"])[
            t_(hc, 128)]
        smalls[:, SELG + hc] = np.asarray(inp["L0_ffn_sel_ln_g"])[
            t_(hc, 128)]
    for dc in range(8):
        smalls[:, SB3 + dc] = np.asarray(inp["L0_ffn_sb3"])[t_(dc, 128)]
    smalls[0:64, GAM] = np.asarray(inp["L0_attn_gamma"])

    smalls2 = np.zeros((128, 64), np.float32)
    eb1 = np.asarray(inp["L0_ffn_eb1"])
    eb2 = np.asarray(inp["L0_ffn_eb2"])
    for e in range(8):
        for hc in range(4):
            smalls2[:, e * 4 + hc] = eb1[e][t_(hc, 128)]
            smalls2[:, 32 + e * 4 + hc] = eb2[e][t_(hc, 128)]

    constb = _consts_host()
    constf = np.zeros((128, 168), np.float32)
    constf[:, 160] = EPS
    constf[:, 161] = 1e-12
    constf[:, IOC:IOC + 8] = np.arange(8)[None, :]
    constf[0, GBC] = np.asarray(inp["L0_ffn_sg_b"])[0]
    constf[0, GBC + 1:GBC + 9] = np.asarray(inp["L0_ffn_eg_b"])
    constf[:, EBC:EBC + 8] = np.asarray(inp["L0_ffn_exp_bias"])[None, :]
    constf[0, LQ1:LQ1 + 32] = np.asarray(inp["L0_attn_lambda_q1"])
    constf[0, LK1:LK1 + 32] = np.asarray(inp["L0_attn_lambda_k1"])
    constf[0, LQ2:LQ2 + 32] = np.asarray(inp["L0_attn_lambda_q2"])
    constf[0, LK2:LK2 + 32] = np.asarray(inp["L0_attn_lambda_k2"])
    constb2 = _constb2_host()

    index_b = np.asarray(inp["L0_ffn_index_b"]).astype(np.int64)
    index_t = np.asarray(inp["L0_ffn_index_t"]).astype(np.int64)
    gidxs_percore = []
    for c in range(n):
        gi = np.full((128, 16), T, np.uint32)
        for m in range(M):
            if index_b[m] == c:
                gi[m % 128, m // 128] = index_t[m]
        gidxs_percore.append(gi)
    rids = []
    for c in range(n):
        rev = np.full(T, T, np.int64)
        own = np.nonzero(index_b == c)[0]
        rev[index_t[own]] = own
        ri = np.zeros((128, 16), np.uint32)
        for t in range(T):
            ri[t % 128, t // 128] = rev[t]
        rids.append(ri)

    shared = dict(w_attn=w_attn, w_proj=w_proj, s1t=s1t, s2t=s2t, s3t=s3t,
                  e1t=e1t, e2t=e2t, e3t=e3t, gwt=gwt, eb3_in=eb3_in,
                  smalls=smalls, smalls2=smalls2, constb=constb,
                  constf=constf, constb2=constb2)
    in_maps = []
    for c in range(n):
        m = dict(shared)
        m["x_in"] = np.ascontiguousarray(x[c].T)
        m["ridx"] = rids[c]
        m["gidx"] = gidxs_percore[c]
        in_maps.append(m)
    return in_maps


def kernel(**inp):
    n = 8
    in_maps = stage_inputs(inp)
    nc = build_program()
    res = run_bass_kernel_spmd(nc, in_maps, core_ids=list(range(n)))
    out = np.stack([np.asarray(res.results[c]["y_out"]).T
                    for c in range(n)], axis=0)
    return out


# version 10
# speedup vs baseline: 1.0540x; 1.0540x over previous
import contextlib
import math

import numpy as np
import ml_dtypes

import concourse.bass as bass
import concourse.mybir as mybir
from concourse.bass import IndirectOffsetOnAxis
from concourse.bass_utils import run_bass_kernel_spmd

F32 = mybir.dt.float32
BF16 = mybir.dt.bfloat16
U32 = mybir.dt.uint32
AF = mybir.ActivationFunctionType
ALU = mybir.AluOpType
AX = mybir.AxisListType

T = 2048
D = 1024
DC = 8
CH = 1024
SUB = 512
HSH = 512
NE = 8
NPAIR = 16
M = 2048
ML = 256
EPS = 1e-5
LAMBDA_INIT = 0.2
MS = 180.0

IDC = 0
LIC = 128
MAC = 256
MBW = 384
ONC = 1280
SELC = 1408
OH8 = 1536
LABC = 1408
IOC = 0
EBC = 8
GBC = 16
LQ1, LK1, LQ2, LK2 = 32, 64, 96, 128
EPSC = 160
SB1, SB2, SB3, SLG, SLB, SELB, GAM, SELG = 0, 4, 8, 16, 20, 24, 28, 32
DMA_STEP = 16


class KB:
    ENGS = ("te", "act", "dve", "gp", "sy")
    CHS = ("te", "act", "dve", "gp", "sy", "gd", "sd", "yd")
    DMA_CHS = ("gd", "sd", "yd")
    SETOF = {AF.Exp: "nlexp", AF.Ln: "nlexp", AF.Sigmoid: "sig",
             AF.Identity: None, AF.Copy: None, AF.Square: None}

    def __init__(self):
        import os as _os
        self.nc = bass.Bass(
            "TRN2", target_bir_lowering=False,
            detect_race_conditions=bool(_os.environ.get("KERN_RACE")))
        self.cnt = {c: 0 for c in self.CHS}
        self.ops = {e: [] for e in self.ENGS}
        self.act_set = None

    def E(self, eng, fn, inc=None, wait=()):
        w = tuple((c, (self.cnt[c] if v is None else v)) for c, v in wait
                  if (self.cnt[c] if v is None else v) > 0)
        if inc is not None:
            self.cnt[inc] += DMA_STEP if inc in self.DMA_CHS else 1
        self.ops[eng].append((fn, w, inc))
        return self.cnt[inc] if inc is not None else None

    SETID = {"nlexp": 6, "sig": 2}

    def act(self, fn, func, inc="act", wait=()):
        need = self.SETOF[func]
        if need is not None and need != self.act_set:
            sid = self.SETID[need]

            def load(s, sid=sid):
                return s.add_instruction(mybir.InstLoadActFuncSet(
                    name=s.bass.get_next_instruction_name(),
                    act_func_set_id=sid, ins=[], outs=[]))
            self.E("act", load)
            self.act_set = need
        return self.E("act", fn, inc=inc, wait=wait)

    def c(self, ch):
        return self.cnt[ch]

    def runner(self, name, sems):
        ops = self.ops[name]

        def run(eng):
            waited = {}
            for fn, waits, inc in ops:
                for ch, val in waits:
                    if waited.get(ch, -1) < val:
                        eng.wait_ge(sems[ch], val)
                        waited[ch] = val
                if name == "dve":
                    eng.drain()
                inst = fn(eng)
                if inc is not None:
                    step = DMA_STEP if inc in self.DMA_CHS else 1
                    inst.then_inc(sems[inc], step)
        return run


def t_(i, n):
    return slice(i * n, (i + 1) * n)


def o_(off, i, n):
    return slice(off + i * n, off + (i + 1) * n)


def build_program():
    k = KB()
    nc = k.nc

    def din(name, shape, dt=F32):
        return nc.dram_tensor(name, shape, dt, kind="ExternalInput").ap()

    import os as _os
    _dbg = bool(_os.environ.get("KERN_DBG"))

    def dint(name, shape, dt=BF16):
        kind = ("ExternalOutput" if (_dbg and name in
                ("XNH", "SHH", "AGO", "E3E", "DBGA", "DBGB", "DBGC", "DBGD", "DBGE")) else "Internal")
        return nc.dram_tensor(name, shape, dt, kind=kind).ap()

    x_in = din("x_in", [D, T])
    w_attn = din("w_attn", [D, 3 * D])
    w_proj = din("w_proj", [D, D])
    s1t = din("s1t", [D, HSH])
    s2t = din("s2t", [D, HSH])
    s3t = din("s3t", [HSH, D])
    e1t = din("e1t", [NE, D, HSH])
    e2t = din("e2t", [NE, D, HSH])
    e3t = din("e3t", [NE, HSH, D])
    gwt = din("gwt", [D, 16])
    eb3_in = din("eb3_in", [NE, D])
    smalls = din("smalls", [128, 64])
    smalls2 = din("smalls2", [128, 64])
    constb = din("constb", [128, 1664], dt=BF16)
    constf = din("constf", [128, 168])
    constb2 = din("constb2", [8, 1024], dt=BF16)
    gidx = din("gidx", [128, 16], U32)
    ridx = din("ridx", [128, 16], U32)
    y_out = nc.dram_tensor("y_out", [D, T], F32, kind="ExternalOutput").ap()

    DBGA = dint("DBGA", [M, D]) if _dbg else None
    DBGD = dint("DBGD", [8, 4096]) if _dbg else None
    DBGE = dint("DBGE", [128, 128], dt=F32) if _dbg else None
    DBGB = dint("DBGB", [128, 8 * M]) if _dbg else None
    DBGC = dint("DBGC", [128, 16 * D]) if _dbg else None
    A_w = dint("A_w", [D, 3 * D])
    A_p = dint("A_p", [D, D])
    S1 = dint("S1", [D, HSH])
    S2 = dint("S2", [D, HSH])
    S3 = dint("S3", [HSH, D])
    E1 = dint("E1", [NE, D, HSH])
    E2 = dint("E2", [NE, D, HSH])
    E3 = dint("E3", [NE, HSH, D])
    GWb = dint("GWb", [D, 16])
    XNH = dint("XNH", [T + 1, D])
    SHH = dint("SHH", [T, D])
    RSI = dint("RSI", [M, D])
    RSO = dint("RSO", [ML, D])
    AGI = dint("AGI", [ML, 1032])
    AGO = dint("AGO", [M + 1, 1032])
    E3E = dint("E3E", [NE, D])

    es = contextlib.ExitStack()

    def sbuf(name, shape, dt=BF16):
        return es.enter_context(nc.sbuf_tensor(name, shape, dt))

    xf = sbuf("xf", [128, DC * T], F32)
    kbf = sbuf("kbf", [128, 8 * T])
    vbf = sbuf("vbf", [128, 16 * D])
    qbf = sbuf("qbf", [128, 8 * CH])
    attn = sbuf("attn", [128, 8 * CH])
    xnb = sbuf("xnb", [128, DC * SUB])
    pbuf = sbuf("pbuf", [128, 2048])
    wsb = sbuf("wsb", [128, 4096])
    tb = sbuf("tb", [128, 1024])
    tb2 = sbuf("tb2", [128, 1032])
    sfr = sbuf("sfr", [128, 1024], F32)
    sfr2 = sbuf("sfr2", [128, 520], F32)
    smal = sbuf("smal", [128, 64], F32)
    smal2 = sbuf("smal2", [128, 64], F32)
    cons = sbuf("cons", [128, 168], F32)
    conb = sbuf("conb", [128, 1664])
    cob2 = sbuf("cob2", [8, 1024])
    gidxs = sbuf("gidxs", [128, 16], U32)
    ridxs = sbuf("ridxs", [128, 16], U32)
    gat = sbuf("gat", [128, 48], F32)
    gat2 = sbuf("gat2", [128, 4], F32)
    idx8 = sbuf("idx8", [128, 8], U32)
    csb = sbuf("csb", [128, 128])
    e3es = sbuf("e3es", [8, 1024])
    svs = sbuf("svs", [8, ML])
    zrow = sbuf("zrow", [128, 1088])
    lamv = sbuf("lamv", [128, 8], F32)
    csc = sbuf("csc", [128, 8], F32)
    gpr = sbuf("gpr", [128, 2], F32)
    grow = sbuf("grow", [1, 64])
    sb3r = sbuf("sb3r", [1, 1024])
    km2 = sbuf("km2", [128, 24], F32)

    psA = es.enter_context(nc.psum_tensor("psA", [128, 3584], F32))
    psT = es.enter_context(nc.psum_tensor("psT", [128, 1024], BF16))

    def bank(i, w=512, p=128):
        return psA[0:p, i * 512: i * 512 + w]

    def ONE():
        return conb[:, ONC:ONC + 1]

    def ONER():
        return conb[0:1, ONC:ONC + 128]

    def IDT():
        return conb[:, IDC:IDC + 128]

    k.E("sy", lambda s: s.dma_start(out=cons[:], in_=constf[:]), inc="sd")
    k.E("sy", lambda s: s.dma_start(out=conb[:], in_=constb[:]), inc="sd")
    k.E("sy", lambda s: s.dma_start(out=smal[:], in_=smalls[:]), inc="sd")
    k.E("sy", lambda s: s.dma_start(out=smal2[:], in_=smalls2[:]), inc="sd")
    k.E("sy", lambda s: s.dma_start(out=gidxs[:], in_=gidx[:]), inc="sd")
    k.E("sy", lambda s: s.dma_start(out=ridxs[:], in_=ridx[:]), inc="sd")
    k.E("sy", lambda s: s.dma_start(out=cob2[:], in_=constb2[:]), inc="sd")
    for dc in range(DC):
        k.E("sy", lambda s, dc=dc: s.dma_start(
            out=xf[:, t_(dc, T)], in_=x_in[t_(dc, 128), :]), inc="sd")
    boot = k.c("sd")
    k.E("dve", lambda v: v.memset(tb2[:], 0.0), inc="dve",
        wait=[("sd", boot)])
    k.E("dve", lambda v: v.memset(tb[:], 0.0), inc="dve")
    k.E("dve", lambda v: v.memset(gpr[:], 0.0), inc="dve")
    k.E("dve", lambda v: v.memset(zrow[:], 0.0), inc="dve")
    z0 = k.c("dve")
    k.E("sy", lambda s: s.dma_start(out=XNH[T:T + 1, :], in_=tb2[0:1, 0:D]),
        inc="sd", wait=[("dve", z0)])
    k.E("sy", lambda s: s.dma_start(out=AGO[M:M + 1, :], in_=tb2[0:1, :]),
        inc="sd", wait=[("dve", z0)])

    k.E("dve", lambda v: v.tensor_tensor(
        out=sfr[0:1, 0:32], in0=cons[0:1, LQ1:LQ1 + 32],
        in1=cons[0:1, LK1:LK1 + 32], op=ALU.mult), inc="dve")
    k.E("dve", lambda v: v.tensor_tensor(
        out=sfr[0:1, 32:64], in0=cons[0:1, LQ2:LQ2 + 32],
        in1=cons[0:1, LK2:LK2 + 32], op=ALU.mult), inc="dve")
    k.E("dve", lambda v: v.reduce_sum(lamv[0:1, 0:1], sfr[0:1, 0:32],
                                      axis=AX.X), inc="dve")
    dv = k.E("dve", lambda v: v.reduce_sum(lamv[0:1, 1:2], sfr[0:1, 32:64],
                                           axis=AX.X), inc="dve")
    a = k.act(lambda s: s.activation(out=lamv[0:1, 2:4], in_=lamv[0:1, 0:2],
                                     func=AF.Exp), AF.Exp, wait=[("dve", dv)])
    k.E("dve", lambda v: v.tensor_tensor(
        out=lamv[0:1, 4:5], in0=lamv[0:1, 2:3], in1=lamv[0:1, 3:4],
        op=ALU.subtract), inc="dve", wait=[("act", a)])
    k.E("dve", lambda v: v.tensor_scalar(
        out=lamv[0:1, 5:6], in0=lamv[0:1, 4:5], scalar1=LAMBDA_INIT,
        scalar2=-1.0, op0=ALU.add, op1=ALU.mult), inc="dve")
    dv = k.E("dve", lambda v: v.tensor_copy(zrow[0:1, 1052:1053], lamv[0:1, 5:6]),
             inc="dve")
    mmn = k.E("te", lambda t: t.matmul(
        psA[:, 3072:3073], ONER(), zrow[0:1, 1052:1053], start=True, stop=True),
        inc="te", wait=[("dve", dv)])
    dv = k.E("dve", lambda v: v.tensor_copy(lamv[:, 6:7], psA[:, 3072:3073]),
             inc="dve", wait=[("te", mmn)])
    k.E("dve", lambda v: v.tensor_copy(csb[0:64, :],
                                       conb[0:64, IDC:IDC + 128]), inc="dve")
    k.E("dve", lambda v: v.tensor_scalar(
        out=csb[64:128, :], in0=conb[64:128, LIC:LIC + 128],
        scalar1=lamv[64:128, 6:7], scalar2=None, op0=ALU.mult), inc="dve")
    k.E("dve", lambda v: v.tensor_scalar(
        out=gpr[0:64, 0:1], in0=smal[0:64, GAM:GAM + 1], scalar1=0.8,
        scalar2=None, op0=ALU.mult), inc="dve")
    dv = k.E("dve", lambda v: v.tensor_copy(tb[:, 0:1], gpr[:, 0:1]),
             inc="dve")
    tr = k.E("te", lambda t: t.transpose(psT[:, 0:128], tb[:, 0:128], IDT()),
             inc="te", wait=[("dve", dv)])
    k.E("dve", lambda v: v.tensor_copy(grow[0:1, 0:64], psT[0:1, 0:64]),
        inc="dve", wait=[("te", tr)])
    for dc in range(DC):
        dv = k.E("dve", lambda v, dc=dc: v.tensor_copy(
            tb[:, 0:1], smal[:, SB3 + dc:SB3 + dc + 1]), inc="dve",
            wait=[("te", None)])
        tr = k.E("te", lambda t: t.transpose(psT[:, 0:128], tb[:, 0:128],
                                             IDT()),
                 inc="te", wait=[("dve", dv)])
        k.E("dve", lambda v, dc=dc: v.tensor_copy(
            sb3r[0:1, t_(dc, 128)], psT[0:1, 0:128]), inc="dve",
            wait=[("te", tr)])
    consts_ready_dve = k.c("dve")

    jobs = []

    def addj(src, dst, rows, cols, gfold=False):
        assert not gfold
        nrb = rows // 128
        g = max(1, 2048 // cols)
        if g > 1:
            for r0 in range(0, nrb, g):
                ga = min(g, nrb - r0)
                jobs.append((
                    src[r0 * 128:(r0 + ga) * 128, :].rearrange(
                        "(a p) c -> p a c", p=128),
                    dst[r0 * 128:(r0 + ga) * 128, :].rearrange(
                        "(a p) c -> p a c", p=128),
                    ga * cols, None))
        else:
            for r in range(nrb):
                for c0 in range(0, cols, 2048):
                    cw = min(2048, cols - c0)
                    jobs.append((src[t_(r, 128), c0:c0 + cw],
                                 dst[t_(r, 128), c0:c0 + cw], cw, None))

    addj(s1t, S1, D, HSH)
    addj(s2t, S2, D, HSH)
    addj(s3t, S3, HSH, D)
    addj(gwt, GWb, D, 16)
    for e in range(NE):
        addj(e1t[e], E1[e], D, HSH)
        addj(e2t[e], E2[e], D, HSH)
    for e in range(NE):
        addj(e3t[e], E3[e], HSH, D)
    addj(w_attn, A_w, D, 3 * D)
    addj(w_proj, A_p, D, D)

    half = 0
    sd_prev = [0, 0]
    pcm = {}
    for j, (src, dst, cw, gf) in enumerate(jobs):
        h = half
        half ^= 1
        buf = wsb[:, h * 2048: h * 2048 + cw]
        if len(src.shape) == 3:
            buf = buf.rearrange("p (a c) -> p a c", a=src.shape[1])
        g = k.E("gp", lambda gp, b=buf, s=src: gp.dma_start(out=b, in_=s),
                inc="gd", wait=[("gd", sd_prev[h])])
        if gf is not None:
            d = k.E("dve", lambda v, b=buf, hc=gf: v.tensor_scalar(
                out=b, in0=b, scalar1=smal[:, SELG + hc:SELG + hc + 1],
                scalar2=None, op0=ALU.mult), inc="dve", wait=[("gd", g)])
            sd_prev[h] = k.E("gp", lambda gp, b=buf, dd=dst: gp.dma_start(
                out=dd, in_=b), inc="gd", wait=[("dve", d)])
        else:
            sd_prev[h] = k.E("gp", lambda gp, b=buf, dd=dst: gp.dma_start(
                out=dd, in_=b), inc="gd", wait=[("gd", g)])
        pcm[dst.tensor.name] = k.c("gd")
    precast = k.c("gd")

    SCR = {"sd": boot, "gd": 0}

    def eb3_setup():
      selb_ok = k.E("dve", lambda v: v.tensor_copy(
        zrow[:, 1040:1044], smal[:, SELB:SELB + 4]), inc="dve")
      for e in range(NE):
          gws = []
          for hc in range(4):
              g = k.E("gp", lambda gp, e=e, hc=hc: gp.dma_start(
                  out=attn[:, hc * 1024:(hc + 1) * 1024],
                  in_=e3t[e][t_(hc, 128), :]), inc="gd",
                  wait=[("te", None), ("dve", None)])
              gws.append(g)
          mm = None
          for hc in range(4):
              for dh in range(2):
                  last = (hc == 3 and dh == 1)
                  mm = k.E("te", lambda t, hc=hc, dh=dh: t.matmul(
                      psA[0:1, dh * 512:(dh + 1) * 512],
                      zrow[:, 1040 + hc:1041 + hc],
                      attn[:, hc * 1024 + dh * 512:
                           hc * 1024 + (dh + 1) * 512],
                      start=(hc == 0), stop=(hc == 3)),
                      inc="te" if last else None,
                      wait=[("gd", gws[hc]), ("dve", selb_ok)]
                      + ([("dve", None)] if hc == 0 else []))
          g2 = k.E("sy", lambda s, e=e: s.dma_start(
              out=sfr[0:1, 0:1024], in_=eb3_in[e:e + 1, :]), inc="sd")
          dv = k.E("dve", lambda v: v.tensor_tensor(
              out=tb[0:1, 0:1024], in0=psA[0:1, 0:1024],
              in1=sfr[0:1, 0:1024], op=ALU.add), inc="dve",
              wait=[("te", mm), ("sd", g2)])
          k.E("sy", lambda s, e=e: s.dma_start(
              out=E3E[e:e + 1, :], in_=tb[0:1, 0:1024]), inc="sd",
              wait=[("dve", dv)])
      k.E("sy", lambda s: s.dma_start(out=e3es[:], in_=E3E[:, :]), inc="sd",
          wait=[("sd", None)])
      k.E("sy", lambda s: s.dma_start(
        out=gwsb[:].rearrange("p (a c) -> p a c", a=8),
        in_=GWb.rearrange("(a p) c -> p a c", p=128)),
        inc="sd", wait=[("gd", pcm["GWb"])])
      setup["sd"] = k.c("sd")
      SCR["sd"] = k.c("sd")

    gwsb = sbuf("gwsb", [128, 128])
    setup = {"sd": 0, "done": False}
    gb_ok = k.E("dve", lambda v: v.tensor_copy(
        zrow[0:1, 1024:1040], cons[0:1, GBC:GBC + 16]), inc="dve")

    def ln_sub(ci, si, out_xnh):
        t0 = ci * CH + si * SUB
        mmv = None
        for dc in range(DC):
            d1 = k.E("dve", lambda v, dc=dc: v.tensor_copy(
                tb[:, 0:512], xf[:, dc * T + t0:dc * T + t0 + 512]),
                inc="dve",
                wait=[("te", None), ("sd", SCR["sd"]), ("gd", SCR["gd"])])
            d2 = k.E("dve", lambda v: v.tensor_tensor(
                out=tb[:, 512:1024], in0=tb[:, 0:512], in1=tb[:, 0:512],
                op=ALU.mult), inc="dve")
            k.E("te", lambda t, dc=dc: t.matmul(
                bank(0, p=1), ONE(), tb[:, 0:512],
                start=(dc == 0), stop=(dc == DC - 1)), inc="te",
                wait=[("dve", d1)])
            mmv = k.E("te", lambda t, dc=dc: t.matmul(
                bank(1, p=1), ONE(), tb[:, 512:1024],
                start=(dc == 0), stop=(dc == DC - 1)), inc="te",
                wait=[("dve", d2)])
        k.E("dve", lambda v: v.tensor_scalar(
            out=sfr[0:1, 0:512], in0=bank(0, p=1), scalar1=1.0 / D,
            scalar2=None, op0=ALU.mult), inc="dve",
            wait=[("te", mmv), ("act", None)])
        k.E("dve", lambda v: v.tensor_scalar(
            out=sfr[0:1, 512:1024], in0=bank(1, p=1), scalar1=1.0 / D,
            scalar2=None, op0=ALU.mult), inc="dve")
        k.E("dve", lambda v: v.tensor_tensor(
            out=sfr2[0:1, 0:512], in0=sfr[0:1, 0:512],
            in1=sfr[0:1, 0:512], op=ALU.mult), inc="dve")
        dv = k.E("dve", lambda v: v.tensor_tensor(
            out=sfr[0:1, 512:1024], in0=sfr[0:1, 512:1024],
            in1=sfr2[0:1, 0:512], op=ALU.subtract), inc="dve")
        k.act(lambda s: s.activation(
            out=sfr2[0:1, 0:512], in_=sfr[0:1, 512:1024], func=AF.Ln,
            bias=cons[0:1, EPSC:EPSC + 1]), AF.Ln, wait=[("dve", dv)])
        a2 = k.act(lambda s: s.activation(
            out=sfr[0:1, 512:1024], in_=sfr2[0:1, 0:512], func=AF.Exp,
            scale=-0.5), AF.Exp)
        k.E("dve", lambda v: v.tensor_tensor(
            out=sfr[0:1, 0:512], in0=sfr[0:1, 0:512],
            in1=sfr[0:1, 512:1024], op=ALU.mult), inc="dve",
            wait=[("act", a2)])
        k.E("dve", lambda v: v.tensor_copy(zrow[0:1, 0:512],
                                           sfr[0:1, 512:1024]), inc="dve",
            wait=[("te", None)])
        dv = k.E("dve", lambda v: v.tensor_copy(zrow[0:1, 512:1024],
                                                sfr[0:1, 0:512]), inc="dve")
        k.E("te", lambda t: t.matmul(
            bank(2), ONER(), zrow[0:1, 0:512], start=True, stop=True),
            inc="te", wait=[("dve", dv)])
        b2 = k.E("te", lambda t: t.matmul(
            bank(3), ONER(), zrow[0:1, 512:1024], start=True, stop=True),
            inc="te")
        dlast = None
        for dc in range(DC):
            k.E("dve", lambda v, dc=dc: v.tensor_tensor(
                out=sfr2[:, 0:512], in0=xf[:, dc * T + t0:dc * T + t0 + 512],
                in1=bank(2), op=ALU.mult), inc="dve", wait=[("te", b2)])
            dlast = k.E("dve", lambda v, dc=dc: v.tensor_tensor(
                out=xnb[:, t_(dc, SUB)], in0=sfr2[:, 0:512], in1=bank(3),
                op=ALU.subtract), inc="dve")
        if out_xnh:
            for q4 in range(4):
                trs = None
                for dc in range(DC):
                    trs = k.E("te", lambda t, dc=dc, q4=q4: t.transpose(
                        psT[:, t_(dc, 128)],
                        xnb[:, dc * SUB + q4 * 128:dc * SUB + (q4 + 1) * 128],
                        IDT()), inc="te" if dc == DC - 1 else None,
                        wait=[("dve", None), ("gd", SCR["gd"])])
                dvt = k.E("dve", lambda v: v.tensor_copy(
                    tb[:, 0:1024], psT[:, :]), inc="dve",
                    wait=[("te", trs), ("gd", SCR["gd"])])
                k.E("gp", lambda gp, t0=t0, q4=q4: gp.dma_start(
                    out=XNH[t0 + q4 * 128:t0 + (q4 + 1) * 128, :],
                    in_=tb[:, 0:1024]), inc="gd", wait=[("dve", dvt)])
                SCR["gd"] = k.c("gd")
        return dlast

    def ffn_shared_sub(ci, si, ln_dve):
        t0 = ci * CH + si * SUB
        for hc in range(4):
            g1 = k.E("sy", lambda sy, hc=hc: sy.dma_start(
                out=wsb[:, 0:1024].rearrange("p (a c) -> p a c", a=8),
                in_=S1[:, t_(hc, 128)].rearrange("(a p) c -> p a c", p=128)),
                inc="sd", wait=[("te", None), ("gd", pcm["S2"])])
            g2 = k.E("sy", lambda sy, hc=hc: sy.dma_start(
                out=wsb[:, 1024:2048].rearrange("p (a c) -> p a c", a=8),
                in_=S2[:, t_(hc, 128)].rearrange("(a p) c -> p a c", p=128)),
                inc="sd")
            mm1 = mm2 = None
            for dc in range(DC):
                mm1 = k.E("te", lambda t, dc=dc: t.matmul(
                    bank(0), wsb[:, t_(dc, 128)], xnb[:, t_(dc, SUB)],
                    start=(dc == 0), stop=(dc == DC - 1)),
                    inc="te" if dc == DC - 1 else None,
                    wait=[("sd", g1), ("dve", ln_dve), ("act", None),
                          ("dve", None)])
                mm2 = k.E("te", lambda t, dc=dc: t.matmul(
                    bank(1), wsb[:, 1024 + dc * 128:1152 + dc * 128],
                    xnb[:, t_(dc, SUB)],
                    start=(dc == 0), stop=(dc == DC - 1)),
                    inc="te" if dc == DC - 1 else None,
                    wait=[("sd", g2)])
            a1 = k.act(lambda s, hc=hc: s.activation(
                out=tb[:, 0:512], in_=bank(0), func=AF.Identity,
                bias=smal[:, SB1 + hc:SB1 + hc + 1]), AF.Identity,
                wait=[("te", mm1), ("sd", SCR["sd"]), ("gd", SCR["gd"])])
            a2 = k.act(lambda s, hc=hc: s.activation(
                out=tb2[:, 0:512], in_=bank(0), func=AF.Sigmoid,
                bias=smal[:, SB1 + hc:SB1 + hc + 1]), AF.Sigmoid)
            a3 = k.act(lambda s, hc=hc: s.activation(
                out=tb[:, 512:1024], in_=bank(1), func=AF.Identity,
                bias=smal[:, SB2 + hc:SB2 + hc + 1]), AF.Identity,
                wait=[("te", mm2)])
            d1 = k.E("dve", lambda v: v.tensor_tensor(
                out=tb2[:, 512:1024], in0=tb[:, 0:512], in1=tb2[:, 0:512],
                op=ALU.mult), inc="dve", wait=[("act", a3)])
            d2 = k.E("dve", lambda v, hc=hc: v.tensor_tensor(
                out=attn[:, t_(hc, 512)], in0=tb2[:, 512:1024],
                in1=tb[:, 512:1024], op=ALU.mult), inc="dve")
            d3 = k.E("dve", lambda v, hc=hc: v.tensor_tensor(
                out=tb[:, 0:512], in0=attn[:, t_(hc, 512)],
                in1=attn[:, t_(hc, 512)], op=ALU.mult), inc="dve")
            k.E("te", lambda t, hc=hc: t.matmul(
                bank(2, p=1), ONE(), attn[:, t_(hc, 512)],
                start=(hc == 0), stop=(hc == 3)), inc="te",
                wait=[("dve", d2)])
            k.E("te", lambda t, hc=hc: t.matmul(
                bank(3, p=1), ONE(), tb[:, 0:512],
                start=(hc == 0), stop=(hc == 3)), inc="te",
                wait=[("dve", d3)])
        mmstat = k.c("te")
        k.E("dve", lambda v: v.tensor_scalar(
            out=sfr[0:1, 0:512], in0=bank(2, p=1), scalar1=1.0 / HSH,
            scalar2=None, op0=ALU.mult), inc="dve",
            wait=[("te", mmstat), ("act", None)])
        k.E("dve", lambda v: v.tensor_scalar(
            out=sfr[0:1, 512:1024], in0=bank(3, p=1), scalar1=1.0 / HSH,
            scalar2=None, op0=ALU.mult), inc="dve")
        k.E("dve", lambda v: v.tensor_tensor(
            out=sfr2[0:1, 0:512], in0=sfr[0:1, 0:512],
            in1=sfr[0:1, 0:512], op=ALU.mult), inc="dve")
        dv = k.E("dve", lambda v: v.tensor_tensor(
            out=sfr[0:1, 512:1024], in0=sfr[0:1, 512:1024],
            in1=sfr2[0:1, 0:512], op=ALU.subtract), inc="dve")
        k.act(lambda s: s.activation(
            out=sfr2[0:1, 0:512], in_=sfr[0:1, 512:1024], func=AF.Ln,
            bias=cons[0:1, EPSC:EPSC + 1]), AF.Ln, wait=[("dve", dv)])
        a2 = k.act(lambda s: s.activation(
            out=sfr[0:1, 512:1024], in_=sfr2[0:1, 0:512], func=AF.Exp,
            scale=-0.5), AF.Exp)
        k.E("dve", lambda v: v.tensor_tensor(
            out=sfr[0:1, 0:512], in0=sfr[0:1, 0:512],
            in1=sfr[0:1, 512:1024], op=ALU.mult), inc="dve",
            wait=[("act", a2)])
        k.E("dve", lambda v: v.tensor_copy(zrow[0:1, 0:512],
                                           sfr[0:1, 512:1024]), inc="dve",
            wait=[("te", None)])
        dv = k.E("dve", lambda v: v.tensor_copy(zrow[0:1, 512:1024],
                                                sfr[0:1, 0:512]), inc="dve")
        k.E("te", lambda t: t.matmul(
            bank(2), ONER(), zrow[0:1, 0:512], start=True, stop=True),
            inc="te", wait=[("dve", dv)])
        b2 = k.E("te", lambda t: t.matmul(
            bank(3), ONER(), zrow[0:1, 512:1024], start=True, stop=True),
            inc="te")
        dlast = None
        for hc in range(4):
            k.E("dve", lambda v, hc=hc: v.tensor_tensor(
                out=sfr2[:, 0:512], in0=attn[:, t_(hc, 512)],
                in1=bank(2), op=ALU.mult), inc="dve", wait=[("te", b2)])
            k.E("dve", lambda v: v.tensor_tensor(
                out=sfr2[:, 0:512], in0=sfr2[:, 0:512], in1=bank(3),
                op=ALU.subtract), inc="dve")
            dlast = k.E("dve", lambda v, hc=hc: v.tensor_scalar(
                out=attn[:, t_(hc, 512)], in0=sfr2[:, 0:512],
                scalar1=smal[:, SLG + hc:SLG + hc + 1],
                scalar2=smal[:, SLB + hc:SLB + hc + 1],
                op0=ALU.mult, op1=ALU.add), inc="dve")
        gws = []
        for hc in range(4):
            g = k.E("sy", lambda sy, hc=hc: sy.dma_start(
                out=wsb[:, t_(hc, 1024)], in_=S3[t_(hc, 128), :]),
                inc="sd", wait=[("te", None), ("gd", pcm["S3"])])
            gws.append(g)
        for tt4 in range(4):
            mm = None
            for hc in range(4):
                for dh in range(2):
                    mm = k.E("te", lambda t, hc=hc, tt4=tt4, dh=dh: t.matmul(
                        psA[:, 2048 + dh * 512:2048 + (dh + 1) * 512],
                        attn[:, o_(hc * 512, tt4, 128)],
                        wsb[:, hc * 1024 + dh * 512:
                            hc * 1024 + (dh + 1) * 512],
                        start=(hc == 0), stop=False),
                        inc=None,
                        wait=[("dve", dlast), ("sd", gws[hc]),
                              ("act", None)])
            for dh in range(2):
                mm = k.E("te", lambda t, dh=dh: t.matmul(
                    psA[:, 2048 + dh * 512:2048 + (dh + 1) * 512],
                    ONER(), sb3r[0:1, t_(dh, 512)],
                    start=False, stop=True), inc="te" if dh == 1 else None,
                    wait=[("dve", consts_ready_dve)])
            a1 = k.act(lambda s, tt4=tt4: s.activation(
                out=tb[:, 0:1024], in_=bank(4, w=1024), func=AF.Identity),
                AF.Identity, wait=[("te", mm), ("sd", SCR["sd"])])
            k.E("sy", lambda s, t0=t0, tt4=tt4: s.dma_start(
                out=SHH[t0 + tt4 * 128:t0 + (tt4 + 1) * 128, :],
                in_=tb[:, 0:1024]), inc="sd", wait=[("act", a1)])
            SCR["sd"] = k.c("sd")
        return k.c("te")

    def moe_phase():
        MTL = ML // 128
        gth = []
        for mt in range(16):
            g = k.E("gp", lambda gp, mt=mt: gp.indirect_dma_start(
                out=vbf[:, t_(mt, D)], out_offset=None, in_=XNH[:, :],
                in_offset=IndirectOffsetOnAxis(ap=gidxs[:, mt:mt + 1],
                                               axis=0)),
                inc="gd", wait=[("gd", None), ("te", None), ("dve", None)])
            gth.append(g)
        for mt in range(16):
            k.E("gp", lambda gp, mt=mt: gp.dma_start(
                out=RSI[t_(mt, 128), :], in_=vbf[:, t_(mt, D)]),
                inc="gd", wait=[("gd", gth[mt])])
        rs_w = k.c("gd")
        k.E("gp", lambda gp: gp.collective_compute(
            "ReduceScatter", ALU.add, ins=[RSI[:, :]], outs=[RSO[:, :]],
            replica_groups=[list(range(8))]), inc="gp",
            wait=[("gd", rs_w)])
        rs_done = k.c("gp")
        for mt in range(MTL):
            k.E("gp", lambda gp, mt=mt: gp.dma_start(
                out=vbf[:, t_(mt, D)], in_=RSO[t_(mt, 128), :]),
                inc="gd", wait=[("gp", rs_done)])
        slice_gd = k.c("gd")
        for dc in range(DC):
            trs = None
            for mt in range(MTL):
                trs = k.E("te", lambda t, dc=dc, mt=mt: t.transpose(
                    psT[:, t_(mt, 128)],
                    vbf[:, mt * D + dc * 128: mt * D + (dc + 1) * 128],
                    IDT()), inc="te" if mt == MTL - 1 else None,
                    wait=[("gd", slice_gd), ("dve", None)])
            k.E("dve", lambda v, dc=dc: v.tensor_copy(
                kbf[:, t_(dc * MTL, 128 * MTL)][:, 0:ML] if False else
                kbf[:, dc * ML: dc * ML + ML], psT[:, 0:ML]), inc="dve",
                wait=[("te", trs)])
        xm_ok = k.c("dve")
        for mt in range(MTL):
            mm = None
            for dc in range(DC):
                mm = k.E("te", lambda t, dc=dc, mt=mt: t.matmul(
                    bank(0, w=16), kbf[:, o_(dc * ML, mt, 128)],
                    gwsb[:, t_(dc, 16)], start=(dc == 0), stop=False),
                    inc=None,
                    wait=[("dve", xm_ok), ("sd", setup["sd"]), ("act", None),
                          ("dve", None)])
            mm = k.E("te", lambda t: t.matmul(
                bank(0, w=16), ONER(), zrow[0:1, 1024:1040],
                start=False, stop=True), inc="te", wait=[("dve", gb_ok)])
            a1 = k.act(lambda s: s.activation(
                out=gat[:, 0:16], in_=bank(0, w=16), func=AF.Sigmoid),
                AF.Sigmoid, wait=[("te", mm)])
            k.E("dve", lambda v: v.tensor_tensor(
                out=gat[:, 16:24], in0=gat[:, 1:9],
                in1=cons[:, EBC:EBC + 8], op=ALU.add), inc="dve",
                wait=[("act", a1)])
            k.E("dve", lambda v: v.reduce_max(gat[:, 24:25], gat[:, 16:24],
                                              axis=AX.X), inc="dve")
            k.E("dve", lambda v: v.tensor_scalar(
                out=gat[:, 32:40], in0=cons[:, IOC:IOC + 8], scalar1=0.0,
                scalar2=gat[:, 24:25], op0=ALU.mult, op1=ALU.add),
                inc="dve")
            k.E("dve", lambda v: v.max_index(idx8[:], gat[:, 32:40],
                                             gat[:, 16:24]), inc="dve")
            k.E("dve", lambda v: v.tensor_copy(gat[:, 25:26],
                                               idx8[:, 0:1]), inc="dve")
            k.E("dve", lambda v: v.tensor_scalar(
                out=gat[:, 32:40], in0=cons[:, IOC:IOC + 8],
                scalar1=gat[:, 25:26], scalar2=None, op0=ALU.is_equal),
                inc="dve")
            k.E("dve", lambda v: v.tensor_tensor(
                out=gat[:, 40:48], in0=gat[:, 32:40], in1=gat[:, 1:9],
                op=ALU.mult), inc="dve")
            k.E("dve", lambda v: v.reduce_sum(gat[:, 26:27], gat[:, 40:48],
                                              axis=AX.X), inc="dve")
            dv = k.E("dve", lambda v: v.tensor_tensor(
                out=gat[:, 27:28], in0=gat[:, 0:1], in1=gat[:, 26:27],
                op=ALU.subtract), inc="dve")
            a2 = k.act(lambda s: s.activation(
                out=gat[:, 28:29], in_=gat[:, 27:28], func=AF.Sigmoid),
                AF.Sigmoid, wait=[("dve", dv)])
            k.E("dve", lambda v: v.tensor_scalar(
                out=gat[:, 29:30], in0=gat[:, 28:29], scalar1=-1.0,
                scalar2=1.0, op0=ALU.mult, op1=ALU.add), inc="dve",
                wait=[("act", a2)])
            k.E("dve", lambda v, mt=mt: v.tensor_scalar(
                out=gat2[:, mt:mt + 1], in0=gat[:, 29:30], scalar1=-1.0,
                scalar2=None, op0=ALU.mult), inc="dve")
            dv = k.E("dve", lambda v: v.tensor_scalar(
                out=gat[:, 40:48], in0=gat[:, 32:40],
                scalar1=gat[:, 29:30], scalar2=None, op0=ALU.mult),
                inc="dve")
            dv = k.E("dve", lambda v: v.tensor_copy(tb[:, 0:8],
                                                    gat[:, 40:48]),
                     inc="dve",
                     wait=[("sd", SCR["sd"]), ("gd", SCR["gd"])])
            tr = k.E("te", lambda t: t.transpose(
                psT[:, 0:128], tb[:, 0:128], IDT()), inc="te",
                wait=[("dve", dv)])
            k.E("dve", lambda v, mt=mt: v.tensor_copy(
                svs[:, t_(mt, 128)], psT[0:8, 0:128]), inc="dve",
                wait=[("te", tr)])
        gating_done = k.c("dve")
        m0 = 0
        ZS = ML
        stat_mm = None
        for e in range(NE):
            for hc in range(4):
                hh = (hc % 2) * 2048
                g1 = k.E("sy", lambda sy, e=e, hc=hc, hh=hh: sy.dma_start(
                    out=wsb[:, hh:hh + 1024].rearrange(
                        "p (a c) -> p a c", a=8),
                    in_=E1[e][:, t_(hc, 128)].rearrange(
                        "(a p) c -> p a c", p=128)),
                    inc="sd", wait=[("te", None), ("gd", pcm["E2"])])
                g2 = k.E("sy", lambda sy, e=e, hc=hc, hh=hh: sy.dma_start(
                    out=wsb[:, hh + 1024:hh + 2048].rearrange(
                        "p (a c) -> p a c", a=8),
                    in_=E2[e][:, t_(hc, 128)].rearrange(
                        "(a p) c -> p a c", p=128)),
                    inc="sd")
                mm1 = mm2 = None
                for dc in range(DC):
                    mm1 = k.E("te", lambda t, dc=dc, hh=hh: t.matmul(
                        bank(0, w=ML), wsb[:, o_(hh, dc, 128)],
                        kbf[:, dc * ML:dc * ML + ML],
                        start=(dc == 0), stop=(dc == DC - 1)),
                        inc="te" if dc == DC - 1 else None,
                        wait=[("sd", g1), ("act", None), ("dve", None)])
                    mm2 = k.E("te", lambda t, dc=dc, hh=hh: t.matmul(
                        bank(1, w=ML), wsb[:, o_(hh + 1024, dc, 128)],
                        kbf[:, dc * ML:dc * ML + ML],
                        start=(dc == 0), stop=(dc == DC - 1)),
                        inc="te" if dc == DC - 1 else None,
                        wait=[("sd", g2)])
                a1 = k.act(lambda s, e=e, hc=hc: s.activation(
                    out=tb[:, 0:ML], in_=bank(0, w=ML), func=AF.Identity,
                    bias=smal2[:, e * 4 + hc:e * 4 + hc + 1]),
                    AF.Identity,
                    wait=[("te", mm1), ("sd", SCR["sd"]),
                          ("gd", SCR["gd"])])
                a2 = k.act(lambda s, e=e, hc=hc: s.activation(
                    out=tb2[:, 0:ML], in_=bank(0, w=ML), func=AF.Sigmoid,
                    bias=smal2[:, e * 4 + hc:e * 4 + hc + 1]),
                    AF.Sigmoid)
                a3 = k.act(lambda s, e=e, hc=hc: s.activation(
                    out=tb[:, 512:512 + ML], in_=bank(1, w=ML),
                    func=AF.Identity,
                    bias=smal2[:, 32 + e * 4 + hc:33 + e * 4 + hc]),
                    AF.Identity, wait=[("te", mm2)])
                zoff = (e * 4 + hc) * ZS
                k.E("dve", lambda v: v.tensor_tensor(
                    out=tb2[:, 512:512 + ML], in0=tb[:, 0:ML],
                    in1=tb2[:, 0:ML], op=ALU.mult), inc="dve",
                    wait=[("act", a3)])
                d2 = k.E("dve", lambda v, zoff=zoff: v.tensor_tensor(
                    out=vbf[:, zoff:zoff + ML], in0=tb2[:, 512:512 + ML],
                    in1=tb[:, 512:512 + ML], op=ALU.mult), inc="dve",
                    wait=[("te", None)])
                d3 = k.E("dve", lambda v, zoff=zoff: v.tensor_tensor(
                    out=tb[:, 0:ML], in0=vbf[:, zoff:zoff + ML],
                    in1=vbf[:, zoff:zoff + ML], op=ALU.mult), inc="dve")
                first = (e == 0 and hc == 0)
                last = (e == NE - 1 and hc == 3)
                k.E("te", lambda t, e=e, zoff=zoff, first=first,
                    last=last: t.matmul(
                        bank(2, w=ML, p=8),
                        conb[:, OH8 + e * 8:OH8 + e * 8 + 8],
                        vbf[:, zoff:zoff + ML],
                        start=first, stop=last), inc="te",
                    wait=[("dve", d2), ("act", None)])
                stat_mm = k.E("te", lambda t, e=e, first=first,
                              last=last: t.matmul(
                        bank(3, w=ML, p=8),
                        conb[:, OH8 + e * 8:OH8 + e * 8 + 8],
                        tb[:, 0:ML], start=first, stop=last), inc="te",
                    wait=[("dve", d3), ("act", None)])
        k.E("dve", lambda v: v.tensor_scalar(
            out=sfr[0:8, 0:ML], in0=bank(2, w=ML, p=8), scalar1=1.0 / HSH,
            scalar2=None, op0=ALU.mult), inc="dve",
            wait=[("te", stat_mm), ("act", None)])
        k.E("dve", lambda v: v.tensor_scalar(
            out=sfr[0:8, 512:512 + ML], in0=bank(3, w=ML, p=8),
            scalar1=1.0 / HSH, scalar2=None, op0=ALU.mult), inc="dve")
        k.E("dve", lambda v: v.tensor_tensor(
            out=sfr2[0:8, 0:ML], in0=sfr[0:8, 0:ML],
            in1=sfr[0:8, 0:ML], op=ALU.mult), inc="dve")
        dv = k.E("dve", lambda v: v.tensor_tensor(
            out=sfr[0:8, 512:512 + ML], in0=sfr[0:8, 512:512 + ML],
            in1=sfr2[0:8, 0:ML], op=ALU.subtract), inc="dve")
        k.act(lambda s: s.activation(
            out=sfr2[0:8, 0:ML], in_=sfr[0:8, 512:512 + ML],
            func=AF.Ln, bias=cons[0:8, EPSC:EPSC + 1]), AF.Ln,
            wait=[("dve", dv)])
        a2 = k.act(lambda s: s.activation(
            out=sfr[0:8, 512:512 + ML], in_=sfr2[0:8, 0:ML],
            func=AF.Exp, scale=-0.5), AF.Exp)
        k.E("dve", lambda v: v.tensor_tensor(
            out=sfr2[0:8, 0:ML], in0=sfr[0:8, 512:512 + ML],
            in1=svs[:, 0:ML], op=ALU.mult), inc="dve",
            wait=[("act", a2), ("dve", gating_done)])
        k.E("dve", lambda v: v.tensor_copy(zrow[0:8, 0:ML],
                                           sfr2[0:8, 0:ML]),
            inc="dve", wait=[("te", None)])
        k.E("dve", lambda v: v.tensor_tensor(
            out=sfr[0:8, 0:ML], in0=sfr[0:8, 0:ML],
            in1=sfr2[0:8, 0:ML], op=ALU.mult), inc="dve")
        dvr = k.E("dve", lambda v: v.tensor_copy(zrow[0:8, 512:512 + ML],
                                                 sfr[0:8, 0:ML]),
                  inc="dve")
        zn_done = None
        for e in range(NE):
            b1 = k.E("te", lambda t, e=e: t.matmul(
                bank(4, w=ML), cob2[:, t_(e, 128)], zrow[0:8, 0:ML],
                start=True, stop=True), inc="te",
                wait=[("dve", None), ("act", None)])
            b2 = k.E("te", lambda t, e=e: t.matmul(
                bank(5, w=ML), cob2[:, t_(e, 128)], zrow[0:8, 512:512 + ML],
                start=True, stop=True), inc="te")
            for hc in range(4):
                zoff = (e * 4 + hc) * ZS
                k.E("dve", lambda v, zoff=zoff: v.tensor_tensor(
                    out=sfr2[:, 0:ML], in0=vbf[:, zoff:zoff + ML],
                    in1=bank(4, w=ML), op=ALU.mult), inc="dve",
                    wait=[("te", b2)])
                k.E("dve", lambda v, zoff=zoff: v.tensor_tensor(
                    out=sfr2[:, 0:ML], in0=sfr2[:, 0:ML],
                    in1=bank(5, w=ML), op=ALU.subtract), inc="dve")
                zn_done = k.E("dve", lambda v, zoff=zoff, hc=hc:
                              v.tensor_scalar(
                    out=vbf[:, zoff:zoff + ML], in0=sfr2[:, 0:ML],
                    scalar1=smal[:, SELG + hc:SELG + hc + 1],
                    scalar2=None, op0=ALU.mult), inc="dve")
        for dh in range(2):
            evac_prev = k.c("act")
            mm_by_mt = [None] * (ML // 128)
            for e in range(NE):
                eh = (e % 2) * 2048
                g = k.E("sy", lambda sy, e=e, dh=dh, eh=eh: sy.dma_start(
                    out=wsb[:, eh:eh + 2048].rearrange(
                        "p (a c) -> p a c", a=4),
                    in_=E3[e][:, t_(dh, 512)].rearrange(
                        "(a p) c -> p a c", p=128)),
                    inc="sd", wait=[("te", None), ("gd", pcm["E3"])])
                for mt4 in range(ML // 128):
                    for hc in range(4):
                        first = (e == 0 and hc == 0)
                        mm_by_mt[mt4] = k.E(
                            "te", lambda t, e=e, mt4=mt4, hc=hc,
                            first=first, eh=eh: t.matmul(
                                bank(mt4),
                                vbf[:, o_((e * 4 + hc) * ZS, mt4, 128)],
                                wsb[:, o_(eh, hc, 512)],
                                start=first, stop=False),
                            inc="te" if (mt4 == ML // 128 - 1 and hc == 3)
                            else None,
                            wait=([("sd", g), ("dve", zn_done),
                                   ("act", evac_prev)] if first or hc == 0
                                  else [("sd", g)]))
            for mt4 in range(ML // 128):
                mm_by_mt[mt4] = k.E("te", lambda t, mt4=mt4, dh=dh:
                                    t.matmul(
                        bank(mt4), svs[:, t_(mt4, 128)],
                        e3es[:, t_(dh, 512)], start=False, stop=True),
                    inc="te", wait=[("sd", setup["sd"])])
            for mt4 in range(ML // 128):
                a1 = k.act(lambda s, mt4=mt4: s.activation(
                    out=tb[:, 0:512], in_=bank(mt4), func=AF.Identity),
                    AF.Identity,
                    wait=[("te", mm_by_mt[mt4]), ("sd", SCR["sd"])])
                k.E("sy", lambda s, mt4=mt4, dh=dh: s.dma_start(
                    out=AGI[t_(mt4, 128), t_(dh, 512)],
                    in_=tb[:, 0:512]), inc="sd",
                    wait=[("act", a1), ("gd", None)])
                SCR["sd"] = k.c("sd")
        for mt4 in range(ML // 128):
            dv = k.E("dve", lambda v, mt4=mt4: v.tensor_copy(
                tb2[:, 0:1], gat2[:, mt4:mt4 + 1]), inc="dve",
                wait=[("sd", SCR["sd"])])
            dv2 = k.E("dve", lambda v: v.memset(tb2[:, 1:8], 0.0),
                      inc="dve")
            k.E("sy", lambda s, mt4=mt4: s.dma_start(
                out=AGI[t_(mt4, 128), 1024:1032],
                in_=tb2[:, 0:8]), inc="sd", wait=[("dve", dv2)])
            SCR["sd"] = k.c("sd")
        ag_w = k.c("sd")
        k.E("gp", lambda gp: gp.collective_compute(
            "AllGather", ALU.bypass, ins=[AGI[:, :]],
            outs=[AGO[0:M, :]], replica_groups=[list(range(8))]),
            inc="gp", wait=[("sd", ag_w), ("gd", None)])

    def corrections():
        ag_done = k.c("gp")
        for tt in range(16):
            g1 = k.E("sy", lambda s, tt=tt: s.dma_start(
                out=tb[:, 0:1024], in_=SHH[t_(tt, 128), :]), inc="sd",
                wait=[("dve", None), ("act", None)])
            g2 = k.E("gp", lambda gp, tt=tt: gp.indirect_dma_start(
                out=tb2[:], out_offset=None, in_=AGO[:, :],
                in_offset=IndirectOffsetOnAxis(ap=ridxs[:, tt:tt + 1],
                                               axis=0)),
                inc="gd", wait=[("gp", ag_done), ("dve", None)])
            dv = k.E("dve", lambda v: v.tensor_scalar(
                out=sfr2[:, 0:1], in0=tb2[:, 1024:1025], scalar1=1.0,
                scalar2=None, op0=ALU.add), inc="dve",
                wait=[("gd", g2), ("sd", g1)])
            dv = k.E("dve", lambda v: v.scalar_tensor_tensor(
                out=tb2[:, 0:1024], in0=tb[:, 0:1024],
                scalar=sfr2[:, 0:1], in1=tb2[:, 0:1024],
                op0=ALU.mult, op1=ALU.add), inc="dve")
            trs = None
            for dc in range(DC):
                trs = k.E("te", lambda t, dc=dc: t.transpose(
                    psT[:, t_(dc, 128)], tb2[:, t_(dc, 128)], IDT()),
                    inc="te" if dc == DC - 1 else None,
                    wait=[("dve", dv)])
            xfv = xf[:, :].rearrange("p (dc t) -> p dc t", dc=DC)
            psv = psT[:, :].rearrange("p (dc t) -> p dc t", dc=DC)
            k.E("dve", lambda v, tt=tt, xfv=xfv, psv=psv:
                v.scalar_tensor_tensor(
                    out=xfv[:, :, t_(tt, 128)], in0=psv[:, :, :],
                    scalar=0.5, in1=xfv[:, :, t_(tt, 128)],
                    op0=ALU.mult, op1=ALU.add), inc="dve",
                wait=[("te", trs)])

    def qkv_sub(ci, si, ln_dve):
        t0 = ci * CH + si * SUB
        for fc in range(16):
            h = fc % 2
            g = k.E("sy", lambda sy, fc=fc, h=h: sy.dma_start(
                out=wsb[:, h * 2048:h * 2048 + 1024].rearrange(
                    "p (a c) -> p a c", a=8),
                in_=A_w[:, t_(fc, 128)].rearrange("(a p) c -> p a c",
                                                  p=128)),
                inc="sd", wait=[("te", None), ("gd", pcm["A_w"])])
            mm = None
            for dc in range(DC):
                mm = k.E("te", lambda t, dc=dc, fc=fc, h=h: t.matmul(
                    bank(h), wsb[:, o_(h * 2048, dc, 128)],
                    xnb[:, t_(dc, SUB)],
                    start=(dc == 0), stop=(dc == DC - 1)),
                    inc="te" if dc == DC - 1 else None,
                    wait=[("sd", g), ("dve", ln_dve), ("dve", None)])
            if fc < 8:
                dst = qbf[:, fc * CH + si * SUB: fc * CH + si * SUB + 512]
            else:
                c0 = (fc - 8) * T + t0
                dst = kbf[:, c0:c0 + 512]
            dv = k.E("dve", lambda v, dst=dst, h=h: v.tensor_copy(
                dst, bank(h)), inc="dve", wait=[("te", mm)])
            dv2 = k.E("dve", lambda v, dst=dst: v.tensor_tensor(
                out=tb[:, 0:512], in0=dst, in1=dst, op=ALU.mult),
                inc="dve")
            mm2 = k.E("te", lambda t: t.matmul(
                bank(2, p=1), ONE(), tb[:, 0:512], start=True, stop=True),
                inc="te", wait=[("dve", dv2)])
            k.E("dve", lambda v, fc=fc, si=si: v.reduce_max(
                km2[0:1, 16 + (fc % 2):17 + (fc % 2)], bank(2, w=512, p=1),
                axis=AX.X), inc="dve", wait=[("te", mm2)])
            slot = fc
            fresh = (si == 0) if fc < 8 else (ci == 0 and si == 0)
            if fresh:
                k.E("dve", lambda v, slot=slot, fc=fc: v.tensor_copy(
                    km2[0:1, slot:slot + 1],
                    km2[0:1, 16 + (fc % 2):17 + (fc % 2)]), inc="dve")
            else:
                k.E("dve", lambda v, slot=slot, fc=fc: v.tensor_tensor(
                    out=km2[0:1, slot:slot + 1],
                    in0=km2[0:1, slot:slot + 1],
                    in1=km2[0:1, 16 + (fc % 2):17 + (fc % 2)],
                    op=ALU.max), inc="dve")
        for dh in range(2):
            g1 = k.E("sy", lambda sy, dh=dh: sy.dma_start(
                out=wsb[:, 0:2048].rearrange("p (a c) -> p a c", a=4),
                in_=A_w[0:512, o_(2048, dh, 512)].rearrange(
                    "(a p) c -> p a c", p=128)),
                inc="sd", wait=[("te", None), ("gd", pcm["A_w"])])
            g2 = k.E("sy", lambda sy, dh=dh: sy.dma_start(
                out=wsb[:, 2048:4096].rearrange("p (a c) -> p a c", a=4),
                in_=A_w[512:1024, o_(2048, dh, 512)].rearrange(
                    "(a p) c -> p a c", p=128)),
                inc="sd")
            for tt4 in range(4):
                b = 3 + (tt4 % 2)
                mm = None
                for dc in range(DC):
                    mm = k.E("te", lambda t, dc=dc, tt4=tt4, b=b: t.matmul(
                        bank(b), xnb[:, o_(dc * SUB, tt4, 128)],
                        wsb[:, t_(dc, 512)],
                        start=(dc == 0), stop=(dc == DC - 1)),
                        inc="te" if dc == DC - 1 else None,
                        wait=[("sd", g1), ("sd", g2), ("dve", None)])
                tt = t0 // 128 + tt4
                k.E("dve", lambda v, tt=tt, dh=dh, b=b: v.tensor_copy(
                    vbf[:, o_(tt * D, dh, 512)], bank(b)), inc="dve",
                    wait=[("te", mm)])
        return k.c("dve")

    def stabilizer(ci):
        dv = k.E("dve", lambda v: v.tensor_tensor(
            out=km2[0:1, 16:24], in0=km2[0:1, 0:8], in1=km2[0:1, 8:16],
            op=ALU.mult), inc="dve", wait=[("te", None)])
        k.act(lambda s: s.activation(
            out=km2[0:1, 16:24], in_=km2[0:1, 16:24], func=AF.Ln,
            bias=cons[0:1, EPSC + 1:EPSC + 2]), AF.Ln, wait=[("dve", dv)])
        a = k.act(lambda s: s.activation(
            out=km2[0:1, 16:24], in_=km2[0:1, 16:24], func=AF.Exp,
            scale=0.5), AF.Exp)
        dv = k.E("dve", lambda v: v.tensor_scalar(
            out=km2[0:1, 16:24], in0=km2[0:1, 16:24], scalar1=-1.0 / 32.0,
            scalar2=None, op0=ALU.mult), inc="dve", wait=[("act", a)])
        dv = k.E("dve", lambda v: v.tensor_copy(zrow[0:1, 1044:1052],
                                                km2[0:1, 16:24]), inc="dve",
                 wait=[("te", None)])
        mm = k.E("te", lambda t: t.matmul(
            bank(2, w=8), ONER(), zrow[0:1, 1044:1052], start=True, stop=True),
            inc="te", wait=[("dve", dv)])
        k.E("dve", lambda v: v.tensor_copy(csc[:, 0:8], bank(2, w=8, p=128)),
            inc="dve", wait=[("te", mm), ("act", None)])
        return k.c("dve")

    def attention_chunk(ci, csc_dve):
        qc0 = ci * CH
        for pr in range(NPAIR):
            fc = pr // 2
            po0 = 64 * (pr % 2)
            grp = fc
            for qs in range(2):
                qoff = qc0 + qs * SUB
                qcol = qs * SUB
                kbmax = (qoff + SUB) // 128
                act_of = {}
                te_av = {}

                def emit_scores(kb):
                    par = kb % 2
                    sb0 = par * 1024
                    diag = (kb >= kbmax - 4)
                    for hi in range(2):
                        po = po0 + 32 * hi
                        sl = psA[:, sb0 + hi * 512: sb0 + (hi + 1) * 512]
                        w = [("act", act_of.get(kb - 2, 0)), ("dve", None)]
                        if kb == 0 and hi == 0:
                            w.append(("dve", csc_dve))
                        is_last = (hi == 1 and not diag)
                        k.E("te", lambda t, fc=fc, po=po, kb=kb,
                            qoff=qoff, sl=sl, diag=diag:
                            t.matmul(
                                sl,
                                kbf[po:po + 32,
                                    fc * T + kb * 128:
                                    fc * T + (kb + 1) * 128],
                                qbf[po:po + 32,
                                    fc * CH + qoff - qc0:
                                    fc * CH + qoff - qc0 + 512],
                                start=True, stop=not diag,
                                tile_position=(po, 0)),
                            inc="te" if is_last else None, wait=w)
                        if diag:
                            qw = kb * 128 - qoff
                            k.E("te", lambda t, sl=sl, qw=qw:
                                t.matmul(
                                    sl,
                                    conb[:, MAC:MAC + 128],
                                    conb[:, MBW + 384 - qw:
                                         MBW + 384 - qw + 512],
                                    start=False, stop=True,
                                    tile_position=(0, 0)),
                                inc="te" if hi == 1 else None)
                    return k.c("te")

                def emit_exp(kb, sc_done):
                    sb0 = (kb % 2) * 1024
                    a = k.act(lambda s, grp=grp, sb0=sb0: s.activation(
                        out=pbuf[:, sb0:sb0 + 1024],
                        in_=psA[:, sb0:sb0 + 1024],
                        func=AF.Exp, scale=1.0 / 32.0,
                        bias=csc[:, grp:grp + 1]), AF.Exp,
                        wait=[("te", sc_done),
                              ("te", te_av.get(kb - 2, 0))])
                    act_of[kb] = a

                def emit_av(kb):
                    sb0 = (kb % 2) * 1024
                    a = act_of[kb]
                    first = (kb == 0)
                    lastk = (kb == kbmax - 1)
                    k.E("te", lambda t, kb=kb, pr=pr, sb0=sb0,
                        first=first, lastk=lastk: t.matmul(
                            bank(4, p=64), vbf[:, kb * D + 64 * pr:
                                               kb * D + 64 * pr + 64],
                            pbuf[:, sb0:sb0 + 512],
                            start=first, stop=lastk,
                            tile_position=(0, 0)),
                        inc=None,
                        wait=([("act", a), ("dve", None)] if first
                              else [("act", a)]))
                    k.E("te", lambda t, kb=kb, pr=pr, sb0=sb0,
                        first=first, lastk=lastk: t.matmul(
                            bank(4)[64:128, :],
                            vbf[:, kb * D + 64 * pr:
                                 kb * D + 64 * pr + 64],
                            pbuf[:, sb0 + 512:sb0 + 1024],
                            start=first, stop=lastk,
                            tile_position=(0, 64),
                            skip_group_check=True),
                        inc=None)
                    k.E("te", lambda t, sb0=sb0, first=first,
                        lastk=lastk: t.matmul(
                        bank(5, p=1), ONE(), pbuf[:, sb0:sb0 + 512],
                        start=first, stop=lastk,
                        tile_position=(0, 0)), inc=None,
                        wait=([("dve", None)] if first else []))
                    te_av[kb] = k.E("te", lambda t, sb0=sb0, first=first,
                        lastk=lastk: t.matmul(
                        psA[32:33, 2560:3072], ONE(),
                        pbuf[:, sb0 + 512:sb0 + 1024],
                        start=first, stop=lastk,
                        tile_position=(0, 32),
                        skip_group_check=True), inc="te")

                for kb in range(kbmax + 1):
                    if kb < kbmax:
                        sc = emit_scores(kb)
                        emit_exp(kb, sc)
                    if kb >= 1:
                        emit_av(kb - 1)
                av_done = k.c("te")
                k.act(lambda s: s.activation(
                    out=sfr[0:1, 0:512], in_=psA[0:1, 2560:3072],
                    func=AF.Ln), AF.Ln,
                    wait=[("te", av_done), ("dve", None)])
                k.act(lambda s: s.activation(
                    out=zrow[0:1, 0:512], in_=sfr[0:1, 0:512],
                    func=AF.Exp, scale=-1.0), AF.Exp)
                k.act(lambda s: s.activation(
                    out=sfr[32:33, 0:512], in_=psA[32:33, 2560:3072],
                    func=AF.Ln), AF.Ln)
                az = k.act(lambda s: s.activation(
                    out=zrow[32:33, 0:512], in_=sfr[32:33, 0:512],
                    func=AF.Exp, scale=-1.0), AF.Exp)
                mm = k.E("te", lambda t: t.matmul(
                    bank(6), conb[0:64, LABC:LABC + 128], zrow[0:64, 0:512],
                    start=True, stop=True, tile_position=(0, 0)),
                    inc="te", wait=[("act", az), ("dve", None)])
                dab = k.E("dve", lambda v: v.tensor_copy(tb[:, 0:512],
                                                         bank(6)),
                          inc="dve", wait=[("te", mm)])
                dt = k.E("dve", lambda v: v.tensor_tensor(
                    out=tb2[:, 0:512], in0=bank(4), in1=tb[:, 0:512],
                    op=ALU.mult), inc="dve")
                mm = k.E("te", lambda t: t.matmul(
                    bank(6), csb[:, :], tb2[:, 0:512], start=True,
                    stop=True), inc="te", wait=[("dve", dt)])
                a1 = k.act(lambda s: s.activation(
                    out=tb[0:64, 0:512], in_=bank(6, p=64),
                    func=AF.Identity), AF.Identity, wait=[("te", mm)])
                dd = k.E("dve", lambda v: v.tensor_tensor(
                    out=tb[0:64, 512:1024], in0=tb[0:64, 0:512],
                    in1=tb[0:64, 0:512], op=ALU.mult), inc="dve",
                    wait=[("act", a1)])
                mm = k.E("te", lambda t: t.matmul(
                    psA[0:1, 2560:3072], conb[0:64, ONC:ONC + 1],
                    tb[0:64, 512:1024], start=True, stop=True), inc="te",
                    wait=[("dve", dd)])
                k.act(lambda s: s.activation(
                    out=sfr2[0:1, 0:512], in_=psA[0:1, 2560:3072],
                    func=AF.Ln, scale=1.0 / 64.0,
                    bias=cons[0:1, EPSC:EPSC + 1]), AF.Ln,
                    wait=[("te", mm)])
                ag = k.act(lambda s: s.activation(
                    out=sfr[0:1, 512:1024], in_=sfr2[0:1, 0:512],
                    func=AF.Exp, scale=-0.5), AF.Exp)
                dg = k.E("dve", lambda v: v.tensor_copy(
                    zrow[0:1, 512:1024], sfr[0:1, 512:1024]), inc="dve",
                    wait=[("act", ag)])
                mm = k.E("te", lambda t: t.matmul(
                    bank(6, p=64), grow[0:1, :], zrow[0:1, 512:1024],
                    start=True, stop=True), inc="te", wait=[("dve", dg)])
                k.E("dve", lambda v, pr=pr, qcol=qcol: v.tensor_tensor(
                    out=attn[64 * (pr % 2):64 * (pr % 2) + 64,
                             (pr // 2) * CH + qcol:
                             (pr // 2) * CH + qcol + 512],
                    in0=tb[0:64, 0:512], in1=bank(6, p=64), op=ALU.mult),
                    inc="dve", wait=[("te", mm)])
        return k.c("dve")

    def proj_chunk(ci, attn_dve):
        for si in range(2):
            for dc in range(DC):
                h = dc % 2
                g = k.E("sy", lambda sy, dc=dc, h=h: sy.dma_start(
                    out=wsb[:, h * 2048:h * 2048 + 1024].rearrange(
                        "p (a c) -> p a c", a=8),
                    in_=A_p[:, t_(dc, 128)].rearrange(
                        "(a p) c -> p a c", p=128)),
                    inc="sd", wait=[("te", None), ("gd", pcm["A_p"])])
                mm = None
                for ac in range(8):
                    mm = k.E("te", lambda t, ac=ac, si=si, h=h: t.matmul(
                        bank(h), wsb[:, o_(h * 2048, ac, 128)],
                        attn[:, o_(ac * CH, si, 512)],
                        start=(ac == 0), stop=(ac == 7)),
                        inc="te" if ac == 7 else None,
                        wait=[("sd", g), ("dve", attn_dve), ("dve", None)])
                k.E("dve", lambda v, dc=dc, ci=ci, si=si, h=h:
                    v.scalar_tensor_tensor(
                        out=xf[:, dc * T + ci * CH + si * 512:
                               dc * T + ci * CH + (si + 1) * 512],
                        in0=bank(h), scalar=1.0,
                        in1=xf[:, dc * T + ci * CH + si * 512:
                               dc * T + ci * CH + (si + 1) * 512],
                        op0=ALU.mult, op1=ALU.add), inc="dve",
                    wait=[("te", mm)])
        return k.c("dve")

    import os
    nphase = int(os.environ.get("KERN_PHASES", "3"))
    for ci in range(2):
        for si in range(2):
            lnd = ln_sub(ci, si, out_xnh=True)
            ffn_shared_sub(ci, si, lnd)
    if not setup["done"]:
        eb3_setup()
        setup["done"] = True
    moe_phase()
    corrections()

    if nphase >= 2:
        for ci in range(2):
            for si in range(2):
                lnd = ln_sub(ci, si, out_xnh=False)
                qkv_sub(ci, si, lnd)
            cdve = stabilizer(ci)
            adve = attention_chunk(ci, cdve)
            proj_chunk(ci, adve)

    if nphase >= 3:
        for ci in range(2):
            for si in range(2):
                lnd = ln_sub(ci, si, out_xnh=True)
                ffn_shared_sub(ci, si, lnd)
        moe_phase()
        corrections()

    for dc in range(DC):
        k.E("sy", lambda s, dc=dc: s.dma_start(
            out=y_out[t_(dc, 128), :], in_=xf[:, t_(dc, T)]), inc="yd",
            wait=[("dve", None), ("te", None), ("act", None)])
    ydone = k.c("yd")
    k.E("sy", lambda s: None, wait=[("yd", ydone)])

    sems = {}
    with contextlib.ExitStack() as ses:
        for ch in KB.CHS:
            sems[ch] = ses.enter_context(nc.semaphore(f"s_{ch}"))
        with nc.Block() as block:
            block.tensor(k.runner("te", sems))
            block.scalar(k.runner("act", sems))
            block.vector(k.runner("dve", sems))
            block.gpsimd(k.runner("gp", sems))
            block.sync(k.runner("sy", sems))
        es.close()
    return nc


def _consts_host():
    c = np.zeros((128, 1664), np.float32)
    p = np.arange(128)
    c[:, IDC:IDC + 128][p, p] = 1.0
    c[64 + np.arange(64), LIC + np.arange(64)] = 1.0
    kp = p[:, None]
    key = np.arange(128)[None, :]
    c[:, MAC:MAC + 128] = np.where(kp <= key, -MS, 0.0)
    r = np.arange(896)[None, :] - 384
    c[:, MBW:MBW + 896] = np.where(kp > r, MS, 0.0)
    c[:, ONC:ONC + 128] = 1.0

    for e in range(8):
        c[:, OH8 + e * 8 + e] = 1.0
    c[0, LABC:LABC + 64] = 1.0
    c[32, LABC + 64:LABC + 128] = 1.0
    return c.astype(ml_dtypes.bfloat16)


def _constb2_host():
    c = np.zeros((8, 1024), np.float32)
    for e in range(8):
        c[e, e * 128:(e + 1) * 128] = 1.0
    return c.astype(ml_dtypes.bfloat16)


def stage_inputs(inp):
    n = 8
    x = np.asarray(inp["L0_attn_x"])
    w_attn = np.ascontiguousarray(np.asarray(inp["L0_attn_w_attn"]))
    w_proj = np.ascontiguousarray(np.asarray(inp["L0_attn_w_proj"]))
    s1t = np.ascontiguousarray(np.asarray(inp["L0_ffn_sw1"]).T)
    s2t = np.ascontiguousarray(np.asarray(inp["L0_ffn_sw2"]).T)
    s3t = np.ascontiguousarray(np.asarray(inp["L0_ffn_sw3"]).T)
    e1t = np.ascontiguousarray(np.asarray(inp["L0_ffn_ew1"]).transpose(0, 2, 1))
    e2t = np.ascontiguousarray(np.asarray(inp["L0_ffn_ew2"]).transpose(0, 2, 1))
    e3t = np.ascontiguousarray(np.asarray(inp["L0_ffn_ew3"]).transpose(0, 2, 1))
    gwt = np.zeros((D, 16), np.float32)
    gwt[:, 0] = np.asarray(inp["L0_ffn_sg_w"])[0]
    gwt[:, 1:9] = np.asarray(inp["L0_ffn_eg_w"]).T
    eb3_in = np.ascontiguousarray(np.asarray(inp["L0_ffn_eb3"]))

    smalls = np.zeros((128, 64), np.float32)
    for hc in range(4):
        smalls[:, SB1 + hc] = np.asarray(inp["L0_ffn_sb1"])[t_(hc, 128)]
        smalls[:, SB2 + hc] = np.asarray(inp["L0_ffn_sb2"])[t_(hc, 128)]
        smalls[:, SLG + hc] = np.asarray(inp["L0_ffn_s_ln_g"])[t_(hc, 128)]
        smalls[:, SLB + hc] = np.asarray(inp["L0_ffn_s_ln_b"])[t_(hc, 128)]
        smalls[:, SELB + hc] = np.asarray(inp["L0_ffn_sel_ln_b"])[
            t_(hc, 128)]
        smalls[:, SELG + hc] = np.asarray(inp["L0_ffn_sel_ln_g"])[
            t_(hc, 128)]
    for dc in range(8):
        smalls[:, SB3 + dc] = np.asarray(inp["L0_ffn_sb3"])[t_(dc, 128)]
    smalls[0:64, GAM] = np.asarray(inp["L0_attn_gamma"])

    smalls2 = np.zeros((128, 64), np.float32)
    eb1 = np.asarray(inp["L0_ffn_eb1"])
    eb2 = np.asarray(inp["L0_ffn_eb2"])
    for e in range(8):
        for hc in range(4):
            smalls2[:, e * 4 + hc] = eb1[e][t_(hc, 128)]
            smalls2[:, 32 + e * 4 + hc] = eb2[e][t_(hc, 128)]

    constb = _consts_host()
    constf = np.zeros((128, 168), np.float32)
    constf[:, 160] = EPS
    constf[:, 161] = 1e-12
    constf[:, IOC:IOC + 8] = np.arange(8)[None, :]
    constf[0, GBC] = np.asarray(inp["L0_ffn_sg_b"])[0]
    constf[0, GBC + 1:GBC + 9] = np.asarray(inp["L0_ffn_eg_b"])
    constf[:, EBC:EBC + 8] = np.asarray(inp["L0_ffn_exp_bias"])[None, :]
    constf[0, LQ1:LQ1 + 32] = np.asarray(inp["L0_attn_lambda_q1"])
    constf[0, LK1:LK1 + 32] = np.asarray(inp["L0_attn_lambda_k1"])
    constf[0, LQ2:LQ2 + 32] = np.asarray(inp["L0_attn_lambda_q2"])
    constf[0, LK2:LK2 + 32] = np.asarray(inp["L0_attn_lambda_k2"])
    constb2 = _constb2_host()

    index_b = np.asarray(inp["L0_ffn_index_b"]).astype(np.int64)
    index_t = np.asarray(inp["L0_ffn_index_t"]).astype(np.int64)
    gidxs_percore = []
    for c in range(n):
        gi = np.full((128, 16), T, np.uint32)
        for m in range(M):
            if index_b[m] == c:
                gi[m % 128, m // 128] = index_t[m]
        gidxs_percore.append(gi)
    rids = []
    for c in range(n):
        rev = np.full(T, T, np.int64)
        own = np.nonzero(index_b == c)[0]
        rev[index_t[own]] = own
        ri = np.zeros((128, 16), np.uint32)
        for t in range(T):
            ri[t % 128, t // 128] = rev[t]
        rids.append(ri)

    shared = dict(w_attn=w_attn, w_proj=w_proj, s1t=s1t, s2t=s2t, s3t=s3t,
                  e1t=e1t, e2t=e2t, e3t=e3t, gwt=gwt, eb3_in=eb3_in,
                  smalls=smalls, smalls2=smalls2, constb=constb,
                  constf=constf, constb2=constb2)
    in_maps = []
    for c in range(n):
        m = dict(shared)
        m["x_in"] = np.ascontiguousarray(x[c].T)
        m["ridx"] = rids[c]
        m["gidx"] = gidxs_percore[c]
        in_maps.append(m)
    return in_maps


def kernel(**inp):
    n = 8
    in_maps = stage_inputs(inp)
    nc = build_program()
    res = run_bass_kernel_spmd(nc, in_maps, core_ids=list(range(n)))
    out = np.stack([np.asarray(res.results[c]["y_out"]).T
                    for c in range(n)], axis=0)
    return out


# version 11
# speedup vs baseline: 1.1142x; 1.0571x over previous
import contextlib
import math

import numpy as np
import ml_dtypes

import concourse.bass as bass
import concourse.mybir as mybir
from concourse.bass import IndirectOffsetOnAxis
from concourse.bass_utils import run_bass_kernel_spmd

F32 = mybir.dt.float32
BF16 = mybir.dt.bfloat16
U32 = mybir.dt.uint32
AF = mybir.ActivationFunctionType
ALU = mybir.AluOpType
AX = mybir.AxisListType

T = 2048
D = 1024
DC = 8
CH = 1024
SUB = 512
HSH = 512
NE = 8
NPAIR = 16
M = 2048
ML = 256
EPS = 1e-5
LAMBDA_INIT = 0.2
MS = 180.0

IDC = 0
LIC = 128
MAC = 256
MBW = 384
ONC = 1280
SELC = 1408
OH8 = 1536
LABC = 1408
IOC = 0
EBC = 8
GBC = 16
LQ1, LK1, LQ2, LK2 = 32, 64, 96, 128
EPSC = 160
SB1, SB2, SB3, SLG, SLB, SELB, GAM, SELG = 0, 4, 8, 16, 20, 24, 28, 32
DMA_STEP = 16


class KB:
    ENGS = ("te", "act", "dve", "gp", "sy")
    CHS = ("te", "act", "dve", "gp", "sy", "gd", "sd", "yd")
    DMA_CHS = ("gd", "sd", "yd")
    SETOF = {AF.Exp: "nlexp", AF.Ln: "nlexp", AF.Sigmoid: "sig",
             AF.Identity: None, AF.Copy: None, AF.Square: None}

    def __init__(self):
        import os as _os
        self.nc = bass.Bass(
            "TRN2", target_bir_lowering=False,
            detect_race_conditions=bool(_os.environ.get("KERN_RACE")))
        self.cnt = {c: 0 for c in self.CHS}
        self.ops = {e: [] for e in self.ENGS}
        self.act_set = None

    def E(self, eng, fn, inc=None, wait=()):
        w = tuple((c, (self.cnt[c] if v is None else v)) for c, v in wait
                  if (self.cnt[c] if v is None else v) > 0)
        if inc is not None:
            self.cnt[inc] += DMA_STEP if inc in self.DMA_CHS else 1
        self.ops[eng].append((fn, w, inc))
        return self.cnt[inc] if inc is not None else None

    SETID = {"nlexp": 6, "sig": 2}

    def act(self, fn, func, inc="act", wait=()):
        need = self.SETOF[func]
        if need is not None and need != self.act_set:
            sid = self.SETID[need]

            def load(s, sid=sid):
                return s.add_instruction(mybir.InstLoadActFuncSet(
                    name=s.bass.get_next_instruction_name(),
                    act_func_set_id=sid, ins=[], outs=[]))
            self.E("act", load)
            self.act_set = need
        return self.E("act", fn, inc=inc, wait=wait)

    def c(self, ch):
        return self.cnt[ch]

    def runner(self, name, sems):
        ops = self.ops[name]

        def run(eng):
            waited = {}
            for fn, waits, inc in ops:
                for ch, val in waits:
                    if waited.get(ch, -1) < val:
                        eng.wait_ge(sems[ch], val)
                        waited[ch] = val
                if name == "dve":
                    eng.drain()
                inst = fn(eng)
                if inc is not None:
                    step = DMA_STEP if inc in self.DMA_CHS else 1
                    inst.then_inc(sems[inc], step)
        return run


def t_(i, n):
    return slice(i * n, (i + 1) * n)


def o_(off, i, n):
    return slice(off + i * n, off + (i + 1) * n)


def build_program():
    k = KB()
    nc = k.nc

    def din(name, shape, dt=F32):
        return nc.dram_tensor(name, shape, dt, kind="ExternalInput").ap()

    import os as _os
    _dbg = bool(_os.environ.get("KERN_DBG"))

    def dint(name, shape, dt=BF16):
        kind = ("ExternalOutput" if (_dbg and name in
                ("XNH", "SHH", "AGO", "E3E", "DBGA", "DBGB", "DBGC", "DBGD", "DBGE")) else "Internal")
        return nc.dram_tensor(name, shape, dt, kind=kind).ap()

    x_in = din("x_in", [D, T])
    w_attn = din("w_attn", [D, 3 * D])
    w_proj = din("w_proj", [D, D])
    s1t = din("s1t", [D, HSH])
    s2t = din("s2t", [D, HSH])
    s3t = din("s3t", [HSH, D])
    e1t = din("e1t", [NE, D, HSH])
    e2t = din("e2t", [NE, D, HSH])
    e3t = din("e3t", [NE, HSH, D])
    gwt = din("gwt", [D, 16])
    eb3_in = din("eb3_in", [NE, D])
    smalls = din("smalls", [128, 64])
    smalls2 = din("smalls2", [128, 64])
    constb = din("constb", [128, 1664], dt=BF16)
    constf = din("constf", [128, 168])
    constb2 = din("constb2", [8, 1024], dt=BF16)
    gidx = din("gidx", [128, 16], U32)
    ridx = din("ridx", [128, 16], U32)
    y_out = nc.dram_tensor("y_out", [D, T], F32, kind="ExternalOutput").ap()

    DBGA = dint("DBGA", [M, D]) if _dbg else None
    DBGD = dint("DBGD", [8, 4096]) if _dbg else None
    DBGE = dint("DBGE", [128, 128], dt=F32) if _dbg else None
    DBGB = dint("DBGB", [128, 8 * M]) if _dbg else None
    DBGC = dint("DBGC", [128, 16 * D]) if _dbg else None
    A_w = dint("A_w", [D, 3 * D])
    A_p = dint("A_p", [D, D])
    S1 = dint("S1", [D, HSH])
    S2 = dint("S2", [D, HSH])
    S3 = dint("S3", [HSH, D])
    E1 = dint("E1", [NE, D, HSH])
    E2 = dint("E2", [NE, D, HSH])
    E3 = dint("E3", [NE, HSH, D])
    GWb = dint("GWb", [D, 16])
    XNH = dint("XNH", [T + 1, D])
    SHH = dint("SHH", [T, D])
    RSI = dint("RSI", [M, D])
    RSO = dint("RSO", [ML, D])
    AGI = dint("AGI", [ML, 1032])
    AGO = dint("AGO", [M + 1, 1032])
    E3E = dint("E3E", [NE, D])

    es = contextlib.ExitStack()

    def sbuf(name, shape, dt=BF16):
        return es.enter_context(nc.sbuf_tensor(name, shape, dt))

    xf = sbuf("xf", [128, DC * T], F32)
    kbf = sbuf("kbf", [128, 8 * T])
    vbf = sbuf("vbf", [128, 16 * D])
    qbf = sbuf("qbf", [128, 8 * CH])
    attn = sbuf("attn", [128, 8 * CH])
    xnb = sbuf("xnb", [128, DC * SUB])
    pbuf = sbuf("pbuf", [128, 2048])
    wsb = sbuf("wsb", [128, 4096])
    tb = sbuf("tb", [128, 1024])
    tb2 = sbuf("tb2", [128, 1032])
    sfr = sbuf("sfr", [128, 1024], F32)
    sfr2 = sbuf("sfr2", [128, 520], F32)
    smal = sbuf("smal", [128, 64], F32)
    smal2 = sbuf("smal2", [128, 64], F32)
    cons = sbuf("cons", [128, 168], F32)
    conb = sbuf("conb", [128, 1664])
    cob2 = sbuf("cob2", [8, 1024])
    gidxs = sbuf("gidxs", [128, 16], U32)
    ridxs = sbuf("ridxs", [128, 16], U32)
    gat = sbuf("gat", [128, 48], F32)
    gat2 = sbuf("gat2", [128, 4], F32)
    idx8 = sbuf("idx8", [128, 8], U32)
    csb = sbuf("csb", [128, 128])
    e3es = sbuf("e3es", [8, 1024])
    svs = sbuf("svs", [8, ML])
    zrow = sbuf("zrow", [128, 1088])
    lamv = sbuf("lamv", [128, 8], F32)
    csc = sbuf("csc", [128, 8], F32)
    gpr = sbuf("gpr", [128, 2], F32)
    grow = sbuf("grow", [1, 64])
    sb3r = sbuf("sb3r", [1, 1024])
    km2 = sbuf("km2", [128, 24], F32)

    psA = es.enter_context(nc.psum_tensor("psA", [128, 3584], F32))
    psT = es.enter_context(nc.psum_tensor("psT", [128, 1024], BF16))

    def bank(i, w=512, p=128):
        return psA[0:p, i * 512: i * 512 + w]

    def ONE():
        return conb[:, ONC:ONC + 1]

    def ONER():
        return conb[0:1, ONC:ONC + 128]

    def IDT():
        return conb[:, IDC:IDC + 128]

    k.E("sy", lambda s: s.dma_start(out=cons[:], in_=constf[:]), inc="sd")
    k.E("sy", lambda s: s.dma_start(out=conb[:], in_=constb[:]), inc="sd")
    k.E("sy", lambda s: s.dma_start(out=smal[:], in_=smalls[:]), inc="sd")
    k.E("sy", lambda s: s.dma_start(out=smal2[:], in_=smalls2[:]), inc="sd")
    k.E("sy", lambda s: s.dma_start(out=gidxs[:], in_=gidx[:]), inc="sd")
    k.E("sy", lambda s: s.dma_start(out=ridxs[:], in_=ridx[:]), inc="sd")
    k.E("sy", lambda s: s.dma_start(out=cob2[:], in_=constb2[:]), inc="sd")
    for dc in range(DC):
        k.E("sy", lambda s, dc=dc: s.dma_start(
            out=xf[:, t_(dc, T)], in_=x_in[t_(dc, 128), :]), inc="sd")
    boot = k.c("sd")
    k.E("dve", lambda v: v.memset(tb2[:], 0.0), inc="dve",
        wait=[("sd", boot)])
    k.E("dve", lambda v: v.memset(tb[:], 0.0), inc="dve")
    k.E("dve", lambda v: v.memset(gpr[:], 0.0), inc="dve")
    k.E("dve", lambda v: v.memset(zrow[:], 0.0), inc="dve")
    z0 = k.c("dve")
    k.E("sy", lambda s: s.dma_start(out=XNH[T:T + 1, :], in_=tb2[0:1, 0:D]),
        inc="sd", wait=[("dve", z0)])
    k.E("sy", lambda s: s.dma_start(out=AGO[M:M + 1, :], in_=tb2[0:1, :]),
        inc="sd", wait=[("dve", z0)])

    k.E("dve", lambda v: v.tensor_tensor(
        out=sfr[0:1, 0:32], in0=cons[0:1, LQ1:LQ1 + 32],
        in1=cons[0:1, LK1:LK1 + 32], op=ALU.mult), inc="dve")
    k.E("dve", lambda v: v.tensor_tensor(
        out=sfr[0:1, 32:64], in0=cons[0:1, LQ2:LQ2 + 32],
        in1=cons[0:1, LK2:LK2 + 32], op=ALU.mult), inc="dve")
    k.E("dve", lambda v: v.reduce_sum(lamv[0:1, 0:1], sfr[0:1, 0:32],
                                      axis=AX.X), inc="dve")
    dv = k.E("dve", lambda v: v.reduce_sum(lamv[0:1, 1:2], sfr[0:1, 32:64],
                                           axis=AX.X), inc="dve")
    a = k.act(lambda s: s.activation(out=lamv[0:1, 2:4], in_=lamv[0:1, 0:2],
                                     func=AF.Exp), AF.Exp, wait=[("dve", dv)])
    k.E("dve", lambda v: v.tensor_tensor(
        out=lamv[0:1, 4:5], in0=lamv[0:1, 2:3], in1=lamv[0:1, 3:4],
        op=ALU.subtract), inc="dve", wait=[("act", a)])
    k.E("dve", lambda v: v.tensor_scalar(
        out=lamv[0:1, 5:6], in0=lamv[0:1, 4:5], scalar1=LAMBDA_INIT,
        scalar2=-1.0, op0=ALU.add, op1=ALU.mult), inc="dve")
    dv = k.E("dve", lambda v: v.tensor_copy(zrow[0:1, 1052:1053], lamv[0:1, 5:6]),
             inc="dve")
    mmn = k.E("te", lambda t: t.matmul(
        psA[:, 3072:3073], ONER(), zrow[0:1, 1052:1053], start=True, stop=True),
        inc="te", wait=[("dve", dv)])
    dv = k.E("dve", lambda v: v.tensor_copy(lamv[:, 6:7], psA[:, 3072:3073]),
             inc="dve", wait=[("te", mmn)])
    k.E("dve", lambda v: v.tensor_copy(csb[0:64, :],
                                       conb[0:64, IDC:IDC + 128]), inc="dve")
    k.E("dve", lambda v: v.tensor_scalar(
        out=csb[64:128, :], in0=conb[64:128, LIC:LIC + 128],
        scalar1=lamv[64:128, 6:7], scalar2=None, op0=ALU.mult), inc="dve")
    k.E("dve", lambda v: v.tensor_scalar(
        out=gpr[0:64, 0:1], in0=smal[0:64, GAM:GAM + 1], scalar1=0.8,
        scalar2=None, op0=ALU.mult), inc="dve")
    dv = k.E("dve", lambda v: v.tensor_copy(tb[:, 0:1], gpr[:, 0:1]),
             inc="dve")
    tr = k.E("te", lambda t: t.transpose(psT[:, 0:128], tb[:, 0:128], IDT()),
             inc="te", wait=[("dve", dv)])
    k.E("dve", lambda v: v.tensor_copy(grow[0:1, 0:64], psT[0:1, 0:64]),
        inc="dve", wait=[("te", tr)])
    for dc in range(DC):
        dv = k.E("dve", lambda v, dc=dc: v.tensor_copy(
            tb[:, 0:1], smal[:, SB3 + dc:SB3 + dc + 1]), inc="dve",
            wait=[("te", None)])
        tr = k.E("te", lambda t: t.transpose(psT[:, 0:128], tb[:, 0:128],
                                             IDT()),
                 inc="te", wait=[("dve", dv)])
        k.E("dve", lambda v, dc=dc: v.tensor_copy(
            sb3r[0:1, t_(dc, 128)], psT[0:1, 0:128]), inc="dve",
            wait=[("te", tr)])
    consts_ready_dve = k.c("dve")

    jobs = []

    def addj(src, dst, rows, cols, gfold=False):
        assert not gfold
        nrb = rows // 128
        g = max(1, 2048 // cols)
        if g > 1:
            for r0 in range(0, nrb, g):
                ga = min(g, nrb - r0)
                jobs.append((
                    src[r0 * 128:(r0 + ga) * 128, :].rearrange(
                        "(a p) c -> p a c", p=128),
                    dst[r0 * 128:(r0 + ga) * 128, :].rearrange(
                        "(a p) c -> p a c", p=128),
                    ga * cols, None))
        else:
            for r in range(nrb):
                for c0 in range(0, cols, 2048):
                    cw = min(2048, cols - c0)
                    jobs.append((src[t_(r, 128), c0:c0 + cw],
                                 dst[t_(r, 128), c0:c0 + cw], cw, None))

    addj(s1t, S1, D, HSH)
    addj(s2t, S2, D, HSH)
    addj(s3t, S3, HSH, D)
    addj(gwt, GWb, D, 16)
    for e in range(NE):
        addj(e1t[e], E1[e], D, HSH)
        addj(e2t[e], E2[e], D, HSH)
    for e in range(NE):
        addj(e3t[e], E3[e], HSH, D)
    addj(w_attn, A_w, D, 3 * D)
    addj(w_proj, A_p, D, D)

    half = 0
    sd_prev = [0, 0]
    pcm = {}
    for j, (src, dst, cw, gf) in enumerate(jobs):
        h = half
        half ^= 1
        buf = qbf[:, 4096 + h * 2048: 4096 + h * 2048 + cw]
        if len(src.shape) == 3:
            buf = buf.rearrange("p (a c) -> p a c", a=src.shape[1])
        g = k.E("gp", lambda gp, b=buf, s=src: gp.dma_start(out=b, in_=s),
                inc="gd", wait=[("gd", sd_prev[h])])
        if gf is not None:
            d = k.E("dve", lambda v, b=buf, hc=gf: v.tensor_scalar(
                out=b, in0=b, scalar1=smal[:, SELG + hc:SELG + hc + 1],
                scalar2=None, op0=ALU.mult), inc="dve", wait=[("gd", g)])
            sd_prev[h] = k.E("gp", lambda gp, b=buf, dd=dst: gp.dma_start(
                out=dd, in_=b), inc="gd", wait=[("dve", d)])
        else:
            sd_prev[h] = k.E("gp", lambda gp, b=buf, dd=dst: gp.dma_start(
                out=dd, in_=b), inc="gd", wait=[("gd", g)])
        pcm[dst.tensor.name] = k.c("gd")
    precast = k.c("gd")

    SCR = {"sd": boot, "gd": 0, "wte": 0}
    SCR["s3"] = k.E("sy", lambda s: s.dma_start(
        out=qbf[:, 0:4096].rearrange("p (a c) -> p a c", a=4),
        in_=S3.rearrange("(a p) c -> p a c", p=128)),
        inc="sd", wait=[("gd", pcm["S3"])])

    def eb3_setup():
      selb_ok = k.E("dve", lambda v: v.tensor_copy(
        zrow[:, 1040:1044], smal[:, SELB:SELB + 4]), inc="dve")
      for e in range(NE):
          gws = []
          for hc in range(4):
              g = k.E("gp", lambda gp, e=e, hc=hc: gp.dma_start(
                  out=attn[:, hc * 1024:(hc + 1) * 1024],
                  in_=e3t[e][t_(hc, 128), :]), inc="gd",
                  wait=[("te", None), ("dve", None)])
              gws.append(g)
          mm = None
          for hc in range(4):
              for dh in range(2):
                  last = (hc == 3 and dh == 1)
                  mm = k.E("te", lambda t, hc=hc, dh=dh: t.matmul(
                      psA[0:1, dh * 512:(dh + 1) * 512],
                      zrow[:, 1040 + hc:1041 + hc],
                      attn[:, hc * 1024 + dh * 512:
                           hc * 1024 + (dh + 1) * 512],
                      start=(hc == 0), stop=(hc == 3)),
                      inc="te" if last else None,
                      wait=[("gd", gws[hc]), ("dve", selb_ok)]
                      + ([("dve", None)] if hc == 0 else []))
          g2 = k.E("sy", lambda s, e=e: s.dma_start(
              out=sfr[0:1, 0:1024], in_=eb3_in[e:e + 1, :]), inc="sd")
          dv = k.E("dve", lambda v: v.tensor_tensor(
              out=tb[0:1, 0:1024], in0=psA[0:1, 0:1024],
              in1=sfr[0:1, 0:1024], op=ALU.add), inc="dve",
              wait=[("te", mm), ("sd", g2)])
          k.E("sy", lambda s, e=e: s.dma_start(
              out=E3E[e:e + 1, :], in_=tb[0:1, 0:1024]), inc="sd",
              wait=[("dve", dv)])
      k.E("sy", lambda s: s.dma_start(out=e3es[:], in_=E3E[:, :]), inc="sd",
          wait=[("sd", None)])
      k.E("sy", lambda s: s.dma_start(
        out=gwsb[:].rearrange("p (a c) -> p a c", a=8),
        in_=GWb.rearrange("(a p) c -> p a c", p=128)),
        inc="sd", wait=[("gd", pcm["GWb"])])
      setup["sd"] = k.c("sd")
      SCR["sd"] = k.c("sd")

    gwsb = sbuf("gwsb", [128, 128])
    setup = {"sd": 0, "done": False}
    gb_ok = k.E("dve", lambda v: v.tensor_copy(
        zrow[0:1, 1024:1040], cons[0:1, GBC:GBC + 16]), inc="dve")

    def ln_sub(ci, si, out_xnh):
        t0 = ci * CH + si * SUB
        mmv = None
        for dc in range(DC):
            d1 = k.E("dve", lambda v, dc=dc: v.tensor_copy(
                tb[:, 0:512], xf[:, dc * T + t0:dc * T + t0 + 512]),
                inc="dve",
                wait=[("te", None), ("sd", SCR["sd"]), ("gd", SCR["gd"])])
            d2 = k.E("dve", lambda v: v.tensor_tensor(
                out=tb[:, 512:1024], in0=tb[:, 0:512], in1=tb[:, 0:512],
                op=ALU.mult), inc="dve")
            k.E("te", lambda t, dc=dc: t.matmul(
                bank(0, p=1), ONE(), tb[:, 0:512],
                start=(dc == 0), stop=(dc == DC - 1)), inc="te",
                wait=[("dve", d1)])
            mmv = k.E("te", lambda t, dc=dc: t.matmul(
                bank(1, p=1), ONE(), tb[:, 512:1024],
                start=(dc == 0), stop=(dc == DC - 1)), inc="te",
                wait=[("dve", d2)])
        k.E("dve", lambda v: v.tensor_scalar(
            out=sfr[0:1, 0:512], in0=bank(0, p=1), scalar1=1.0 / D,
            scalar2=None, op0=ALU.mult), inc="dve",
            wait=[("te", mmv), ("act", None)])
        k.E("dve", lambda v: v.tensor_scalar(
            out=sfr[0:1, 512:1024], in0=bank(1, p=1), scalar1=1.0 / D,
            scalar2=None, op0=ALU.mult), inc="dve")
        k.E("dve", lambda v: v.tensor_tensor(
            out=sfr2[0:1, 0:512], in0=sfr[0:1, 0:512],
            in1=sfr[0:1, 0:512], op=ALU.mult), inc="dve")
        dv = k.E("dve", lambda v: v.tensor_tensor(
            out=sfr[0:1, 512:1024], in0=sfr[0:1, 512:1024],
            in1=sfr2[0:1, 0:512], op=ALU.subtract), inc="dve")
        k.act(lambda s: s.activation(
            out=sfr2[0:1, 0:512], in_=sfr[0:1, 512:1024], func=AF.Ln,
            bias=cons[0:1, EPSC:EPSC + 1]), AF.Ln, wait=[("dve", dv)])
        a2 = k.act(lambda s: s.activation(
            out=sfr[0:1, 512:1024], in_=sfr2[0:1, 0:512], func=AF.Exp,
            scale=-0.5), AF.Exp)
        k.E("dve", lambda v: v.tensor_tensor(
            out=sfr[0:1, 0:512], in0=sfr[0:1, 0:512],
            in1=sfr[0:1, 512:1024], op=ALU.mult), inc="dve",
            wait=[("act", a2)])
        k.E("dve", lambda v: v.tensor_copy(zrow[0:1, 0:512],
                                           sfr[0:1, 512:1024]), inc="dve",
            wait=[("te", None)])
        dv = k.E("dve", lambda v: v.tensor_copy(zrow[0:1, 512:1024],
                                                sfr[0:1, 0:512]), inc="dve")
        k.E("te", lambda t: t.matmul(
            bank(2), ONER(), zrow[0:1, 0:512], start=True, stop=True),
            inc="te", wait=[("dve", dv)])
        b2 = k.E("te", lambda t: t.matmul(
            bank(3), ONER(), zrow[0:1, 512:1024], start=True, stop=True),
            inc="te")
        dlast = None
        for dc in range(DC):
            k.E("dve", lambda v, dc=dc: v.tensor_tensor(
                out=sfr2[:, 0:512], in0=xf[:, dc * T + t0:dc * T + t0 + 512],
                in1=bank(2), op=ALU.mult), inc="dve", wait=[("te", b2)])
            dlast = k.E("dve", lambda v, dc=dc: v.tensor_tensor(
                out=xnb[:, t_(dc, SUB)], in0=sfr2[:, 0:512], in1=bank(3),
                op=ALU.subtract), inc="dve")
        if out_xnh:
            for q4 in range(4):
                trs = None
                for dc in range(DC):
                    trs = k.E("te", lambda t, dc=dc, q4=q4: t.transpose(
                        psT[:, t_(dc, 128)],
                        xnb[:, dc * SUB + q4 * 128:dc * SUB + (q4 + 1) * 128],
                        IDT()), inc="te" if dc == DC - 1 else None,
                        wait=[("dve", None), ("gd", SCR["gd"])])
                dvt = k.E("dve", lambda v: v.tensor_copy(
                    tb[:, 0:1024], psT[:, :]), inc="dve",
                    wait=[("te", trs), ("gd", SCR["gd"])])
                k.E("gp", lambda gp, t0=t0, q4=q4: gp.dma_start(
                    out=XNH[t0 + q4 * 128:t0 + (q4 + 1) * 128, :],
                    in_=tb[:, 0:1024]), inc="gd", wait=[("dve", dvt)])
                SCR["gd"] = k.c("gd")
        return dlast

    def ffn_shared_sub(ci, si, ln_dve):
        t0 = ci * CH + si * SUB
        for hc in range(4):
            g1 = k.E("sy", lambda sy, hc=hc: sy.dma_start(
                out=wsb[:, 0:1024].rearrange("p (a c) -> p a c", a=8),
                in_=S1[:, t_(hc, 128)].rearrange("(a p) c -> p a c", p=128)),
                inc="sd", wait=[("te", SCR["wte"]), ("gd", pcm["S2"])])
            g2 = k.E("sy", lambda sy, hc=hc: sy.dma_start(
                out=wsb[:, 1024:2048].rearrange("p (a c) -> p a c", a=8),
                in_=S2[:, t_(hc, 128)].rearrange("(a p) c -> p a c", p=128)),
                inc="sd")
            mm1 = mm2 = None
            for dc in range(DC):
                mm1 = k.E("te", lambda t, dc=dc: t.matmul(
                    bank(0), wsb[:, t_(dc, 128)], xnb[:, t_(dc, SUB)],
                    start=(dc == 0), stop=(dc == DC - 1)),
                    inc="te" if dc == DC - 1 else None,
                    wait=[("sd", g1), ("dve", ln_dve), ("act", None),
                          ("dve", None)])
                mm2 = k.E("te", lambda t, dc=dc: t.matmul(
                    bank(1), wsb[:, 1024 + dc * 128:1152 + dc * 128],
                    xnb[:, t_(dc, SUB)],
                    start=(dc == 0), stop=(dc == DC - 1)),
                    inc="te" if dc == DC - 1 else None,
                    wait=[("sd", g2)])
            SCR["wte"] = k.c("te")
            a1 = k.act(lambda s, hc=hc: s.activation(
                out=tb[:, 0:512], in_=bank(0), func=AF.Identity,
                bias=smal[:, SB1 + hc:SB1 + hc + 1]), AF.Identity,
                wait=[("te", mm1), ("sd", SCR["sd"]), ("gd", SCR["gd"])])
            a2 = k.act(lambda s, hc=hc: s.activation(
                out=tb2[:, 0:512], in_=bank(0), func=AF.Sigmoid,
                bias=smal[:, SB1 + hc:SB1 + hc + 1]), AF.Sigmoid)
            a3 = k.act(lambda s, hc=hc: s.activation(
                out=tb[:, 512:1024], in_=bank(1), func=AF.Identity,
                bias=smal[:, SB2 + hc:SB2 + hc + 1]), AF.Identity,
                wait=[("te", mm2)])
            d1 = k.E("dve", lambda v: v.tensor_tensor(
                out=tb2[:, 512:1024], in0=tb[:, 0:512], in1=tb2[:, 0:512],
                op=ALU.mult), inc="dve", wait=[("act", a3)])
            d2 = k.E("dve", lambda v, hc=hc: v.tensor_tensor(
                out=attn[:, t_(hc, 512)], in0=tb2[:, 512:1024],
                in1=tb[:, 512:1024], op=ALU.mult), inc="dve")
            d3 = k.E("dve", lambda v, hc=hc: v.tensor_tensor(
                out=tb[:, 0:512], in0=attn[:, t_(hc, 512)],
                in1=attn[:, t_(hc, 512)], op=ALU.mult), inc="dve")
            k.E("te", lambda t, hc=hc: t.matmul(
                bank(2, p=1), ONE(), attn[:, t_(hc, 512)],
                start=(hc == 0), stop=(hc == 3)), inc="te",
                wait=[("dve", d2)])
            k.E("te", lambda t, hc=hc: t.matmul(
                bank(3, p=1), ONE(), tb[:, 0:512],
                start=(hc == 0), stop=(hc == 3)), inc="te",
                wait=[("dve", d3)])
        mmstat = k.c("te")
        k.E("dve", lambda v: v.tensor_scalar(
            out=sfr[0:1, 0:512], in0=bank(2, p=1), scalar1=1.0 / HSH,
            scalar2=None, op0=ALU.mult), inc="dve",
            wait=[("te", mmstat), ("act", None)])
        k.E("dve", lambda v: v.tensor_scalar(
            out=sfr[0:1, 512:1024], in0=bank(3, p=1), scalar1=1.0 / HSH,
            scalar2=None, op0=ALU.mult), inc="dve")
        k.E("dve", lambda v: v.tensor_tensor(
            out=sfr2[0:1, 0:512], in0=sfr[0:1, 0:512],
            in1=sfr[0:1, 0:512], op=ALU.mult), inc="dve")
        dv = k.E("dve", lambda v: v.tensor_tensor(
            out=sfr[0:1, 512:1024], in0=sfr[0:1, 512:1024],
            in1=sfr2[0:1, 0:512], op=ALU.subtract), inc="dve")
        k.act(lambda s: s.activation(
            out=sfr2[0:1, 0:512], in_=sfr[0:1, 512:1024], func=AF.Ln,
            bias=cons[0:1, EPSC:EPSC + 1]), AF.Ln, wait=[("dve", dv)])
        a2 = k.act(lambda s: s.activation(
            out=sfr[0:1, 512:1024], in_=sfr2[0:1, 0:512], func=AF.Exp,
            scale=-0.5), AF.Exp)
        k.E("dve", lambda v: v.tensor_tensor(
            out=sfr[0:1, 0:512], in0=sfr[0:1, 0:512],
            in1=sfr[0:1, 512:1024], op=ALU.mult), inc="dve",
            wait=[("act", a2)])
        k.E("dve", lambda v: v.tensor_copy(zrow[0:1, 0:512],
                                           sfr[0:1, 512:1024]), inc="dve",
            wait=[("te", None)])
        dv = k.E("dve", lambda v: v.tensor_copy(zrow[0:1, 512:1024],
                                                sfr[0:1, 0:512]), inc="dve")
        k.E("te", lambda t: t.matmul(
            bank(2), ONER(), zrow[0:1, 0:512], start=True, stop=True),
            inc="te", wait=[("dve", dv)])
        b2 = k.E("te", lambda t: t.matmul(
            bank(3), ONER(), zrow[0:1, 512:1024], start=True, stop=True),
            inc="te")
        dlast = None
        for hc in range(4):
            k.E("dve", lambda v, hc=hc: v.tensor_tensor(
                out=sfr2[:, 0:512], in0=attn[:, t_(hc, 512)],
                in1=bank(2), op=ALU.mult), inc="dve", wait=[("te", b2)])
            k.E("dve", lambda v: v.tensor_tensor(
                out=sfr2[:, 0:512], in0=sfr2[:, 0:512], in1=bank(3),
                op=ALU.subtract), inc="dve")
            dlast = k.E("dve", lambda v, hc=hc: v.tensor_scalar(
                out=attn[:, t_(hc, 512)], in0=sfr2[:, 0:512],
                scalar1=smal[:, SLG + hc:SLG + hc + 1],
                scalar2=smal[:, SLB + hc:SLB + hc + 1],
                op0=ALU.mult, op1=ALU.add), inc="dve")
        for tt4 in range(4):
            mm = None
            for hc in range(4):
                for dh in range(2):
                    mm = k.E("te", lambda t, hc=hc, tt4=tt4, dh=dh: t.matmul(
                        psA[:, 2048 + dh * 512:2048 + (dh + 1) * 512],
                        attn[:, o_(hc * 512, tt4, 128)],
                        qbf[:, hc * 1024 + dh * 512:
                            hc * 1024 + (dh + 1) * 512],
                        start=(hc == 0), stop=False),
                        inc=None,
                        wait=[("dve", dlast), ("sd", SCR["s3"]),
                              ("act", None)])
            for dh in range(2):
                mm = k.E("te", lambda t, dh=dh: t.matmul(
                    psA[:, 2048 + dh * 512:2048 + (dh + 1) * 512],
                    ONER(), sb3r[0:1, t_(dh, 512)],
                    start=False, stop=True), inc="te" if dh == 1 else None,
                    wait=[("dve", consts_ready_dve)])
            a1 = k.act(lambda s, tt4=tt4: s.activation(
                out=tb[:, 0:1024], in_=bank(4, w=1024), func=AF.Identity),
                AF.Identity, wait=[("te", mm), ("sd", SCR["sd"])])
            k.E("sy", lambda s, t0=t0, tt4=tt4: s.dma_start(
                out=SHH[t0 + tt4 * 128:t0 + (tt4 + 1) * 128, :],
                in_=tb[:, 0:1024]), inc="sd", wait=[("act", a1)])
            SCR["sd"] = k.c("sd")
        return k.c("te")

    def moe_phase():
        MTL = ML // 128
        gth = []
        for mt in range(16):
            g = k.E("gp", lambda gp, mt=mt: gp.indirect_dma_start(
                out=vbf[:, t_(mt, D)], out_offset=None, in_=XNH[:, :],
                in_offset=IndirectOffsetOnAxis(ap=gidxs[:, mt:mt + 1],
                                               axis=0)),
                inc="gd", wait=[("gd", None), ("te", None), ("dve", None)])
            gth.append(g)
        for mt in range(16):
            k.E("gp", lambda gp, mt=mt: gp.dma_start(
                out=RSI[t_(mt, 128), :], in_=vbf[:, t_(mt, D)]),
                inc="gd", wait=[("gd", gth[mt])])
        rs_w = k.c("gd")
        k.E("gp", lambda gp: gp.collective_compute(
            "ReduceScatter", ALU.add, ins=[RSI[:, :]], outs=[RSO[:, :]],
            replica_groups=[list(range(8))]), inc="gp",
            wait=[("gd", rs_w)])
        rs_done = k.c("gp")
        for mt in range(MTL):
            k.E("gp", lambda gp, mt=mt: gp.dma_start(
                out=vbf[:, t_(mt, D)], in_=RSO[t_(mt, 128), :]),
                inc="gd", wait=[("gp", rs_done)])
        slice_gd = k.c("gd")
        for dc in range(DC):
            trs = None
            for mt in range(MTL):
                trs = k.E("te", lambda t, dc=dc, mt=mt: t.transpose(
                    psT[:, t_(mt, 128)],
                    vbf[:, mt * D + dc * 128: mt * D + (dc + 1) * 128],
                    IDT()), inc="te" if mt == MTL - 1 else None,
                    wait=[("gd", slice_gd), ("dve", None)])
            k.E("dve", lambda v, dc=dc: v.tensor_copy(
                kbf[:, t_(dc * MTL, 128 * MTL)][:, 0:ML] if False else
                kbf[:, dc * ML: dc * ML + ML], psT[:, 0:ML]), inc="dve",
                wait=[("te", trs)])
        xm_ok = k.c("dve")
        for mt in range(MTL):
            mm = None
            for dc in range(DC):
                mm = k.E("te", lambda t, dc=dc, mt=mt: t.matmul(
                    bank(0, w=16), kbf[:, o_(dc * ML, mt, 128)],
                    gwsb[:, t_(dc, 16)], start=(dc == 0), stop=False),
                    inc=None,
                    wait=[("dve", xm_ok), ("sd", setup["sd"]), ("act", None),
                          ("dve", None)])
            mm = k.E("te", lambda t: t.matmul(
                bank(0, w=16), ONER(), zrow[0:1, 1024:1040],
                start=False, stop=True), inc="te", wait=[("dve", gb_ok)])
            a1 = k.act(lambda s: s.activation(
                out=gat[:, 0:16], in_=bank(0, w=16), func=AF.Sigmoid),
                AF.Sigmoid, wait=[("te", mm)])
            k.E("dve", lambda v: v.tensor_tensor(
                out=gat[:, 16:24], in0=gat[:, 1:9],
                in1=cons[:, EBC:EBC + 8], op=ALU.add), inc="dve",
                wait=[("act", a1)])
            k.E("dve", lambda v: v.reduce_max(gat[:, 24:25], gat[:, 16:24],
                                              axis=AX.X), inc="dve")
            k.E("dve", lambda v: v.tensor_scalar(
                out=gat[:, 32:40], in0=cons[:, IOC:IOC + 8], scalar1=0.0,
                scalar2=gat[:, 24:25], op0=ALU.mult, op1=ALU.add),
                inc="dve")
            k.E("dve", lambda v: v.max_index(idx8[:], gat[:, 32:40],
                                             gat[:, 16:24]), inc="dve")
            k.E("dve", lambda v: v.tensor_copy(gat[:, 25:26],
                                               idx8[:, 0:1]), inc="dve")
            k.E("dve", lambda v: v.tensor_scalar(
                out=gat[:, 32:40], in0=cons[:, IOC:IOC + 8],
                scalar1=gat[:, 25:26], scalar2=None, op0=ALU.is_equal),
                inc="dve")
            k.E("dve", lambda v: v.tensor_tensor(
                out=gat[:, 40:48], in0=gat[:, 32:40], in1=gat[:, 1:9],
                op=ALU.mult), inc="dve")
            k.E("dve", lambda v: v.reduce_sum(gat[:, 26:27], gat[:, 40:48],
                                              axis=AX.X), inc="dve")
            dv = k.E("dve", lambda v: v.tensor_tensor(
                out=gat[:, 27:28], in0=gat[:, 0:1], in1=gat[:, 26:27],
                op=ALU.subtract), inc="dve")
            a2 = k.act(lambda s: s.activation(
                out=gat[:, 28:29], in_=gat[:, 27:28], func=AF.Sigmoid),
                AF.Sigmoid, wait=[("dve", dv)])
            k.E("dve", lambda v: v.tensor_scalar(
                out=gat[:, 29:30], in0=gat[:, 28:29], scalar1=-1.0,
                scalar2=1.0, op0=ALU.mult, op1=ALU.add), inc="dve",
                wait=[("act", a2)])
            k.E("dve", lambda v, mt=mt: v.tensor_scalar(
                out=gat2[:, mt:mt + 1], in0=gat[:, 29:30], scalar1=-1.0,
                scalar2=None, op0=ALU.mult), inc="dve")
            dv = k.E("dve", lambda v: v.tensor_scalar(
                out=gat[:, 40:48], in0=gat[:, 32:40],
                scalar1=gat[:, 29:30], scalar2=None, op0=ALU.mult),
                inc="dve")
            dv = k.E("dve", lambda v: v.tensor_copy(tb[:, 0:8],
                                                    gat[:, 40:48]),
                     inc="dve",
                     wait=[("sd", SCR["sd"]), ("gd", SCR["gd"])])
            tr = k.E("te", lambda t: t.transpose(
                psT[:, 0:128], tb[:, 0:128], IDT()), inc="te",
                wait=[("dve", dv)])
            k.E("dve", lambda v, mt=mt: v.tensor_copy(
                svs[:, t_(mt, 128)], psT[0:8, 0:128]), inc="dve",
                wait=[("te", tr)])
        gating_done = k.c("dve")
        m0 = 0
        ZS = ML
        stat_mm = None
        for e in range(NE):
            for hc in range(4):
                hh = (hc % 2) * 2048
                g1 = k.E("sy", lambda sy, e=e, hc=hc, hh=hh: sy.dma_start(
                    out=wsb[:, hh:hh + 1024].rearrange(
                        "p (a c) -> p a c", a=8),
                    in_=E1[e][:, t_(hc, 128)].rearrange(
                        "(a p) c -> p a c", p=128)),
                    inc="sd", wait=[("te", SCR["wte"]), ("gd", pcm["E2"])])
                g2 = k.E("sy", lambda sy, e=e, hc=hc, hh=hh: sy.dma_start(
                    out=wsb[:, hh + 1024:hh + 2048].rearrange(
                        "p (a c) -> p a c", a=8),
                    in_=E2[e][:, t_(hc, 128)].rearrange(
                        "(a p) c -> p a c", p=128)),
                    inc="sd")
                mm1 = mm2 = None
                for dc in range(DC):
                    mm1 = k.E("te", lambda t, dc=dc, hh=hh: t.matmul(
                        bank(0, w=ML), wsb[:, o_(hh, dc, 128)],
                        kbf[:, dc * ML:dc * ML + ML],
                        start=(dc == 0), stop=(dc == DC - 1)),
                        inc="te" if dc == DC - 1 else None,
                        wait=[("sd", g1), ("act", None), ("dve", None)])
                    mm2 = k.E("te", lambda t, dc=dc, hh=hh: t.matmul(
                        bank(1, w=ML), wsb[:, o_(hh + 1024, dc, 128)],
                        kbf[:, dc * ML:dc * ML + ML],
                        start=(dc == 0), stop=(dc == DC - 1)),
                        inc="te" if dc == DC - 1 else None,
                        wait=[("sd", g2)])
                SCR["wte"] = k.c("te")
                a1 = k.act(lambda s, e=e, hc=hc: s.activation(
                    out=tb[:, 0:ML], in_=bank(0, w=ML), func=AF.Identity,
                    bias=smal2[:, e * 4 + hc:e * 4 + hc + 1]),
                    AF.Identity,
                    wait=[("te", mm1), ("sd", SCR["sd"]),
                          ("gd", SCR["gd"])])
                a2 = k.act(lambda s, e=e, hc=hc: s.activation(
                    out=tb2[:, 0:ML], in_=bank(0, w=ML), func=AF.Sigmoid,
                    bias=smal2[:, e * 4 + hc:e * 4 + hc + 1]),
                    AF.Sigmoid)
                a3 = k.act(lambda s, e=e, hc=hc: s.activation(
                    out=tb[:, 512:512 + ML], in_=bank(1, w=ML),
                    func=AF.Identity,
                    bias=smal2[:, 32 + e * 4 + hc:33 + e * 4 + hc]),
                    AF.Identity, wait=[("te", mm2)])
                zoff = (e * 4 + hc) * ZS
                k.E("dve", lambda v: v.tensor_tensor(
                    out=tb2[:, 512:512 + ML], in0=tb[:, 0:ML],
                    in1=tb2[:, 0:ML], op=ALU.mult), inc="dve",
                    wait=[("act", a3)])
                d2 = k.E("dve", lambda v, zoff=zoff: v.tensor_tensor(
                    out=vbf[:, zoff:zoff + ML], in0=tb2[:, 512:512 + ML],
                    in1=tb[:, 512:512 + ML], op=ALU.mult), inc="dve",
                    wait=[("te", None)])
                d3 = k.E("dve", lambda v, zoff=zoff: v.tensor_tensor(
                    out=tb[:, 0:ML], in0=vbf[:, zoff:zoff + ML],
                    in1=vbf[:, zoff:zoff + ML], op=ALU.mult), inc="dve")
                first = (e == 0 and hc == 0)
                last = (e == NE - 1 and hc == 3)
                k.E("te", lambda t, e=e, zoff=zoff, first=first,
                    last=last: t.matmul(
                        bank(2, w=ML, p=8),
                        conb[:, OH8 + e * 8:OH8 + e * 8 + 8],
                        vbf[:, zoff:zoff + ML],
                        start=first, stop=last), inc="te",
                    wait=[("dve", d2), ("act", None)])
                stat_mm = k.E("te", lambda t, e=e, first=first,
                              last=last: t.matmul(
                        bank(3, w=ML, p=8),
                        conb[:, OH8 + e * 8:OH8 + e * 8 + 8],
                        tb[:, 0:ML], start=first, stop=last), inc="te",
                    wait=[("dve", d3), ("act", None)])
        k.E("dve", lambda v: v.tensor_scalar(
            out=sfr[0:8, 0:ML], in0=bank(2, w=ML, p=8), scalar1=1.0 / HSH,
            scalar2=None, op0=ALU.mult), inc="dve",
            wait=[("te", stat_mm), ("act", None)])
        k.E("dve", lambda v: v.tensor_scalar(
            out=sfr[0:8, 512:512 + ML], in0=bank(3, w=ML, p=8),
            scalar1=1.0 / HSH, scalar2=None, op0=ALU.mult), inc="dve")
        k.E("dve", lambda v: v.tensor_tensor(
            out=sfr2[0:8, 0:ML], in0=sfr[0:8, 0:ML],
            in1=sfr[0:8, 0:ML], op=ALU.mult), inc="dve")
        dv = k.E("dve", lambda v: v.tensor_tensor(
            out=sfr[0:8, 512:512 + ML], in0=sfr[0:8, 512:512 + ML],
            in1=sfr2[0:8, 0:ML], op=ALU.subtract), inc="dve")
        k.act(lambda s: s.activation(
            out=sfr2[0:8, 0:ML], in_=sfr[0:8, 512:512 + ML],
            func=AF.Ln, bias=cons[0:8, EPSC:EPSC + 1]), AF.Ln,
            wait=[("dve", dv)])
        a2 = k.act(lambda s: s.activation(
            out=sfr[0:8, 512:512 + ML], in_=sfr2[0:8, 0:ML],
            func=AF.Exp, scale=-0.5), AF.Exp)
        k.E("dve", lambda v: v.tensor_tensor(
            out=sfr2[0:8, 0:ML], in0=sfr[0:8, 512:512 + ML],
            in1=svs[:, 0:ML], op=ALU.mult), inc="dve",
            wait=[("act", a2), ("dve", gating_done)])
        k.E("dve", lambda v: v.tensor_copy(zrow[0:8, 0:ML],
                                           sfr2[0:8, 0:ML]),
            inc="dve", wait=[("te", None)])
        k.E("dve", lambda v: v.tensor_tensor(
            out=sfr[0:8, 0:ML], in0=sfr[0:8, 0:ML],
            in1=sfr2[0:8, 0:ML], op=ALU.mult), inc="dve")
        dvr = k.E("dve", lambda v: v.tensor_copy(zrow[0:8, 512:512 + ML],
                                                 sfr[0:8, 0:ML]),
                  inc="dve")
        zn_done = None
        for e in range(NE):
            b1 = k.E("te", lambda t, e=e: t.matmul(
                bank(4, w=ML), cob2[:, t_(e, 128)], zrow[0:8, 0:ML],
                start=True, stop=True), inc="te",
                wait=[("dve", None), ("act", None)])
            b2 = k.E("te", lambda t, e=e: t.matmul(
                bank(5, w=ML), cob2[:, t_(e, 128)], zrow[0:8, 512:512 + ML],
                start=True, stop=True), inc="te")
            for hc in range(4):
                zoff = (e * 4 + hc) * ZS
                k.E("dve", lambda v, zoff=zoff: v.tensor_tensor(
                    out=sfr2[:, 0:ML], in0=vbf[:, zoff:zoff + ML],
                    in1=bank(4, w=ML), op=ALU.mult), inc="dve",
                    wait=[("te", b2)])
                k.E("dve", lambda v, zoff=zoff: v.tensor_tensor(
                    out=sfr2[:, 0:ML], in0=sfr2[:, 0:ML],
                    in1=bank(5, w=ML), op=ALU.subtract), inc="dve")
                zn_done = k.E("dve", lambda v, zoff=zoff, hc=hc:
                              v.tensor_scalar(
                    out=vbf[:, zoff:zoff + ML], in0=sfr2[:, 0:ML],
                    scalar1=smal[:, SELG + hc:SELG + hc + 1],
                    scalar2=None, op0=ALU.mult), inc="dve")
        for dh in range(2):
            evac_prev = k.c("act")
            mm_by_mt = [None] * (ML // 128)
            for e in range(NE):
                eh = (e % 2) * 2048
                g = k.E("sy", lambda sy, e=e, dh=dh, eh=eh: sy.dma_start(
                    out=wsb[:, eh:eh + 2048].rearrange(
                        "p (a c) -> p a c", a=4),
                    in_=E3[e][:, t_(dh, 512)].rearrange(
                        "(a p) c -> p a c", p=128)),
                    inc="sd", wait=[("te", SCR["wte"]), ("gd", pcm["E3"])])
                for mt4 in range(ML // 128):
                    for hc in range(4):
                        first = (e == 0 and hc == 0)
                        mm_by_mt[mt4] = k.E(
                            "te", lambda t, e=e, mt4=mt4, hc=hc,
                            first=first, eh=eh: t.matmul(
                                bank(mt4),
                                vbf[:, o_((e * 4 + hc) * ZS, mt4, 128)],
                                wsb[:, o_(eh, hc, 512)],
                                start=first, stop=False),
                            inc="te" if (mt4 == ML // 128 - 1 and hc == 3)
                            else None,
                            wait=([("sd", g), ("dve", zn_done),
                                   ("act", evac_prev)] if first or hc == 0
                                  else [("sd", g)]))
                SCR["wte"] = k.c("te")
            for mt4 in range(ML // 128):
                mm_by_mt[mt4] = k.E("te", lambda t, mt4=mt4, dh=dh:
                                    t.matmul(
                        bank(mt4), svs[:, t_(mt4, 128)],
                        e3es[:, t_(dh, 512)], start=False, stop=True),
                    inc="te", wait=[("sd", setup["sd"])])
            for mt4 in range(ML // 128):
                a1 = k.act(lambda s, mt4=mt4: s.activation(
                    out=tb[:, 0:512], in_=bank(mt4), func=AF.Identity),
                    AF.Identity,
                    wait=[("te", mm_by_mt[mt4]), ("sd", SCR["sd"])])
                k.E("sy", lambda s, mt4=mt4, dh=dh: s.dma_start(
                    out=AGI[t_(mt4, 128), t_(dh, 512)],
                    in_=tb[:, 0:512]), inc="sd",
                    wait=[("act", a1), ("gd", None)])
                SCR["sd"] = k.c("sd")
        for mt4 in range(ML // 128):
            dv = k.E("dve", lambda v, mt4=mt4: v.tensor_copy(
                tb2[:, 0:1], gat2[:, mt4:mt4 + 1]), inc="dve",
                wait=[("sd", SCR["sd"])])
            dv2 = k.E("dve", lambda v: v.memset(tb2[:, 1:8], 0.0),
                      inc="dve")
            k.E("sy", lambda s, mt4=mt4: s.dma_start(
                out=AGI[t_(mt4, 128), 1024:1032],
                in_=tb2[:, 0:8]), inc="sd", wait=[("dve", dv2)])
            SCR["sd"] = k.c("sd")
        ag_w = k.c("sd")
        k.E("gp", lambda gp: gp.collective_compute(
            "AllGather", ALU.bypass, ins=[AGI[:, :]],
            outs=[AGO[0:M, :]], replica_groups=[list(range(8))]),
            inc="gp", wait=[("sd", ag_w), ("gd", None)])

    def corrections():
        ag_done = k.c("gp")
        for tt in range(16):
            g1 = k.E("sy", lambda s, tt=tt: s.dma_start(
                out=tb[:, 0:1024], in_=SHH[t_(tt, 128), :]), inc="sd",
                wait=[("dve", None), ("act", None)])
            g2 = k.E("gp", lambda gp, tt=tt: gp.indirect_dma_start(
                out=tb2[:], out_offset=None, in_=AGO[:, :],
                in_offset=IndirectOffsetOnAxis(ap=ridxs[:, tt:tt + 1],
                                               axis=0)),
                inc="gd", wait=[("gp", ag_done), ("dve", None)])
            dv = k.E("dve", lambda v: v.tensor_scalar(
                out=sfr2[:, 0:1], in0=tb2[:, 1024:1025], scalar1=1.0,
                scalar2=None, op0=ALU.add), inc="dve",
                wait=[("gd", g2), ("sd", g1)])
            dv = k.E("dve", lambda v: v.scalar_tensor_tensor(
                out=tb2[:, 0:1024], in0=tb[:, 0:1024],
                scalar=sfr2[:, 0:1], in1=tb2[:, 0:1024],
                op0=ALU.mult, op1=ALU.add), inc="dve")
            trs = None
            for dc in range(DC):
                trs = k.E("te", lambda t, dc=dc: t.transpose(
                    psT[:, t_(dc, 128)], tb2[:, t_(dc, 128)], IDT()),
                    inc="te" if dc == DC - 1 else None,
                    wait=[("dve", dv)])
            xfv = xf[:, :].rearrange("p (dc t) -> p dc t", dc=DC)
            psv = psT[:, :].rearrange("p (dc t) -> p dc t", dc=DC)
            k.E("dve", lambda v, tt=tt, xfv=xfv, psv=psv:
                v.scalar_tensor_tensor(
                    out=xfv[:, :, t_(tt, 128)], in0=psv[:, :, :],
                    scalar=0.5, in1=xfv[:, :, t_(tt, 128)],
                    op0=ALU.mult, op1=ALU.add), inc="dve",
                wait=[("te", trs)])

    def qkv_sub(ci, si, ln_dve):
        t0 = ci * CH + si * SUB
        for fc in range(16):
            h = fc % 2
            g = k.E("sy", lambda sy, fc=fc, h=h: sy.dma_start(
                out=wsb[:, h * 2048:h * 2048 + 1024].rearrange(
                    "p (a c) -> p a c", a=8),
                in_=A_w[:, t_(fc, 128)].rearrange("(a p) c -> p a c",
                                                  p=128)),
                inc="sd", wait=[("te", SCR["wte"]), ("gd", pcm["A_w"])])
            mm = None
            for dc in range(DC):
                mm = k.E("te", lambda t, dc=dc, fc=fc, h=h: t.matmul(
                    bank(h), wsb[:, o_(h * 2048, dc, 128)],
                    xnb[:, t_(dc, SUB)],
                    start=(dc == 0), stop=(dc == DC - 1)),
                    inc="te" if dc == DC - 1 else None,
                    wait=[("sd", g), ("dve", ln_dve), ("dve", None)])
            SCR["wte"] = k.c("te")
            if fc < 8:
                dst = qbf[:, fc * CH + si * SUB: fc * CH + si * SUB + 512]
            else:
                c0 = (fc - 8) * T + t0
                dst = kbf[:, c0:c0 + 512]
            dv = k.E("dve", lambda v, dst=dst, h=h: v.tensor_copy(
                dst, bank(h)), inc="dve",
                wait=[("te", mm)] + ([("gd", precast)] if fc == 0 else []))
            dv2 = k.E("dve", lambda v, dst=dst: v.tensor_tensor(
                out=tb[:, 0:512], in0=dst, in1=dst, op=ALU.mult),
                inc="dve")
            mm2 = k.E("te", lambda t: t.matmul(
                bank(2, p=1), ONE(), tb[:, 0:512], start=True, stop=True),
                inc="te", wait=[("dve", dv2)])
            k.E("dve", lambda v, fc=fc, si=si: v.reduce_max(
                km2[0:1, 16 + (fc % 2):17 + (fc % 2)], bank(2, w=512, p=1),
                axis=AX.X), inc="dve", wait=[("te", mm2)])
            slot = fc
            fresh = (si == 0) if fc < 8 else (ci == 0 and si == 0)
            if fresh:
                k.E("dve", lambda v, slot=slot, fc=fc: v.tensor_copy(
                    km2[0:1, slot:slot + 1],
                    km2[0:1, 16 + (fc % 2):17 + (fc % 2)]), inc="dve")
            else:
                k.E("dve", lambda v, slot=slot, fc=fc: v.tensor_tensor(
                    out=km2[0:1, slot:slot + 1],
                    in0=km2[0:1, slot:slot + 1],
                    in1=km2[0:1, 16 + (fc % 2):17 + (fc % 2)],
                    op=ALU.max), inc="dve")
        for dh in range(2):
            g1 = k.E("sy", lambda sy, dh=dh: sy.dma_start(
                out=wsb[:, 0:2048].rearrange("p (a c) -> p a c", a=4),
                in_=A_w[0:512, o_(2048, dh, 512)].rearrange(
                    "(a p) c -> p a c", p=128)),
                inc="sd", wait=[("te", SCR["wte"]), ("gd", pcm["A_w"])])
            g2 = k.E("sy", lambda sy, dh=dh: sy.dma_start(
                out=wsb[:, 2048:4096].rearrange("p (a c) -> p a c", a=4),
                in_=A_w[512:1024, o_(2048, dh, 512)].rearrange(
                    "(a p) c -> p a c", p=128)),
                inc="sd")
            for tt4 in range(4):
                b = 3 + (tt4 % 2)
                mm = None
                for dc in range(DC):
                    mm = k.E("te", lambda t, dc=dc, tt4=tt4, b=b: t.matmul(
                        bank(b), xnb[:, o_(dc * SUB, tt4, 128)],
                        wsb[:, t_(dc, 512)],
                        start=(dc == 0), stop=(dc == DC - 1)),
                        inc="te" if dc == DC - 1 else None,
                        wait=[("sd", g1), ("sd", g2), ("dve", None)])
                SCR["wte"] = k.c("te")
                tt = t0 // 128 + tt4
                k.E("dve", lambda v, tt=tt, dh=dh, b=b: v.tensor_copy(
                    vbf[:, o_(tt * D, dh, 512)], bank(b)), inc="dve",
                    wait=[("te", mm)])
        return k.c("dve")

    def stabilizer(ci):
        dv = k.E("dve", lambda v: v.tensor_tensor(
            out=km2[0:1, 16:24], in0=km2[0:1, 0:8], in1=km2[0:1, 8:16],
            op=ALU.mult), inc="dve", wait=[("te", None)])
        k.act(lambda s: s.activation(
            out=km2[0:1, 16:24], in_=km2[0:1, 16:24], func=AF.Ln,
            bias=cons[0:1, EPSC + 1:EPSC + 2]), AF.Ln, wait=[("dve", dv)])
        a = k.act(lambda s: s.activation(
            out=km2[0:1, 16:24], in_=km2[0:1, 16:24], func=AF.Exp,
            scale=0.5), AF.Exp)
        dv = k.E("dve", lambda v: v.tensor_scalar(
            out=km2[0:1, 16:24], in0=km2[0:1, 16:24], scalar1=-1.0 / 32.0,
            scalar2=None, op0=ALU.mult), inc="dve", wait=[("act", a)])
        dv = k.E("dve", lambda v: v.tensor_copy(zrow[0:1, 1044:1052],
                                                km2[0:1, 16:24]), inc="dve",
                 wait=[("te", None)])
        mm = k.E("te", lambda t: t.matmul(
            bank(2, w=8), ONER(), zrow[0:1, 1044:1052], start=True, stop=True),
            inc="te", wait=[("dve", dv)])
        k.E("dve", lambda v: v.tensor_copy(csc[:, 0:8], bank(2, w=8, p=128)),
            inc="dve", wait=[("te", mm), ("act", None)])
        return k.c("dve")

    def attention_chunk(ci, csc_dve):
        qc0 = ci * CH
        for pr in range(NPAIR):
            fc = pr // 2
            po0 = 64 * (pr % 2)
            grp = fc
            for qs in range(2):
                qoff = qc0 + qs * SUB
                qcol = qs * SUB
                kbmax = (qoff + SUB) // 128
                act_of = {}
                te_av = {}

                def emit_scores(kb):
                    par = kb % 2
                    sb0 = par * 1024
                    diag = (kb >= kbmax - 4)
                    for hi in range(2):
                        po = po0 + 32 * hi
                        sl = psA[:, sb0 + hi * 512: sb0 + (hi + 1) * 512]
                        w = [("act", act_of.get(kb - 2, 0)), ("dve", None)]
                        if kb == 0 and hi == 0:
                            w.append(("dve", csc_dve))
                        is_last = (hi == 1 and not diag)
                        k.E("te", lambda t, fc=fc, po=po, kb=kb,
                            qoff=qoff, sl=sl, diag=diag:
                            t.matmul(
                                sl,
                                kbf[po:po + 32,
                                    fc * T + kb * 128:
                                    fc * T + (kb + 1) * 128],
                                qbf[po:po + 32,
                                    fc * CH + qoff - qc0:
                                    fc * CH + qoff - qc0 + 512],
                                start=True, stop=not diag,
                                tile_position=(po, 0)),
                            inc="te" if is_last else None, wait=w)
                        if diag:
                            qw = kb * 128 - qoff
                            k.E("te", lambda t, sl=sl, qw=qw:
                                t.matmul(
                                    sl,
                                    conb[:, MAC:MAC + 128],
                                    conb[:, MBW + 384 - qw:
                                         MBW + 384 - qw + 512],
                                    start=False, stop=True,
                                    tile_position=(0, 0)),
                                inc="te" if hi == 1 else None)
                    return k.c("te")

                def emit_exp(kb, sc_done):
                    sb0 = (kb % 2) * 1024
                    a = k.act(lambda s, grp=grp, sb0=sb0: s.activation(
                        out=pbuf[:, sb0:sb0 + 1024],
                        in_=psA[:, sb0:sb0 + 1024],
                        func=AF.Exp, scale=1.0 / 32.0,
                        bias=csc[:, grp:grp + 1]), AF.Exp,
                        wait=[("te", sc_done),
                              ("te", te_av.get(kb - 2, 0))])
                    act_of[kb] = a

                def emit_av(kb):
                    sb0 = (kb % 2) * 1024
                    a = act_of[kb]
                    first = (kb == 0)
                    lastk = (kb == kbmax - 1)
                    k.E("te", lambda t, kb=kb, pr=pr, sb0=sb0,
                        first=first, lastk=lastk: t.matmul(
                            bank(4, p=64), vbf[:, kb * D + 64 * pr:
                                               kb * D + 64 * pr + 64],
                            pbuf[:, sb0:sb0 + 512],
                            start=first, stop=lastk,
                            tile_position=(0, 0)),
                        inc=None,
                        wait=([("act", a), ("dve", None)] if first
                              else [("act", a)]))
                    k.E("te", lambda t, kb=kb, pr=pr, sb0=sb0,
                        first=first, lastk=lastk: t.matmul(
                            bank(4)[64:128, :],
                            vbf[:, kb * D + 64 * pr:
                                 kb * D + 64 * pr + 64],
                            pbuf[:, sb0 + 512:sb0 + 1024],
                            start=first, stop=lastk,
                            tile_position=(0, 64),
                            skip_group_check=True),
                        inc=None)
                    k.E("te", lambda t, sb0=sb0, first=first,
                        lastk=lastk: t.matmul(
                        bank(5, p=1), ONE(), pbuf[:, sb0:sb0 + 512],
                        start=first, stop=lastk,
                        tile_position=(0, 0)), inc=None,
                        wait=([("dve", None)] if first else []))
                    te_av[kb] = k.E("te", lambda t, sb0=sb0, first=first,
                        lastk=lastk: t.matmul(
                        psA[32:33, 2560:3072], ONE(),
                        pbuf[:, sb0 + 512:sb0 + 1024],
                        start=first, stop=lastk,
                        tile_position=(0, 32),
                        skip_group_check=True), inc="te")

                for kb in range(kbmax + 1):
                    if kb < kbmax:
                        sc = emit_scores(kb)
                        emit_exp(kb, sc)
                    if kb >= 1:
                        emit_av(kb - 1)
                av_done = k.c("te")
                k.act(lambda s: s.activation(
                    out=sfr[0:1, 0:512], in_=psA[0:1, 2560:3072],
                    func=AF.Ln), AF.Ln,
                    wait=[("te", av_done), ("dve", None)])
                k.act(lambda s: s.activation(
                    out=zrow[0:1, 0:512], in_=sfr[0:1, 0:512],
                    func=AF.Exp, scale=-1.0), AF.Exp)
                k.act(lambda s: s.activation(
                    out=sfr[32:33, 0:512], in_=psA[32:33, 2560:3072],
                    func=AF.Ln), AF.Ln)
                az = k.act(lambda s: s.activation(
                    out=zrow[32:33, 0:512], in_=sfr[32:33, 0:512],
                    func=AF.Exp, scale=-1.0), AF.Exp)
                mm = k.E("te", lambda t: t.matmul(
                    bank(6), conb[0:64, LABC:LABC + 128], zrow[0:64, 0:512],
                    start=True, stop=True, tile_position=(0, 0)),
                    inc="te", wait=[("act", az), ("dve", None)])
                dab = k.E("dve", lambda v: v.tensor_copy(tb[:, 0:512],
                                                         bank(6)),
                          inc="dve", wait=[("te", mm)])
                dt = k.E("dve", lambda v: v.tensor_tensor(
                    out=tb2[:, 0:512], in0=bank(4), in1=tb[:, 0:512],
                    op=ALU.mult), inc="dve")
                mm = k.E("te", lambda t: t.matmul(
                    bank(6), csb[:, :], tb2[:, 0:512], start=True,
                    stop=True), inc="te", wait=[("dve", dt)])
                a1 = k.act(lambda s: s.activation(
                    out=tb[0:64, 0:512], in_=bank(6, p=64),
                    func=AF.Identity), AF.Identity, wait=[("te", mm)])
                dd = k.E("dve", lambda v: v.tensor_tensor(
                    out=tb[0:64, 512:1024], in0=tb[0:64, 0:512],
                    in1=tb[0:64, 0:512], op=ALU.mult), inc="dve",
                    wait=[("act", a1)])
                mm = k.E("te", lambda t: t.matmul(
                    psA[0:1, 2560:3072], conb[0:64, ONC:ONC + 1],
                    tb[0:64, 512:1024], start=True, stop=True), inc="te",
                    wait=[("dve", dd)])
                k.act(lambda s: s.activation(
                    out=sfr2[0:1, 0:512], in_=psA[0:1, 2560:3072],
                    func=AF.Ln, scale=1.0 / 64.0,
                    bias=cons[0:1, EPSC:EPSC + 1]), AF.Ln,
                    wait=[("te", mm)])
                ag = k.act(lambda s: s.activation(
                    out=sfr[0:1, 512:1024], in_=sfr2[0:1, 0:512],
                    func=AF.Exp, scale=-0.5), AF.Exp)
                dg = k.E("dve", lambda v: v.tensor_copy(
                    zrow[0:1, 512:1024], sfr[0:1, 512:1024]), inc="dve",
                    wait=[("act", ag)])
                mm = k.E("te", lambda t: t.matmul(
                    bank(6, p=64), grow[0:1, :], zrow[0:1, 512:1024],
                    start=True, stop=True), inc="te", wait=[("dve", dg)])
                k.E("dve", lambda v, pr=pr, qcol=qcol: v.tensor_tensor(
                    out=attn[64 * (pr % 2):64 * (pr % 2) + 64,
                             (pr // 2) * CH + qcol:
                             (pr // 2) * CH + qcol + 512],
                    in0=tb[0:64, 0:512], in1=bank(6, p=64), op=ALU.mult),
                    inc="dve", wait=[("te", mm)])
        return k.c("dve")

    def proj_chunk(ci, attn_dve):
        for si in range(2):
            for dc in range(DC):
                h = dc % 2
                g = k.E("sy", lambda sy, dc=dc, h=h: sy.dma_start(
                    out=wsb[:, h * 2048:h * 2048 + 1024].rearrange(
                        "p (a c) -> p a c", a=8),
                    in_=A_p[:, t_(dc, 128)].rearrange(
                        "(a p) c -> p a c", p=128)),
                    inc="sd", wait=[("te", SCR["wte"]), ("gd", pcm["A_p"])])
                mm = None
                for ac in range(8):
                    mm = k.E("te", lambda t, ac=ac, si=si, h=h: t.matmul(
                        bank(h), wsb[:, o_(h * 2048, ac, 128)],
                        attn[:, o_(ac * CH, si, 512)],
                        start=(ac == 0), stop=(ac == 7)),
                        inc="te" if ac == 7 else None,
                        wait=[("sd", g), ("dve", attn_dve), ("dve", None)])
                SCR["wte"] = k.c("te")
                k.E("dve", lambda v, dc=dc, ci=ci, si=si, h=h:
                    v.scalar_tensor_tensor(
                        out=xf[:, dc * T + ci * CH + si * 512:
                               dc * T + ci * CH + (si + 1) * 512],
                        in0=bank(h), scalar=1.0,
                        in1=xf[:, dc * T + ci * CH + si * 512:
                               dc * T + ci * CH + (si + 1) * 512],
                        op0=ALU.mult, op1=ALU.add), inc="dve",
                    wait=[("te", mm)])
        return k.c("dve")

    import os
    nphase = int(os.environ.get("KERN_PHASES", "3"))
    for ci in range(2):
        for si in range(2):
            lnd = ln_sub(ci, si, out_xnh=True)
            ffn_shared_sub(ci, si, lnd)
    if not setup["done"]:
        eb3_setup()
        setup["done"] = True
    moe_phase()
    corrections()

    if nphase >= 2:
        for ci in range(2):
            for si in range(2):
                lnd = ln_sub(ci, si, out_xnh=False)
                qkv_sub(ci, si, lnd)
            cdve = stabilizer(ci)
            adve = attention_chunk(ci, cdve)
            proj_chunk(ci, adve)

    if nphase >= 3:
        SCR["s3"] = k.E("sy", lambda s: s.dma_start(
            out=qbf[:, 0:4096].rearrange("p (a c) -> p a c", a=4),
            in_=S3.rearrange("(a p) c -> p a c", p=128)),
            inc="sd", wait=[("te", None)])
        for ci in range(2):
            for si in range(2):
                lnd = ln_sub(ci, si, out_xnh=True)
                ffn_shared_sub(ci, si, lnd)
        moe_phase()
        corrections()

    for dc in range(DC):
        k.E("sy", lambda s, dc=dc: s.dma_start(
            out=y_out[t_(dc, 128), :], in_=xf[:, t_(dc, T)]), inc="yd",
            wait=[("dve", None), ("te", None), ("act", None)])
    ydone = k.c("yd")
    k.E("sy", lambda s: None, wait=[("yd", ydone)])

    sems = {}
    with contextlib.ExitStack() as ses:
        for ch in KB.CHS:
            sems[ch] = ses.enter_context(nc.semaphore(f"s_{ch}"))
        with nc.Block() as block:
            block.tensor(k.runner("te", sems))
            block.scalar(k.runner("act", sems))
            block.vector(k.runner("dve", sems))
            block.gpsimd(k.runner("gp", sems))
            block.sync(k.runner("sy", sems))
        es.close()
    return nc


def _consts_host():
    c = np.zeros((128, 1664), np.float32)
    p = np.arange(128)
    c[:, IDC:IDC + 128][p, p] = 1.0
    c[64 + np.arange(64), LIC + np.arange(64)] = 1.0
    kp = p[:, None]
    key = np.arange(128)[None, :]
    c[:, MAC:MAC + 128] = np.where(kp <= key, -MS, 0.0)
    r = np.arange(896)[None, :] - 384
    c[:, MBW:MBW + 896] = np.where(kp > r, MS, 0.0)
    c[:, ONC:ONC + 128] = 1.0

    for e in range(8):
        c[:, OH8 + e * 8 + e] = 1.0
    c[0, LABC:LABC + 64] = 1.0
    c[32, LABC + 64:LABC + 128] = 1.0
    return c.astype(ml_dtypes.bfloat16)


def _constb2_host():
    c = np.zeros((8, 1024), np.float32)
    for e in range(8):
        c[e, e * 128:(e + 1) * 128] = 1.0
    return c.astype(ml_dtypes.bfloat16)


def stage_inputs(inp):
    n = 8
    x = np.asarray(inp["L0_attn_x"])
    w_attn = np.ascontiguousarray(np.asarray(inp["L0_attn_w_attn"]))
    w_proj = np.ascontiguousarray(np.asarray(inp["L0_attn_w_proj"]))
    s1t = np.ascontiguousarray(np.asarray(inp["L0_ffn_sw1"]).T)
    s2t = np.ascontiguousarray(np.asarray(inp["L0_ffn_sw2"]).T)
    s3t = np.ascontiguousarray(np.asarray(inp["L0_ffn_sw3"]).T)
    e1t = np.ascontiguousarray(np.asarray(inp["L0_ffn_ew1"]).transpose(0, 2, 1))
    e2t = np.ascontiguousarray(np.asarray(inp["L0_ffn_ew2"]).transpose(0, 2, 1))
    e3t = np.ascontiguousarray(np.asarray(inp["L0_ffn_ew3"]).transpose(0, 2, 1))
    gwt = np.zeros((D, 16), np.float32)
    gwt[:, 0] = np.asarray(inp["L0_ffn_sg_w"])[0]
    gwt[:, 1:9] = np.asarray(inp["L0_ffn_eg_w"]).T
    eb3_in = np.ascontiguousarray(np.asarray(inp["L0_ffn_eb3"]))

    smalls = np.zeros((128, 64), np.float32)
    for hc in range(4):
        smalls[:, SB1 + hc] = np.asarray(inp["L0_ffn_sb1"])[t_(hc, 128)]
        smalls[:, SB2 + hc] = np.asarray(inp["L0_ffn_sb2"])[t_(hc, 128)]
        smalls[:, SLG + hc] = np.asarray(inp["L0_ffn_s_ln_g"])[t_(hc, 128)]
        smalls[:, SLB + hc] = np.asarray(inp["L0_ffn_s_ln_b"])[t_(hc, 128)]
        smalls[:, SELB + hc] = np.asarray(inp["L0_ffn_sel_ln_b"])[
            t_(hc, 128)]
        smalls[:, SELG + hc] = np.asarray(inp["L0_ffn_sel_ln_g"])[
            t_(hc, 128)]
    for dc in range(8):
        smalls[:, SB3 + dc] = np.asarray(inp["L0_ffn_sb3"])[t_(dc, 128)]
    smalls[0:64, GAM] = np.asarray(inp["L0_attn_gamma"])

    smalls2 = np.zeros((128, 64), np.float32)
    eb1 = np.asarray(inp["L0_ffn_eb1"])
    eb2 = np.asarray(inp["L0_ffn_eb2"])
    for e in range(8):
        for hc in range(4):
            smalls2[:, e * 4 + hc] = eb1[e][t_(hc, 128)]
            smalls2[:, 32 + e * 4 + hc] = eb2[e][t_(hc, 128)]

    constb = _consts_host()
    constf = np.zeros((128, 168), np.float32)
    constf[:, 160] = EPS
    constf[:, 161] = 1e-12
    constf[:, IOC:IOC + 8] = np.arange(8)[None, :]
    constf[0, GBC] = np.asarray(inp["L0_ffn_sg_b"])[0]
    constf[0, GBC + 1:GBC + 9] = np.asarray(inp["L0_ffn_eg_b"])
    constf[:, EBC:EBC + 8] = np.asarray(inp["L0_ffn_exp_bias"])[None, :]
    constf[0, LQ1:LQ1 + 32] = np.asarray(inp["L0_attn_lambda_q1"])
    constf[0, LK1:LK1 + 32] = np.asarray(inp["L0_attn_lambda_k1"])
    constf[0, LQ2:LQ2 + 32] = np.asarray(inp["L0_attn_lambda_q2"])
    constf[0, LK2:LK2 + 32] = np.asarray(inp["L0_attn_lambda_k2"])
    constb2 = _constb2_host()

    index_b = np.asarray(inp["L0_ffn_index_b"]).astype(np.int64)
    index_t = np.asarray(inp["L0_ffn_index_t"]).astype(np.int64)
    gidxs_percore = []
    for c in range(n):
        gi = np.full((128, 16), T, np.uint32)
        for m in range(M):
            if index_b[m] == c:
                gi[m % 128, m // 128] = index_t[m]
        gidxs_percore.append(gi)
    rids = []
    for c in range(n):
        rev = np.full(T, T, np.int64)
        own = np.nonzero(index_b == c)[0]
        rev[index_t[own]] = own
        ri = np.zeros((128, 16), np.uint32)
        for t in range(T):
            ri[t % 128, t // 128] = rev[t]
        rids.append(ri)

    shared = dict(w_attn=w_attn, w_proj=w_proj, s1t=s1t, s2t=s2t, s3t=s3t,
                  e1t=e1t, e2t=e2t, e3t=e3t, gwt=gwt, eb3_in=eb3_in,
                  smalls=smalls, smalls2=smalls2, constb=constb,
                  constf=constf, constb2=constb2)
    in_maps = []
    for c in range(n):
        m = dict(shared)
        m["x_in"] = np.ascontiguousarray(x[c].T)
        m["ridx"] = rids[c]
        m["gidx"] = gidxs_percore[c]
        in_maps.append(m)
    return in_maps


def kernel(**inp):
    n = 8
    in_maps = stage_inputs(inp)
    nc = build_program()
    res = run_bass_kernel_spmd(nc, in_maps, core_ids=list(range(n)))
    out = np.stack([np.asarray(res.results[c]["y_out"]).T
                    for c in range(n)], axis=0)
    return out


# version 12 (final)
# speedup vs baseline: 1.1461x; 1.0286x over previous
import contextlib
import math

import numpy as np
import ml_dtypes

import concourse.bass as bass
import concourse.mybir as mybir
from concourse.bass import IndirectOffsetOnAxis
from concourse.bass_utils import run_bass_kernel_spmd

F32 = mybir.dt.float32
BF16 = mybir.dt.bfloat16
U32 = mybir.dt.uint32
AF = mybir.ActivationFunctionType
ALU = mybir.AluOpType
AX = mybir.AxisListType

T = 2048
D = 1024
DC = 8
CH = 1024
SUB = 512
HSH = 512
NE = 8
NPAIR = 16
M = 2048
ML = 256
EPS = 1e-5
LAMBDA_INIT = 0.2
MS = 180.0

IDC = 0
LIC = 128
MAC = 256
MBW = 384
ONC = 1280
SELC = 1408
OH8 = 1536
LABC = 1408
IOC = 0
EBC = 8
GBC = 16
LQ1, LK1, LQ2, LK2 = 32, 64, 96, 128
EPSC = 160
SB1, SB2, SB3, SLG, SLB, SELB, GAM, SELG = 0, 4, 8, 16, 20, 24, 28, 32
DMA_STEP = 16


class KB:
    ENGS = ("te", "act", "dve", "gp", "sy")
    CHS = ("te", "act", "dve", "gp", "sy", "gd", "sd", "yd")
    DMA_CHS = ("gd", "sd", "yd")
    SETOF = {AF.Exp: "nlexp", AF.Ln: "nlexp", AF.Sigmoid: "sig",
             AF.Identity: None, AF.Copy: None, AF.Square: None}

    def __init__(self):
        import os as _os
        self.nc = bass.Bass(
            "TRN2", target_bir_lowering=False,
            detect_race_conditions=bool(_os.environ.get("KERN_RACE")))
        self.cnt = {c: 0 for c in self.CHS}
        self.ops = {e: [] for e in self.ENGS}
        self.act_set = None

    def E(self, eng, fn, inc=None, wait=(), drain=True):
        w = tuple((c, (self.cnt[c] if v is None else v)) for c, v in wait
                  if (self.cnt[c] if v is None else v) > 0)
        if inc is not None:
            self.cnt[inc] += DMA_STEP if inc in self.DMA_CHS else 1
        self.ops[eng].append((fn, w, inc, drain))
        return self.cnt[inc] if inc is not None else None

    SETID = {"nlexp": 6, "sig": 2}

    def act(self, fn, func, inc="act", wait=()):
        need = self.SETOF[func]
        if need is not None and need != self.act_set:
            sid = self.SETID[need]

            def load(s, sid=sid):
                return s.add_instruction(mybir.InstLoadActFuncSet(
                    name=s.bass.get_next_instruction_name(),
                    act_func_set_id=sid, ins=[], outs=[]))
            self.E("act", load)
            self.act_set = need
        return self.E("act", fn, inc=inc, wait=wait)

    def c(self, ch):
        return self.cnt[ch]

    def runner(self, name, sems):
        ops = self.ops[name]

        def run(eng):
            waited = {}
            for fn, waits, inc, drain in ops:
                for ch, val in waits:
                    if waited.get(ch, -1) < val:
                        eng.wait_ge(sems[ch], val)
                        waited[ch] = val
                if name == "dve" and drain:
                    eng.drain()
                inst = fn(eng)
                if inc is not None:
                    step = DMA_STEP if inc in self.DMA_CHS else 1
                    inst.then_inc(sems[inc], step)
        return run


def t_(i, n):
    return slice(i * n, (i + 1) * n)


def o_(off, i, n):
    return slice(off + i * n, off + (i + 1) * n)


def build_program():
    k = KB()
    nc = k.nc

    def din(name, shape, dt=F32):
        return nc.dram_tensor(name, shape, dt, kind="ExternalInput").ap()

    import os as _os
    _dbg = bool(_os.environ.get("KERN_DBG"))

    def dint(name, shape, dt=BF16):
        kind = ("ExternalOutput" if (_dbg and name in
                ("XNH", "SHH", "AGO", "E3E", "DBGA", "DBGB", "DBGC", "DBGD", "DBGE")) else "Internal")
        return nc.dram_tensor(name, shape, dt, kind=kind).ap()

    x_in = din("x_in", [D, T])
    w_attn = din("w_attn", [D, 3 * D])
    w_proj = din("w_proj", [D, D])
    s1t = din("s1t", [D, HSH])
    s2t = din("s2t", [D, HSH])
    s3t = din("s3t", [HSH, D])
    e1t = din("e1t", [NE, D, HSH])
    e2t = din("e2t", [NE, D, HSH])
    e3t = din("e3t", [NE, HSH, D])
    gwt = din("gwt", [D, 16])
    eb3_in = din("eb3_in", [NE, D])
    smalls = din("smalls", [128, 64])
    smalls2 = din("smalls2", [128, 64])
    constb = din("constb", [128, 1664], dt=BF16)
    constf = din("constf", [128, 168])
    constb2 = din("constb2", [8, 1024], dt=BF16)
    gidx = din("gidx", [128, 16], U32)
    ridx = din("ridx", [128, 16], U32)
    y_out = nc.dram_tensor("y_out", [D, T], F32, kind="ExternalOutput").ap()

    DBGA = dint("DBGA", [M, D]) if _dbg else None
    DBGD = dint("DBGD", [8, 4096]) if _dbg else None
    DBGE = dint("DBGE", [128, 128], dt=F32) if _dbg else None
    DBGB = dint("DBGB", [128, 8 * M]) if _dbg else None
    DBGC = dint("DBGC", [128, 16 * D]) if _dbg else None
    A_w = dint("A_w", [D, 3 * D])
    A_p = dint("A_p", [D, D])
    S1 = dint("S1", [D, HSH])
    S2 = dint("S2", [D, HSH])
    S3 = dint("S3", [HSH, D])
    E1 = dint("E1", [NE, D, HSH])
    E2 = dint("E2", [NE, D, HSH])
    E3 = dint("E3", [NE, HSH, D])
    GWb = dint("GWb", [D, 16])
    XNH = dint("XNH", [T + 1, D])
    SHH = dint("SHH", [T, D])
    RSI = dint("RSI", [M, D])
    RSO = dint("RSO", [ML, D])
    AGI = dint("AGI", [ML, 1032])
    AGO = dint("AGO", [M + 1, 1032])
    E3E = dint("E3E", [NE, D])

    es = contextlib.ExitStack()

    def sbuf(name, shape, dt=BF16):
        return es.enter_context(nc.sbuf_tensor(name, shape, dt))

    xf = sbuf("xf", [128, DC * T], F32)
    kbf = sbuf("kbf", [128, 8 * T])
    vbf = sbuf("vbf", [128, 16 * D])
    qbf = sbuf("qbf", [128, 8 * CH])
    attn = sbuf("attn", [128, 8 * CH])
    xnb = sbuf("xnb", [128, DC * SUB])
    pbuf = sbuf("pbuf", [128, 2048])
    wsb = sbuf("wsb", [128, 4096])
    tb = sbuf("tb", [128, 1024])
    tb2 = sbuf("tb2", [128, 1032])
    sfr = sbuf("sfr", [128, 1024], F32)
    sfr2 = sbuf("sfr2", [128, 520], F32)
    smal = sbuf("smal", [128, 64], F32)
    smal2 = sbuf("smal2", [128, 64], F32)
    cons = sbuf("cons", [128, 168], F32)
    conb = sbuf("conb", [128, 1664])
    cob2 = sbuf("cob2", [8, 1024])
    gidxs = sbuf("gidxs", [128, 16], U32)
    ridxs = sbuf("ridxs", [128, 16], U32)
    gat = sbuf("gat", [128, 48], F32)
    gat2 = sbuf("gat2", [128, 4], F32)
    idx8 = sbuf("idx8", [128, 8], U32)
    csb = sbuf("csb", [128, 128])
    e3es = sbuf("e3es", [8, 1024])
    svs = sbuf("svs", [8, ML])
    zrow = sbuf("zrow", [128, 1088])
    lamv = sbuf("lamv", [128, 8], F32)
    csc = sbuf("csc", [128, 8], F32)
    gpr = sbuf("gpr", [128, 2], F32)
    grow = sbuf("grow", [1, 64])
    sb3r = sbuf("sb3r", [1, 1024])
    km2 = sbuf("km2", [128, 24], F32)

    psA = es.enter_context(nc.psum_tensor("psA", [128, 3584], F32))
    psT = es.enter_context(nc.psum_tensor("psT", [128, 1024], BF16))

    def bank(i, w=512, p=128):
        return psA[0:p, i * 512: i * 512 + w]

    def ONE():
        return conb[:, ONC:ONC + 1]

    def ONER():
        return conb[0:1, ONC:ONC + 128]

    def IDT():
        return conb[:, IDC:IDC + 128]

    k.E("sy", lambda s: s.dma_start(out=cons[:], in_=constf[:]), inc="sd")
    k.E("sy", lambda s: s.dma_start(out=conb[:], in_=constb[:]), inc="sd")
    k.E("sy", lambda s: s.dma_start(out=smal[:], in_=smalls[:]), inc="sd")
    k.E("sy", lambda s: s.dma_start(out=smal2[:], in_=smalls2[:]), inc="sd")
    k.E("sy", lambda s: s.dma_start(out=gidxs[:], in_=gidx[:]), inc="sd")
    k.E("sy", lambda s: s.dma_start(out=ridxs[:], in_=ridx[:]), inc="sd")
    k.E("sy", lambda s: s.dma_start(out=cob2[:], in_=constb2[:]), inc="sd")
    for dc in range(DC):
        k.E("sy", lambda s, dc=dc: s.dma_start(
            out=xf[:, t_(dc, T)], in_=x_in[t_(dc, 128), :]), inc="sd")
    boot = k.c("sd")
    k.E("dve", lambda v: v.memset(tb2[:], 0.0), inc="dve",
        wait=[("sd", boot)])
    k.E("dve", lambda v: v.memset(tb[:], 0.0), inc="dve")
    k.E("dve", lambda v: v.memset(gpr[:], 0.0), inc="dve")
    k.E("dve", lambda v: v.memset(zrow[:], 0.0), inc="dve")
    z0 = k.c("dve")
    k.E("sy", lambda s: s.dma_start(out=XNH[T:T + 1, :], in_=tb2[0:1, 0:D]),
        inc="sd", wait=[("dve", z0)])
    k.E("sy", lambda s: s.dma_start(out=AGO[M:M + 1, :], in_=tb2[0:1, :]),
        inc="sd", wait=[("dve", z0)])

    k.E("dve", lambda v: v.tensor_tensor(
        out=sfr[0:1, 0:32], in0=cons[0:1, LQ1:LQ1 + 32],
        in1=cons[0:1, LK1:LK1 + 32], op=ALU.mult), inc="dve")
    k.E("dve", lambda v: v.tensor_tensor(
        out=sfr[0:1, 32:64], in0=cons[0:1, LQ2:LQ2 + 32],
        in1=cons[0:1, LK2:LK2 + 32], op=ALU.mult), inc="dve")
    k.E("dve", lambda v: v.reduce_sum(lamv[0:1, 0:1], sfr[0:1, 0:32],
                                      axis=AX.X), inc="dve")
    dv = k.E("dve", lambda v: v.reduce_sum(lamv[0:1, 1:2], sfr[0:1, 32:64],
                                           axis=AX.X), inc="dve")
    a = k.act(lambda s: s.activation(out=lamv[0:1, 2:4], in_=lamv[0:1, 0:2],
                                     func=AF.Exp), AF.Exp, wait=[("dve", dv)])
    k.E("dve", lambda v: v.tensor_tensor(
        out=lamv[0:1, 4:5], in0=lamv[0:1, 2:3], in1=lamv[0:1, 3:4],
        op=ALU.subtract), inc="dve", wait=[("act", a)])
    k.E("dve", lambda v: v.tensor_scalar(
        out=lamv[0:1, 5:6], in0=lamv[0:1, 4:5], scalar1=LAMBDA_INIT,
        scalar2=-1.0, op0=ALU.add, op1=ALU.mult), inc="dve")
    dv = k.E("dve", lambda v: v.tensor_copy(zrow[0:1, 1052:1053], lamv[0:1, 5:6]),
             inc="dve")
    mmn = k.E("te", lambda t: t.matmul(
        psA[:, 3072:3073], ONER(), zrow[0:1, 1052:1053], start=True, stop=True),
        inc="te", wait=[("dve", dv)])
    dv = k.E("dve", lambda v: v.tensor_copy(lamv[:, 6:7], psA[:, 3072:3073]),
             inc="dve", wait=[("te", mmn)])
    k.E("dve", lambda v: v.tensor_copy(csb[0:64, :],
                                       conb[0:64, IDC:IDC + 128]), inc="dve")
    k.E("dve", lambda v: v.tensor_scalar(
        out=csb[64:128, :], in0=conb[64:128, LIC:LIC + 128],
        scalar1=lamv[64:128, 6:7], scalar2=None, op0=ALU.mult), inc="dve")
    k.E("dve", lambda v: v.tensor_scalar(
        out=gpr[0:64, 0:1], in0=smal[0:64, GAM:GAM + 1], scalar1=0.8,
        scalar2=None, op0=ALU.mult), inc="dve")
    dv = k.E("dve", lambda v: v.tensor_copy(tb[:, 0:1], gpr[:, 0:1]),
             inc="dve")
    tr = k.E("te", lambda t: t.transpose(psT[:, 0:128], tb[:, 0:128], IDT()),
             inc="te", wait=[("dve", dv)])
    k.E("dve", lambda v: v.tensor_copy(grow[0:1, 0:64], psT[0:1, 0:64]),
        inc="dve", wait=[("te", tr)])
    for dc in range(DC):
        dv = k.E("dve", lambda v, dc=dc: v.tensor_copy(
            tb[:, 0:1], smal[:, SB3 + dc:SB3 + dc + 1]), inc="dve",
            wait=[("te", None)])
        tr = k.E("te", lambda t: t.transpose(psT[:, 0:128], tb[:, 0:128],
                                             IDT()),
                 inc="te", wait=[("dve", dv)])
        k.E("dve", lambda v, dc=dc: v.tensor_copy(
            sb3r[0:1, t_(dc, 128)], psT[0:1, 0:128]), inc="dve",
            wait=[("te", tr)])
    consts_ready_dve = k.c("dve")

    jobs = []

    def addj(src, dst, rows, cols, gfold=False):
        assert not gfold
        nrb = rows // 128
        g = max(1, 2048 // cols)
        if g > 1:
            for r0 in range(0, nrb, g):
                ga = min(g, nrb - r0)
                jobs.append((
                    src[r0 * 128:(r0 + ga) * 128, :].rearrange(
                        "(a p) c -> p a c", p=128),
                    dst[r0 * 128:(r0 + ga) * 128, :].rearrange(
                        "(a p) c -> p a c", p=128),
                    ga * cols, None))
        else:
            for r in range(nrb):
                for c0 in range(0, cols, 2048):
                    cw = min(2048, cols - c0)
                    jobs.append((src[t_(r, 128), c0:c0 + cw],
                                 dst[t_(r, 128), c0:c0 + cw], cw, None))

    addj(s1t, S1, D, HSH)
    addj(s2t, S2, D, HSH)
    addj(s3t, S3, HSH, D)
    addj(gwt, GWb, D, 16)
    for e in range(NE):
        addj(e1t[e], E1[e], D, HSH)
        addj(e2t[e], E2[e], D, HSH)
    for e in range(NE):
        addj(e3t[e], E3[e], HSH, D)
    addj(w_attn, A_w, D, 3 * D)
    addj(w_proj, A_p, D, D)

    half = 0
    sd_prev = [0, 0]
    pcm = {}
    for j, (src, dst, cw, gf) in enumerate(jobs):
        h = half
        half ^= 1
        buf = qbf[:, 4096 + h * 2048: 4096 + h * 2048 + cw]
        if len(src.shape) == 3:
            buf = buf.rearrange("p (a c) -> p a c", a=src.shape[1])
        g = k.E("gp", lambda gp, b=buf, s=src: gp.dma_start(out=b, in_=s),
                inc="gd", wait=[("gd", sd_prev[h])])
        if gf is not None:
            d = k.E("dve", lambda v, b=buf, hc=gf: v.tensor_scalar(
                out=b, in0=b, scalar1=smal[:, SELG + hc:SELG + hc + 1],
                scalar2=None, op0=ALU.mult), inc="dve", wait=[("gd", g)])
            sd_prev[h] = k.E("gp", lambda gp, b=buf, dd=dst: gp.dma_start(
                out=dd, in_=b), inc="gd", wait=[("dve", d)])
        else:
            sd_prev[h] = k.E("gp", lambda gp, b=buf, dd=dst: gp.dma_start(
                out=dd, in_=b), inc="gd", wait=[("gd", g)])
        pcm[dst.tensor.name] = k.c("gd")
    precast = k.c("gd")

    SCR = {"sd": boot, "gd": 0, "wte": 0}
    SCR["s3"] = k.E("sy", lambda s: s.dma_start(
        out=qbf[:, 0:4096].rearrange("p (a c) -> p a c", a=4),
        in_=S3.rearrange("(a p) c -> p a c", p=128)),
        inc="sd", wait=[("gd", pcm["S3"])])

    def eb3_setup():
      selb_ok = k.E("dve", lambda v: v.tensor_copy(
        zrow[:, 1040:1044], smal[:, SELB:SELB + 4]), inc="dve")
      for e in range(NE):
          gws = []
          for hc in range(4):
              g = k.E("gp", lambda gp, e=e, hc=hc: gp.dma_start(
                  out=attn[:, hc * 1024:(hc + 1) * 1024],
                  in_=e3t[e][t_(hc, 128), :]), inc="gd",
                  wait=[("te", None), ("dve", None)])
              gws.append(g)
          mm = None
          for hc in range(4):
              for dh in range(2):
                  last = (hc == 3 and dh == 1)
                  mm = k.E("te", lambda t, hc=hc, dh=dh: t.matmul(
                      psA[0:1, dh * 512:(dh + 1) * 512],
                      zrow[:, 1040 + hc:1041 + hc],
                      attn[:, hc * 1024 + dh * 512:
                           hc * 1024 + (dh + 1) * 512],
                      start=(hc == 0), stop=(hc == 3)),
                      inc="te" if last else None,
                      wait=[("gd", gws[hc]), ("dve", selb_ok)]
                      + ([("dve", None)] if hc == 0 else []))
          g2 = k.E("sy", lambda s, e=e: s.dma_start(
              out=sfr[0:1, 0:1024], in_=eb3_in[e:e + 1, :]), inc="sd")
          dv = k.E("dve", lambda v: v.tensor_tensor(
              out=tb[0:1, 0:1024], in0=psA[0:1, 0:1024],
              in1=sfr[0:1, 0:1024], op=ALU.add), inc="dve",
              wait=[("te", mm), ("sd", g2)])
          k.E("sy", lambda s, e=e: s.dma_start(
              out=E3E[e:e + 1, :], in_=tb[0:1, 0:1024]), inc="sd",
              wait=[("dve", dv)])
      k.E("sy", lambda s: s.dma_start(out=e3es[:], in_=E3E[:, :]), inc="sd",
          wait=[("sd", None)])
      k.E("sy", lambda s: s.dma_start(
        out=gwsb[:].rearrange("p (a c) -> p a c", a=8),
        in_=GWb.rearrange("(a p) c -> p a c", p=128)),
        inc="sd", wait=[("gd", pcm["GWb"])])
      setup["sd"] = k.c("sd")
      SCR["sd"] = k.c("sd")

    gwsb = sbuf("gwsb", [128, 128])
    setup = {"sd": 0, "done": False}
    gb_ok = k.E("dve", lambda v: v.tensor_copy(
        zrow[0:1, 1024:1040], cons[0:1, GBC:GBC + 16]), inc="dve")

    def ln_sub(ci, si, out_xnh):
        t0 = ci * CH + si * SUB
        mmv = None
        for dc in range(DC):
            d1 = k.E("dve", lambda v, dc=dc: v.tensor_copy(
                tb[:, 0:512], xf[:, dc * T + t0:dc * T + t0 + 512]),
                inc="dve", drain=False,
                wait=[("te", None), ("sd", SCR["sd"]), ("gd", SCR["gd"])])
            d2 = k.E("dve", lambda v: v.tensor_tensor(
                out=tb[:, 512:1024], in0=tb[:, 0:512], in1=tb[:, 0:512],
                op=ALU.mult), inc="dve", drain=False)
            k.E("te", lambda t, dc=dc: t.matmul(
                bank(0, p=1), ONE(), tb[:, 0:512],
                start=(dc == 0), stop=(dc == DC - 1)), inc="te",
                wait=[("dve", d1)])
            mmv = k.E("te", lambda t, dc=dc: t.matmul(
                bank(1, p=1), ONE(), tb[:, 512:1024],
                start=(dc == 0), stop=(dc == DC - 1)), inc="te",
                wait=[("dve", d2)])
        k.E("dve", lambda v: v.tensor_scalar(
            out=sfr[0:1, 0:512], in0=bank(0, p=1), scalar1=1.0 / D,
            scalar2=None, op0=ALU.mult), inc="dve",
            wait=[("te", mmv), ("act", None)])
        k.E("dve", lambda v: v.tensor_scalar(
            out=sfr[0:1, 512:1024], in0=bank(1, p=1), scalar1=1.0 / D,
            scalar2=None, op0=ALU.mult), inc="dve")
        k.E("dve", lambda v: v.tensor_tensor(
            out=sfr2[0:1, 0:512], in0=sfr[0:1, 0:512],
            in1=sfr[0:1, 0:512], op=ALU.mult), inc="dve")
        dv = k.E("dve", lambda v: v.tensor_tensor(
            out=sfr[0:1, 512:1024], in0=sfr[0:1, 512:1024],
            in1=sfr2[0:1, 0:512], op=ALU.subtract), inc="dve")
        k.act(lambda s: s.activation(
            out=sfr2[0:1, 0:512], in_=sfr[0:1, 512:1024], func=AF.Ln,
            bias=cons[0:1, EPSC:EPSC + 1]), AF.Ln, wait=[("dve", dv)])
        a2 = k.act(lambda s: s.activation(
            out=sfr[0:1, 512:1024], in_=sfr2[0:1, 0:512], func=AF.Exp,
            scale=-0.5), AF.Exp)
        k.E("dve", lambda v: v.tensor_tensor(
            out=sfr[0:1, 0:512], in0=sfr[0:1, 0:512],
            in1=sfr[0:1, 512:1024], op=ALU.mult), inc="dve",
            wait=[("act", a2)])
        k.E("dve", lambda v: v.tensor_copy(zrow[0:1, 0:512],
                                           sfr[0:1, 512:1024]), inc="dve",
            wait=[("te", None)])
        dv = k.E("dve", lambda v: v.tensor_copy(zrow[0:1, 512:1024],
                                                sfr[0:1, 0:512]), inc="dve")
        k.E("te", lambda t: t.matmul(
            bank(2), ONER(), zrow[0:1, 0:512], start=True, stop=True),
            inc="te", wait=[("dve", dv)])
        b2 = k.E("te", lambda t: t.matmul(
            bank(3), ONER(), zrow[0:1, 512:1024], start=True, stop=True),
            inc="te")
        dlast = None
        for dc in range(DC):
            k.E("dve", lambda v, dc=dc: v.tensor_tensor(
                out=sfr2[:, 0:512], in0=xf[:, dc * T + t0:dc * T + t0 + 512],
                in1=bank(2), op=ALU.mult), inc="dve", drain=False,
                wait=[("te", b2)])
            dlast = k.E("dve", lambda v, dc=dc: v.tensor_tensor(
                out=xnb[:, t_(dc, SUB)], in0=sfr2[:, 0:512], in1=bank(3),
                op=ALU.subtract), inc="dve", drain=False)
        if out_xnh:
            for q4 in range(4):
                trs = None
                for dc in range(DC):
                    trs = k.E("te", lambda t, dc=dc, q4=q4: t.transpose(
                        psT[:, t_(dc, 128)],
                        xnb[:, dc * SUB + q4 * 128:dc * SUB + (q4 + 1) * 128],
                        IDT()), inc="te" if dc == DC - 1 else None,
                        wait=[("dve", None), ("gd", SCR["gd"])])
                dvt = k.E("dve", lambda v: v.tensor_copy(
                    tb[:, 0:1024], psT[:, :]), inc="dve", drain=False,
                    wait=[("te", trs), ("gd", SCR["gd"])])
                k.E("gp", lambda gp, t0=t0, q4=q4: gp.dma_start(
                    out=XNH[t0 + q4 * 128:t0 + (q4 + 1) * 128, :],
                    in_=tb[:, 0:1024]), inc="gd", wait=[("dve", dvt)])
                SCR["gd"] = k.c("gd")
        return dlast

    def ffn_shared_sub(ci, si, ln_dve):
        t0 = ci * CH + si * SUB
        for hc in range(4):
            g1 = k.E("sy", lambda sy, hc=hc: sy.dma_start(
                out=wsb[:, 0:1024].rearrange("p (a c) -> p a c", a=8),
                in_=S1[:, t_(hc, 128)].rearrange("(a p) c -> p a c", p=128)),
                inc="sd", wait=[("te", SCR["wte"]), ("gd", pcm["S2"])])
            g2 = k.E("sy", lambda sy, hc=hc: sy.dma_start(
                out=wsb[:, 1024:2048].rearrange("p (a c) -> p a c", a=8),
                in_=S2[:, t_(hc, 128)].rearrange("(a p) c -> p a c", p=128)),
                inc="sd")
            mm1 = mm2 = None
            for dc in range(DC):
                mm1 = k.E("te", lambda t, dc=dc: t.matmul(
                    bank(0), wsb[:, t_(dc, 128)], xnb[:, t_(dc, SUB)],
                    start=(dc == 0), stop=(dc == DC - 1)),
                    inc="te" if dc == DC - 1 else None,
                    wait=[("sd", g1), ("dve", ln_dve), ("act", None),
                          ("dve", None)])
                mm2 = k.E("te", lambda t, dc=dc: t.matmul(
                    bank(1), wsb[:, 1024 + dc * 128:1152 + dc * 128],
                    xnb[:, t_(dc, SUB)],
                    start=(dc == 0), stop=(dc == DC - 1)),
                    inc="te" if dc == DC - 1 else None,
                    wait=[("sd", g2)])
            SCR["wte"] = k.c("te")
            a1 = k.act(lambda s, hc=hc: s.activation(
                out=tb[:, 0:512], in_=bank(0), func=AF.Identity,
                bias=smal[:, SB1 + hc:SB1 + hc + 1]), AF.Identity,
                wait=[("te", mm1), ("sd", SCR["sd"]), ("gd", SCR["gd"])])
            a2 = k.act(lambda s, hc=hc: s.activation(
                out=tb2[:, 0:512], in_=bank(0), func=AF.Sigmoid,
                bias=smal[:, SB1 + hc:SB1 + hc + 1]), AF.Sigmoid)
            a3 = k.act(lambda s, hc=hc: s.activation(
                out=tb[:, 512:1024], in_=bank(1), func=AF.Identity,
                bias=smal[:, SB2 + hc:SB2 + hc + 1]), AF.Identity,
                wait=[("te", mm2)])
            d1 = k.E("dve", lambda v: v.tensor_tensor(
                out=tb2[:, 512:1024], in0=tb[:, 0:512], in1=tb2[:, 0:512],
                op=ALU.mult), inc="dve", drain=False, wait=[("act", a3)])
            d2 = k.E("dve", lambda v, hc=hc: v.tensor_tensor(
                out=attn[:, t_(hc, 512)], in0=tb2[:, 512:1024],
                in1=tb[:, 512:1024], op=ALU.mult), inc="dve", drain=False)
            d3 = k.E("dve", lambda v, hc=hc: v.tensor_tensor(
                out=tb[:, 0:512], in0=attn[:, t_(hc, 512)],
                in1=attn[:, t_(hc, 512)], op=ALU.mult), inc="dve",
                drain=False)
            k.E("te", lambda t, hc=hc: t.matmul(
                bank(2, p=1), ONE(), attn[:, t_(hc, 512)],
                start=(hc == 0), stop=(hc == 3)), inc="te",
                wait=[("dve", d2)])
            k.E("te", lambda t, hc=hc: t.matmul(
                bank(3, p=1), ONE(), tb[:, 0:512],
                start=(hc == 0), stop=(hc == 3)), inc="te",
                wait=[("dve", d3)])
        mmstat = k.c("te")
        k.E("dve", lambda v: v.tensor_scalar(
            out=sfr[0:1, 0:512], in0=bank(2, p=1), scalar1=1.0 / HSH,
            scalar2=None, op0=ALU.mult), inc="dve",
            wait=[("te", mmstat), ("act", None)])
        k.E("dve", lambda v: v.tensor_scalar(
            out=sfr[0:1, 512:1024], in0=bank(3, p=1), scalar1=1.0 / HSH,
            scalar2=None, op0=ALU.mult), inc="dve")
        k.E("dve", lambda v: v.tensor_tensor(
            out=sfr2[0:1, 0:512], in0=sfr[0:1, 0:512],
            in1=sfr[0:1, 0:512], op=ALU.mult), inc="dve")
        dv = k.E("dve", lambda v: v.tensor_tensor(
            out=sfr[0:1, 512:1024], in0=sfr[0:1, 512:1024],
            in1=sfr2[0:1, 0:512], op=ALU.subtract), inc="dve")
        k.act(lambda s: s.activation(
            out=sfr2[0:1, 0:512], in_=sfr[0:1, 512:1024], func=AF.Ln,
            bias=cons[0:1, EPSC:EPSC + 1]), AF.Ln, wait=[("dve", dv)])
        a2 = k.act(lambda s: s.activation(
            out=sfr[0:1, 512:1024], in_=sfr2[0:1, 0:512], func=AF.Exp,
            scale=-0.5), AF.Exp)
        k.E("dve", lambda v: v.tensor_tensor(
            out=sfr[0:1, 0:512], in0=sfr[0:1, 0:512],
            in1=sfr[0:1, 512:1024], op=ALU.mult), inc="dve",
            wait=[("act", a2)])
        k.E("dve", lambda v: v.tensor_copy(zrow[0:1, 0:512],
                                           sfr[0:1, 512:1024]), inc="dve",
            wait=[("te", None)])
        dv = k.E("dve", lambda v: v.tensor_copy(zrow[0:1, 512:1024],
                                                sfr[0:1, 0:512]), inc="dve")
        k.E("te", lambda t: t.matmul(
            bank(2), ONER(), zrow[0:1, 0:512], start=True, stop=True),
            inc="te", wait=[("dve", dv)])
        b2 = k.E("te", lambda t: t.matmul(
            bank(3), ONER(), zrow[0:1, 512:1024], start=True, stop=True),
            inc="te")
        dlast = None
        for hc in range(4):
            k.E("dve", lambda v, hc=hc: v.tensor_tensor(
                out=sfr2[:, 0:512], in0=attn[:, t_(hc, 512)],
                in1=bank(2), op=ALU.mult), inc="dve", drain=False,
                wait=[("te", b2)])
            k.E("dve", lambda v: v.tensor_tensor(
                out=sfr2[:, 0:512], in0=sfr2[:, 0:512], in1=bank(3),
                op=ALU.subtract), inc="dve", drain=False)
            dlast = k.E("dve", lambda v, hc=hc: v.tensor_scalar(
                out=attn[:, t_(hc, 512)], in0=sfr2[:, 0:512],
                scalar1=smal[:, SLG + hc:SLG + hc + 1],
                scalar2=smal[:, SLB + hc:SLB + hc + 1],
                op0=ALU.mult, op1=ALU.add), inc="dve", drain=False)
        for tt4 in range(4):
            mm = None
            for hc in range(4):
                for dh in range(2):
                    mm = k.E("te", lambda t, hc=hc, tt4=tt4, dh=dh: t.matmul(
                        psA[:, 2048 + dh * 512:2048 + (dh + 1) * 512],
                        attn[:, o_(hc * 512, tt4, 128)],
                        qbf[:, hc * 1024 + dh * 512:
                            hc * 1024 + (dh + 1) * 512],
                        start=(hc == 0), stop=False),
                        inc=None,
                        wait=[("dve", dlast), ("sd", SCR["s3"]),
                              ("act", None)])
            for dh in range(2):
                mm = k.E("te", lambda t, dh=dh: t.matmul(
                    psA[:, 2048 + dh * 512:2048 + (dh + 1) * 512],
                    ONER(), sb3r[0:1, t_(dh, 512)],
                    start=False, stop=True), inc="te" if dh == 1 else None,
                    wait=[("dve", consts_ready_dve)])
            a1 = k.act(lambda s, tt4=tt4: s.activation(
                out=tb[:, 0:1024], in_=bank(4, w=1024), func=AF.Identity),
                AF.Identity, wait=[("te", mm), ("sd", SCR["sd"])])
            k.E("sy", lambda s, t0=t0, tt4=tt4: s.dma_start(
                out=SHH[t0 + tt4 * 128:t0 + (tt4 + 1) * 128, :],
                in_=tb[:, 0:1024]), inc="sd", wait=[("act", a1)])
            SCR["sd"] = k.c("sd")
        return k.c("te")

    def moe_phase():
        MTL = ML // 128
        gth = []
        for mt in range(16):
            g = k.E("gp", lambda gp, mt=mt: gp.indirect_dma_start(
                out=vbf[:, t_(mt, D)], out_offset=None, in_=XNH[:, :],
                in_offset=IndirectOffsetOnAxis(ap=gidxs[:, mt:mt + 1],
                                               axis=0)),
                inc="gd", wait=[("gd", None), ("te", None), ("dve", None)])
            gth.append(g)
        for mt in range(16):
            k.E("gp", lambda gp, mt=mt: gp.dma_start(
                out=RSI[t_(mt, 128), :], in_=vbf[:, t_(mt, D)]),
                inc="gd", wait=[("gd", gth[mt])])
        rs_w = k.c("gd")
        k.E("gp", lambda gp: gp.collective_compute(
            "ReduceScatter", ALU.add, ins=[RSI[:, :]], outs=[RSO[:, :]],
            replica_groups=[list(range(8))]), inc="gp",
            wait=[("gd", rs_w)])
        rs_done = k.c("gp")
        for mt in range(MTL):
            k.E("gp", lambda gp, mt=mt: gp.dma_start(
                out=vbf[:, t_(mt, D)], in_=RSO[t_(mt, 128), :]),
                inc="gd", wait=[("gp", rs_done)])
        slice_gd = k.c("gd")
        for dc in range(DC):
            trs = None
            for mt in range(MTL):
                trs = k.E("te", lambda t, dc=dc, mt=mt: t.transpose(
                    psT[:, t_(mt, 128)],
                    vbf[:, mt * D + dc * 128: mt * D + (dc + 1) * 128],
                    IDT()), inc="te" if mt == MTL - 1 else None,
                    wait=[("gd", slice_gd), ("dve", None)])
            k.E("dve", lambda v, dc=dc: v.tensor_copy(
                kbf[:, t_(dc * MTL, 128 * MTL)][:, 0:ML] if False else
                kbf[:, dc * ML: dc * ML + ML], psT[:, 0:ML]), inc="dve",
                wait=[("te", trs)])
        xm_ok = k.c("dve")
        for mt in range(MTL):
            mm = None
            for dc in range(DC):
                mm = k.E("te", lambda t, dc=dc, mt=mt: t.matmul(
                    bank(0, w=16), kbf[:, o_(dc * ML, mt, 128)],
                    gwsb[:, t_(dc, 16)], start=(dc == 0), stop=False),
                    inc=None,
                    wait=[("dve", xm_ok), ("sd", setup["sd"]), ("act", None),
                          ("dve", None)])
            mm = k.E("te", lambda t: t.matmul(
                bank(0, w=16), ONER(), zrow[0:1, 1024:1040],
                start=False, stop=True), inc="te", wait=[("dve", gb_ok)])
            a1 = k.act(lambda s: s.activation(
                out=gat[:, 0:16], in_=bank(0, w=16), func=AF.Sigmoid),
                AF.Sigmoid, wait=[("te", mm)])
            k.E("dve", lambda v: v.tensor_tensor(
                out=gat[:, 16:24], in0=gat[:, 1:9],
                in1=cons[:, EBC:EBC + 8], op=ALU.add), inc="dve",
                wait=[("act", a1)])
            k.E("dve", lambda v: v.reduce_max(gat[:, 24:25], gat[:, 16:24],
                                              axis=AX.X), inc="dve")
            k.E("dve", lambda v: v.tensor_scalar(
                out=gat[:, 32:40], in0=cons[:, IOC:IOC + 8], scalar1=0.0,
                scalar2=gat[:, 24:25], op0=ALU.mult, op1=ALU.add),
                inc="dve")
            k.E("dve", lambda v: v.max_index(idx8[:], gat[:, 32:40],
                                             gat[:, 16:24]), inc="dve")
            k.E("dve", lambda v: v.tensor_copy(gat[:, 25:26],
                                               idx8[:, 0:1]), inc="dve")
            k.E("dve", lambda v: v.tensor_scalar(
                out=gat[:, 32:40], in0=cons[:, IOC:IOC + 8],
                scalar1=gat[:, 25:26], scalar2=None, op0=ALU.is_equal),
                inc="dve")
            k.E("dve", lambda v: v.tensor_tensor(
                out=gat[:, 40:48], in0=gat[:, 32:40], in1=gat[:, 1:9],
                op=ALU.mult), inc="dve")
            k.E("dve", lambda v: v.reduce_sum(gat[:, 26:27], gat[:, 40:48],
                                              axis=AX.X), inc="dve")
            dv = k.E("dve", lambda v: v.tensor_tensor(
                out=gat[:, 27:28], in0=gat[:, 0:1], in1=gat[:, 26:27],
                op=ALU.subtract), inc="dve")
            a2 = k.act(lambda s: s.activation(
                out=gat[:, 28:29], in_=gat[:, 27:28], func=AF.Sigmoid),
                AF.Sigmoid, wait=[("dve", dv)])
            k.E("dve", lambda v: v.tensor_scalar(
                out=gat[:, 29:30], in0=gat[:, 28:29], scalar1=-1.0,
                scalar2=1.0, op0=ALU.mult, op1=ALU.add), inc="dve",
                wait=[("act", a2)])
            k.E("dve", lambda v, mt=mt: v.tensor_scalar(
                out=gat2[:, mt:mt + 1], in0=gat[:, 29:30], scalar1=-1.0,
                scalar2=None, op0=ALU.mult), inc="dve")
            dv = k.E("dve", lambda v: v.tensor_scalar(
                out=gat[:, 40:48], in0=gat[:, 32:40],
                scalar1=gat[:, 29:30], scalar2=None, op0=ALU.mult),
                inc="dve")
            dv = k.E("dve", lambda v: v.tensor_copy(tb[:, 0:8],
                                                    gat[:, 40:48]),
                     inc="dve",
                     wait=[("sd", SCR["sd"]), ("gd", SCR["gd"])])
            tr = k.E("te", lambda t: t.transpose(
                psT[:, 0:128], tb[:, 0:128], IDT()), inc="te",
                wait=[("dve", dv)])
            k.E("dve", lambda v, mt=mt: v.tensor_copy(
                svs[:, t_(mt, 128)], psT[0:8, 0:128]), inc="dve",
                wait=[("te", tr)])
        gating_done = k.c("dve")
        m0 = 0
        ZS = ML
        stat_mm = None
        for e in range(NE):
            for hc in range(4):
                hh = (hc % 2) * 2048
                g1 = k.E("sy", lambda sy, e=e, hc=hc, hh=hh: sy.dma_start(
                    out=wsb[:, hh:hh + 1024].rearrange(
                        "p (a c) -> p a c", a=8),
                    in_=E1[e][:, t_(hc, 128)].rearrange(
                        "(a p) c -> p a c", p=128)),
                    inc="sd", wait=[("te", SCR["wte"]), ("gd", pcm["E2"])])
                g2 = k.E("sy", lambda sy, e=e, hc=hc, hh=hh: sy.dma_start(
                    out=wsb[:, hh + 1024:hh + 2048].rearrange(
                        "p (a c) -> p a c", a=8),
                    in_=E2[e][:, t_(hc, 128)].rearrange(
                        "(a p) c -> p a c", p=128)),
                    inc="sd")
                mm1 = mm2 = None
                for dc in range(DC):
                    mm1 = k.E("te", lambda t, dc=dc, hh=hh: t.matmul(
                        bank(0, w=ML), wsb[:, o_(hh, dc, 128)],
                        kbf[:, dc * ML:dc * ML + ML],
                        start=(dc == 0), stop=(dc == DC - 1)),
                        inc="te" if dc == DC - 1 else None,
                        wait=[("sd", g1), ("act", None), ("dve", None)])
                    mm2 = k.E("te", lambda t, dc=dc, hh=hh: t.matmul(
                        bank(1, w=ML), wsb[:, o_(hh + 1024, dc, 128)],
                        kbf[:, dc * ML:dc * ML + ML],
                        start=(dc == 0), stop=(dc == DC - 1)),
                        inc="te" if dc == DC - 1 else None,
                        wait=[("sd", g2)])
                SCR["wte"] = k.c("te")
                a1 = k.act(lambda s, e=e, hc=hc: s.activation(
                    out=tb[:, 0:ML], in_=bank(0, w=ML), func=AF.Identity,
                    bias=smal2[:, e * 4 + hc:e * 4 + hc + 1]),
                    AF.Identity,
                    wait=[("te", mm1), ("sd", SCR["sd"]),
                          ("gd", SCR["gd"])])
                a2 = k.act(lambda s, e=e, hc=hc: s.activation(
                    out=tb2[:, 0:ML], in_=bank(0, w=ML), func=AF.Sigmoid,
                    bias=smal2[:, e * 4 + hc:e * 4 + hc + 1]),
                    AF.Sigmoid)
                a3 = k.act(lambda s, e=e, hc=hc: s.activation(
                    out=tb[:, 512:512 + ML], in_=bank(1, w=ML),
                    func=AF.Identity,
                    bias=smal2[:, 32 + e * 4 + hc:33 + e * 4 + hc]),
                    AF.Identity, wait=[("te", mm2)])
                zoff = (e * 4 + hc) * ZS
                k.E("dve", lambda v: v.tensor_tensor(
                    out=tb2[:, 512:512 + ML], in0=tb[:, 0:ML],
                    in1=tb2[:, 0:ML], op=ALU.mult), inc="dve",
                    wait=[("act", a3)])
                d2 = k.E("dve", lambda v, zoff=zoff: v.tensor_tensor(
                    out=vbf[:, zoff:zoff + ML], in0=tb2[:, 512:512 + ML],
                    in1=tb[:, 512:512 + ML], op=ALU.mult), inc="dve",
                    wait=[("te", None)])
                d3 = k.E("dve", lambda v, zoff=zoff: v.tensor_tensor(
                    out=tb[:, 0:ML], in0=vbf[:, zoff:zoff + ML],
                    in1=vbf[:, zoff:zoff + ML], op=ALU.mult), inc="dve")
                first = (e == 0 and hc == 0)
                last = (e == NE - 1 and hc == 3)
                k.E("te", lambda t, e=e, zoff=zoff, first=first,
                    last=last: t.matmul(
                        bank(2, w=ML, p=8),
                        conb[:, OH8 + e * 8:OH8 + e * 8 + 8],
                        vbf[:, zoff:zoff + ML],
                        start=first, stop=last), inc="te",
                    wait=[("dve", d2), ("act", None)])
                stat_mm = k.E("te", lambda t, e=e, first=first,
                              last=last: t.matmul(
                        bank(3, w=ML, p=8),
                        conb[:, OH8 + e * 8:OH8 + e * 8 + 8],
                        tb[:, 0:ML], start=first, stop=last), inc="te",
                    wait=[("dve", d3), ("act", None)])
        k.E("dve", lambda v: v.tensor_scalar(
            out=sfr[0:8, 0:ML], in0=bank(2, w=ML, p=8), scalar1=1.0 / HSH,
            scalar2=None, op0=ALU.mult), inc="dve",
            wait=[("te", stat_mm), ("act", None)])
        k.E("dve", lambda v: v.tensor_scalar(
            out=sfr[0:8, 512:512 + ML], in0=bank(3, w=ML, p=8),
            scalar1=1.0 / HSH, scalar2=None, op0=ALU.mult), inc="dve")
        k.E("dve", lambda v: v.tensor_tensor(
            out=sfr2[0:8, 0:ML], in0=sfr[0:8, 0:ML],
            in1=sfr[0:8, 0:ML], op=ALU.mult), inc="dve")
        dv = k.E("dve", lambda v: v.tensor_tensor(
            out=sfr[0:8, 512:512 + ML], in0=sfr[0:8, 512:512 + ML],
            in1=sfr2[0:8, 0:ML], op=ALU.subtract), inc="dve")
        k.act(lambda s: s.activation(
            out=sfr2[0:8, 0:ML], in_=sfr[0:8, 512:512 + ML],
            func=AF.Ln, bias=cons[0:8, EPSC:EPSC + 1]), AF.Ln,
            wait=[("dve", dv)])
        a2 = k.act(lambda s: s.activation(
            out=sfr[0:8, 512:512 + ML], in_=sfr2[0:8, 0:ML],
            func=AF.Exp, scale=-0.5), AF.Exp)
        k.E("dve", lambda v: v.tensor_tensor(
            out=sfr2[0:8, 0:ML], in0=sfr[0:8, 512:512 + ML],
            in1=svs[:, 0:ML], op=ALU.mult), inc="dve",
            wait=[("act", a2), ("dve", gating_done)])
        k.E("dve", lambda v: v.tensor_copy(zrow[0:8, 0:ML],
                                           sfr2[0:8, 0:ML]),
            inc="dve", wait=[("te", None)])
        k.E("dve", lambda v: v.tensor_tensor(
            out=sfr[0:8, 0:ML], in0=sfr[0:8, 0:ML],
            in1=sfr2[0:8, 0:ML], op=ALU.mult), inc="dve")
        dvr = k.E("dve", lambda v: v.tensor_copy(zrow[0:8, 512:512 + ML],
                                                 sfr[0:8, 0:ML]),
                  inc="dve")
        zn_done = None
        for e in range(NE):
            b1 = k.E("te", lambda t, e=e: t.matmul(
                bank(4, w=ML), cob2[:, t_(e, 128)], zrow[0:8, 0:ML],
                start=True, stop=True), inc="te",
                wait=[("dve", None), ("act", None)])
            b2 = k.E("te", lambda t, e=e: t.matmul(
                bank(5, w=ML), cob2[:, t_(e, 128)], zrow[0:8, 512:512 + ML],
                start=True, stop=True), inc="te")
            for hc in range(4):
                zoff = (e * 4 + hc) * ZS
                k.E("dve", lambda v, zoff=zoff: v.tensor_tensor(
                    out=sfr2[:, 0:ML], in0=vbf[:, zoff:zoff + ML],
                    in1=bank(4, w=ML), op=ALU.mult), inc="dve",
                    wait=[("te", b2)])
                k.E("dve", lambda v, zoff=zoff: v.tensor_tensor(
                    out=sfr2[:, 0:ML], in0=sfr2[:, 0:ML],
                    in1=bank(5, w=ML), op=ALU.subtract), inc="dve")
                zn_done = k.E("dve", lambda v, zoff=zoff, hc=hc:
                              v.tensor_scalar(
                    out=vbf[:, zoff:zoff + ML], in0=sfr2[:, 0:ML],
                    scalar1=smal[:, SELG + hc:SELG + hc + 1],
                    scalar2=None, op0=ALU.mult), inc="dve")
        for dh in range(2):
            evac_prev = k.c("act")
            mm_by_mt = [None] * (ML // 128)
            for e in range(NE):
                eh = (e % 2) * 2048
                g = k.E("sy", lambda sy, e=e, dh=dh, eh=eh: sy.dma_start(
                    out=wsb[:, eh:eh + 2048].rearrange(
                        "p (a c) -> p a c", a=4),
                    in_=E3[e][:, t_(dh, 512)].rearrange(
                        "(a p) c -> p a c", p=128)),
                    inc="sd", wait=[("te", SCR["wte"]), ("gd", pcm["E3"])])
                for mt4 in range(ML // 128):
                    for hc in range(4):
                        first = (e == 0 and hc == 0)
                        mm_by_mt[mt4] = k.E(
                            "te", lambda t, e=e, mt4=mt4, hc=hc,
                            first=first, eh=eh: t.matmul(
                                bank(mt4),
                                vbf[:, o_((e * 4 + hc) * ZS, mt4, 128)],
                                wsb[:, o_(eh, hc, 512)],
                                start=first, stop=False),
                            inc="te" if (mt4 == ML // 128 - 1 and hc == 3)
                            else None,
                            wait=([("sd", g), ("dve", zn_done),
                                   ("act", evac_prev)] if first or hc == 0
                                  else [("sd", g)]))
                SCR["wte"] = k.c("te")
            for mt4 in range(ML // 128):
                mm_by_mt[mt4] = k.E("te", lambda t, mt4=mt4, dh=dh:
                                    t.matmul(
                        bank(mt4), svs[:, t_(mt4, 128)],
                        e3es[:, t_(dh, 512)], start=False, stop=True),
                    inc="te", wait=[("sd", setup["sd"])])
            for mt4 in range(ML // 128):
                a1 = k.act(lambda s, mt4=mt4: s.activation(
                    out=tb[:, 0:512], in_=bank(mt4), func=AF.Identity),
                    AF.Identity,
                    wait=[("te", mm_by_mt[mt4]), ("sd", SCR["sd"])])
                k.E("sy", lambda s, mt4=mt4, dh=dh: s.dma_start(
                    out=AGI[t_(mt4, 128), t_(dh, 512)],
                    in_=tb[:, 0:512]), inc="sd",
                    wait=[("act", a1), ("gd", None)])
                SCR["sd"] = k.c("sd")
        for mt4 in range(ML // 128):
            dv = k.E("dve", lambda v, mt4=mt4: v.tensor_copy(
                tb2[:, 0:1], gat2[:, mt4:mt4 + 1]), inc="dve",
                wait=[("sd", SCR["sd"])])
            dv2 = k.E("dve", lambda v: v.memset(tb2[:, 1:8], 0.0),
                      inc="dve")
            k.E("sy", lambda s, mt4=mt4: s.dma_start(
                out=AGI[t_(mt4, 128), 1024:1032],
                in_=tb2[:, 0:8]), inc="sd", wait=[("dve", dv2)])
            SCR["sd"] = k.c("sd")
        ag_w = k.c("sd")
        k.E("gp", lambda gp: gp.collective_compute(
            "AllGather", ALU.bypass, ins=[AGI[:, :]],
            outs=[AGO[0:M, :]], replica_groups=[list(range(8))]),
            inc="gp", wait=[("sd", ag_w), ("gd", None)])

    def corrections():
        ag_done = k.c("gp")
        for tt in range(16):
            g1 = k.E("sy", lambda s, tt=tt: s.dma_start(
                out=tb[:, 0:1024], in_=SHH[t_(tt, 128), :]), inc="sd",
                wait=[("dve", None), ("act", None)])
            g2 = k.E("gp", lambda gp, tt=tt: gp.indirect_dma_start(
                out=tb2[:], out_offset=None, in_=AGO[:, :],
                in_offset=IndirectOffsetOnAxis(ap=ridxs[:, tt:tt + 1],
                                               axis=0)),
                inc="gd", wait=[("gp", ag_done), ("dve", None)])
            dv = k.E("dve", lambda v: v.tensor_scalar(
                out=sfr2[:, 0:1], in0=tb2[:, 1024:1025], scalar1=1.0,
                scalar2=None, op0=ALU.add), inc="dve",
                wait=[("gd", g2), ("sd", g1)])
            dv = k.E("dve", lambda v: v.scalar_tensor_tensor(
                out=tb2[:, 0:1024], in0=tb[:, 0:1024],
                scalar=sfr2[:, 0:1], in1=tb2[:, 0:1024],
                op0=ALU.mult, op1=ALU.add), inc="dve")
            trs = None
            for dc in range(DC):
                trs = k.E("te", lambda t, dc=dc: t.transpose(
                    psT[:, t_(dc, 128)], tb2[:, t_(dc, 128)], IDT()),
                    inc="te" if dc == DC - 1 else None,
                    wait=[("dve", dv)])
            xfv = xf[:, :].rearrange("p (dc t) -> p dc t", dc=DC)
            psv = psT[:, :].rearrange("p (dc t) -> p dc t", dc=DC)
            k.E("dve", lambda v, tt=tt, xfv=xfv, psv=psv:
                v.scalar_tensor_tensor(
                    out=xfv[:, :, t_(tt, 128)], in0=psv[:, :, :],
                    scalar=0.5, in1=xfv[:, :, t_(tt, 128)],
                    op0=ALU.mult, op1=ALU.add), inc="dve", drain=False,
                wait=[("te", trs)])

    def qkv_sub(ci, si, ln_dve):
        t0 = ci * CH + si * SUB
        for fc in range(16):
            h = fc % 2
            g = k.E("sy", lambda sy, fc=fc, h=h: sy.dma_start(
                out=wsb[:, h * 2048:h * 2048 + 1024].rearrange(
                    "p (a c) -> p a c", a=8),
                in_=A_w[:, t_(fc, 128)].rearrange("(a p) c -> p a c",
                                                  p=128)),
                inc="sd", wait=[("te", SCR["wte"]), ("gd", pcm["A_w"])])
            mm = None
            for dc in range(DC):
                mm = k.E("te", lambda t, dc=dc, fc=fc, h=h: t.matmul(
                    bank(h), wsb[:, o_(h * 2048, dc, 128)],
                    xnb[:, t_(dc, SUB)],
                    start=(dc == 0), stop=(dc == DC - 1)),
                    inc="te" if dc == DC - 1 else None,
                    wait=[("sd", g), ("dve", ln_dve), ("dve", None)])
            SCR["wte"] = k.c("te")
            if fc < 8:
                dst = qbf[:, fc * CH + si * SUB: fc * CH + si * SUB + 512]
            else:
                c0 = (fc - 8) * T + t0
                dst = kbf[:, c0:c0 + 512]
            dv = k.E("dve", lambda v, dst=dst, h=h: v.tensor_copy(
                dst, bank(h)), inc="dve", drain=False,
                wait=[("te", mm)] + ([("gd", precast)] if fc == 0 else []))
            dv2 = k.E("dve", lambda v, dst=dst: v.tensor_tensor(
                out=tb[:, 0:512], in0=dst, in1=dst, op=ALU.mult),
                inc="dve", drain=False)
            mm2 = k.E("te", lambda t: t.matmul(
                bank(2, p=1), ONE(), tb[:, 0:512], start=True, stop=True),
                inc="te", wait=[("dve", dv2)])
            k.E("dve", lambda v, fc=fc, si=si: v.reduce_max(
                km2[0:1, 16 + (fc % 2):17 + (fc % 2)], bank(2, w=512, p=1),
                axis=AX.X), inc="dve", wait=[("te", mm2)])
            slot = fc
            fresh = (si == 0) if fc < 8 else (ci == 0 and si == 0)
            if fresh:
                k.E("dve", lambda v, slot=slot, fc=fc: v.tensor_copy(
                    km2[0:1, slot:slot + 1],
                    km2[0:1, 16 + (fc % 2):17 + (fc % 2)]), inc="dve")
            else:
                k.E("dve", lambda v, slot=slot, fc=fc: v.tensor_tensor(
                    out=km2[0:1, slot:slot + 1],
                    in0=km2[0:1, slot:slot + 1],
                    in1=km2[0:1, 16 + (fc % 2):17 + (fc % 2)],
                    op=ALU.max), inc="dve")
        for dh in range(2):
            g1 = k.E("sy", lambda sy, dh=dh: sy.dma_start(
                out=wsb[:, 0:2048].rearrange("p (a c) -> p a c", a=4),
                in_=A_w[0:512, o_(2048, dh, 512)].rearrange(
                    "(a p) c -> p a c", p=128)),
                inc="sd", wait=[("te", SCR["wte"]), ("gd", pcm["A_w"])])
            g2 = k.E("sy", lambda sy, dh=dh: sy.dma_start(
                out=wsb[:, 2048:4096].rearrange("p (a c) -> p a c", a=4),
                in_=A_w[512:1024, o_(2048, dh, 512)].rearrange(
                    "(a p) c -> p a c", p=128)),
                inc="sd")
            for tt4 in range(4):
                b = 3 + (tt4 % 2)
                mm = None
                for dc in range(DC):
                    mm = k.E("te", lambda t, dc=dc, tt4=tt4, b=b: t.matmul(
                        bank(b), xnb[:, o_(dc * SUB, tt4, 128)],
                        wsb[:, t_(dc, 512)],
                        start=(dc == 0), stop=(dc == DC - 1)),
                        inc="te" if dc == DC - 1 else None,
                        wait=[("sd", g1), ("sd", g2), ("dve", None)])
                SCR["wte"] = k.c("te")
                tt = t0 // 128 + tt4
                k.E("dve", lambda v, tt=tt, dh=dh, b=b: v.tensor_copy(
                    vbf[:, o_(tt * D, dh, 512)], bank(b)), inc="dve",
                    drain=False, wait=[("te", mm)])
        return k.c("dve")

    def stabilizer(ci):
        dv = k.E("dve", lambda v: v.tensor_tensor(
            out=km2[0:1, 16:24], in0=km2[0:1, 0:8], in1=km2[0:1, 8:16],
            op=ALU.mult), inc="dve", wait=[("te", None)])
        k.act(lambda s: s.activation(
            out=km2[0:1, 16:24], in_=km2[0:1, 16:24], func=AF.Ln,
            bias=cons[0:1, EPSC + 1:EPSC + 2]), AF.Ln, wait=[("dve", dv)])
        a = k.act(lambda s: s.activation(
            out=km2[0:1, 16:24], in_=km2[0:1, 16:24], func=AF.Exp,
            scale=0.5), AF.Exp)
        dv = k.E("dve", lambda v: v.tensor_scalar(
            out=km2[0:1, 16:24], in0=km2[0:1, 16:24], scalar1=-1.0 / 32.0,
            scalar2=None, op0=ALU.mult), inc="dve", wait=[("act", a)])
        dv = k.E("dve", lambda v: v.tensor_copy(zrow[0:1, 1044:1052],
                                                km2[0:1, 16:24]), inc="dve",
                 wait=[("te", None)])
        mm = k.E("te", lambda t: t.matmul(
            bank(2, w=8), ONER(), zrow[0:1, 1044:1052], start=True, stop=True),
            inc="te", wait=[("dve", dv)])
        k.E("dve", lambda v: v.tensor_copy(csc[:, 0:8], bank(2, w=8, p=128)),
            inc="dve", wait=[("te", mm), ("act", None)])
        return k.c("dve")

    def attention_chunk(ci, csc_dve):
        qc0 = ci * CH
        for pr in range(NPAIR):
            fc = pr // 2
            po0 = 64 * (pr % 2)
            grp = fc
            for qs in range(2):
                qoff = qc0 + qs * SUB
                qcol = qs * SUB
                kbmax = (qoff + SUB) // 128
                act_of = {}
                te_av = {}

                def emit_scores(kb):
                    par = kb % 2
                    sb0 = par * 1024
                    diag = (kb >= kbmax - 4)
                    for hi in range(2):
                        po = po0 + 32 * hi
                        sl = psA[:, sb0 + hi * 512: sb0 + (hi + 1) * 512]
                        w = [("act", act_of.get(kb - 2, 0)), ("dve", None)]
                        if kb == 0 and hi == 0:
                            w.append(("dve", csc_dve))
                        is_last = (hi == 1 and not diag)
                        k.E("te", lambda t, fc=fc, po=po, kb=kb,
                            qoff=qoff, sl=sl, diag=diag:
                            t.matmul(
                                sl,
                                kbf[po:po + 32,
                                    fc * T + kb * 128:
                                    fc * T + (kb + 1) * 128],
                                qbf[po:po + 32,
                                    fc * CH + qoff - qc0:
                                    fc * CH + qoff - qc0 + 512],
                                start=True, stop=not diag,
                                tile_position=(po, 0)),
                            inc="te" if is_last else None, wait=w)
                        if diag:
                            qw = kb * 128 - qoff
                            k.E("te", lambda t, sl=sl, qw=qw:
                                t.matmul(
                                    sl,
                                    conb[:, MAC:MAC + 128],
                                    conb[:, MBW + 384 - qw:
                                         MBW + 384 - qw + 512],
                                    start=False, stop=True,
                                    tile_position=(0, 0)),
                                inc="te" if hi == 1 else None)
                    return k.c("te")

                def emit_exp(kb, sc_done):
                    sb0 = (kb % 2) * 1024
                    a = k.act(lambda s, grp=grp, sb0=sb0: s.activation(
                        out=pbuf[:, sb0:sb0 + 1024],
                        in_=psA[:, sb0:sb0 + 1024],
                        func=AF.Exp, scale=1.0 / 32.0,
                        bias=csc[:, grp:grp + 1]), AF.Exp,
                        wait=[("te", sc_done),
                              ("te", te_av.get(kb - 2, 0))])
                    act_of[kb] = a

                def emit_av(kb):
                    sb0 = (kb % 2) * 1024
                    a = act_of[kb]
                    first = (kb == 0)
                    lastk = (kb == kbmax - 1)
                    k.E("te", lambda t, kb=kb, pr=pr, sb0=sb0,
                        first=first, lastk=lastk: t.matmul(
                            bank(4, p=64), vbf[:, kb * D + 64 * pr:
                                               kb * D + 64 * pr + 64],
                            pbuf[:, sb0:sb0 + 512],
                            start=first, stop=lastk,
                            tile_position=(0, 0)),
                        inc=None,
                        wait=([("act", a), ("dve", None)] if first
                              else [("act", a)]))
                    k.E("te", lambda t, kb=kb, pr=pr, sb0=sb0,
                        first=first, lastk=lastk: t.matmul(
                            bank(4)[64:128, :],
                            vbf[:, kb * D + 64 * pr:
                                 kb * D + 64 * pr + 64],
                            pbuf[:, sb0 + 512:sb0 + 1024],
                            start=first, stop=lastk,
                            tile_position=(0, 64),
                            skip_group_check=True),
                        inc=None)
                    k.E("te", lambda t, sb0=sb0, first=first,
                        lastk=lastk: t.matmul(
                        bank(5, p=1), ONE(), pbuf[:, sb0:sb0 + 512],
                        start=first, stop=lastk,
                        tile_position=(0, 0)), inc=None,
                        wait=([("dve", None)] if first else []))
                    te_av[kb] = k.E("te", lambda t, sb0=sb0, first=first,
                        lastk=lastk: t.matmul(
                        psA[32:33, 2560:3072], ONE(),
                        pbuf[:, sb0 + 512:sb0 + 1024],
                        start=first, stop=lastk,
                        tile_position=(0, 32),
                        skip_group_check=True), inc="te")

                for kb in range(kbmax + 1):
                    if kb < kbmax:
                        sc = emit_scores(kb)
                        emit_exp(kb, sc)
                    if kb >= 1:
                        emit_av(kb - 1)
                av_done = k.c("te")
                k.act(lambda s: s.activation(
                    out=sfr[0:1, 0:512], in_=psA[0:1, 2560:3072],
                    func=AF.Ln), AF.Ln,
                    wait=[("te", av_done), ("dve", None)])
                k.act(lambda s: s.activation(
                    out=zrow[0:1, 0:512], in_=sfr[0:1, 0:512],
                    func=AF.Exp, scale=-1.0), AF.Exp)
                k.act(lambda s: s.activation(
                    out=sfr[32:33, 0:512], in_=psA[32:33, 2560:3072],
                    func=AF.Ln), AF.Ln)
                az = k.act(lambda s: s.activation(
                    out=zrow[32:33, 0:512], in_=sfr[32:33, 0:512],
                    func=AF.Exp, scale=-1.0), AF.Exp)
                mm = k.E("te", lambda t: t.matmul(
                    bank(6), conb[0:64, LABC:LABC + 128], zrow[0:64, 0:512],
                    start=True, stop=True, tile_position=(0, 0)),
                    inc="te", wait=[("act", az), ("dve", None)])
                dab = k.E("dve", lambda v: v.tensor_copy(tb[:, 0:512],
                                                         bank(6)),
                          inc="dve", drain=False, wait=[("te", mm)])
                dt = k.E("dve", lambda v: v.tensor_tensor(
                    out=tb2[:, 0:512], in0=bank(4), in1=tb[:, 0:512],
                    op=ALU.mult), inc="dve", drain=False)
                mm = k.E("te", lambda t: t.matmul(
                    bank(6), csb[:, :], tb2[:, 0:512], start=True,
                    stop=True), inc="te", wait=[("dve", dt)])
                a1 = k.act(lambda s: s.activation(
                    out=tb[0:64, 0:512], in_=bank(6, p=64),
                    func=AF.Identity), AF.Identity, wait=[("te", mm)])
                dd = k.E("dve", lambda v: v.tensor_tensor(
                    out=tb[0:64, 512:1024], in0=tb[0:64, 0:512],
                    in1=tb[0:64, 0:512], op=ALU.mult), inc="dve",
                    drain=False, wait=[("act", a1)])
                mm = k.E("te", lambda t: t.matmul(
                    psA[0:1, 2560:3072], conb[0:64, ONC:ONC + 1],
                    tb[0:64, 512:1024], start=True, stop=True), inc="te",
                    wait=[("dve", dd)])
                k.act(lambda s: s.activation(
                    out=sfr2[0:1, 0:512], in_=psA[0:1, 2560:3072],
                    func=AF.Ln, scale=1.0 / 64.0,
                    bias=cons[0:1, EPSC:EPSC + 1]), AF.Ln,
                    wait=[("te", mm)])
                ag = k.act(lambda s: s.activation(
                    out=sfr[0:1, 512:1024], in_=sfr2[0:1, 0:512],
                    func=AF.Exp, scale=-0.5), AF.Exp)
                dg = k.E("dve", lambda v: v.tensor_copy(
                    zrow[0:1, 512:1024], sfr[0:1, 512:1024]), inc="dve",
                    wait=[("act", ag)])
                mm = k.E("te", lambda t: t.matmul(
                    bank(6, p=64), grow[0:1, :], zrow[0:1, 512:1024],
                    start=True, stop=True), inc="te", wait=[("dve", dg)])
                k.E("dve", lambda v, pr=pr, qcol=qcol: v.tensor_tensor(
                    out=attn[64 * (pr % 2):64 * (pr % 2) + 64,
                             (pr // 2) * CH + qcol:
                             (pr // 2) * CH + qcol + 512],
                    in0=tb[0:64, 0:512], in1=bank(6, p=64), op=ALU.mult),
                    inc="dve", drain=False, wait=[("te", mm)])
        return k.c("dve")

    def proj_chunk(ci, attn_dve):
        for si in range(2):
            for dc in range(DC):
                h = dc % 2
                g = k.E("sy", lambda sy, dc=dc, h=h: sy.dma_start(
                    out=wsb[:, h * 2048:h * 2048 + 1024].rearrange(
                        "p (a c) -> p a c", a=8),
                    in_=A_p[:, t_(dc, 128)].rearrange(
                        "(a p) c -> p a c", p=128)),
                    inc="sd", wait=[("te", SCR["wte"]), ("gd", pcm["A_p"])])
                mm = None
                for ac in range(8):
                    mm = k.E("te", lambda t, ac=ac, si=si, h=h: t.matmul(
                        bank(h), wsb[:, o_(h * 2048, ac, 128)],
                        attn[:, o_(ac * CH, si, 512)],
                        start=(ac == 0), stop=(ac == 7)),
                        inc="te" if ac == 7 else None,
                        wait=[("sd", g), ("dve", attn_dve), ("dve", None)])
                SCR["wte"] = k.c("te")
                k.E("dve", lambda v, dc=dc, ci=ci, si=si, h=h:
                    v.scalar_tensor_tensor(
                        out=xf[:, dc * T + ci * CH + si * 512:
                               dc * T + ci * CH + (si + 1) * 512],
                        in0=bank(h), scalar=1.0,
                        in1=xf[:, dc * T + ci * CH + si * 512:
                               dc * T + ci * CH + (si + 1) * 512],
                        op0=ALU.mult, op1=ALU.add), inc="dve", drain=False,
                    wait=[("te", mm)])
        return k.c("dve")

    import os
    nphase = int(os.environ.get("KERN_PHASES", "3"))
    for ci in range(2):
        for si in range(2):
            lnd = ln_sub(ci, si, out_xnh=True)
            ffn_shared_sub(ci, si, lnd)
    if not setup["done"]:
        eb3_setup()
        setup["done"] = True
    moe_phase()
    corrections()

    if nphase >= 2:
        for ci in range(2):
            for si in range(2):
                lnd = ln_sub(ci, si, out_xnh=False)
                qkv_sub(ci, si, lnd)
            cdve = stabilizer(ci)
            adve = attention_chunk(ci, cdve)
            proj_chunk(ci, adve)

    if nphase >= 3:
        SCR["s3"] = k.E("sy", lambda s: s.dma_start(
            out=qbf[:, 0:4096].rearrange("p (a c) -> p a c", a=4),
            in_=S3.rearrange("(a p) c -> p a c", p=128)),
            inc="sd", wait=[("te", None)])
        for ci in range(2):
            for si in range(2):
                lnd = ln_sub(ci, si, out_xnh=True)
                ffn_shared_sub(ci, si, lnd)
        moe_phase()
        corrections()

    for dc in range(DC):
        k.E("sy", lambda s, dc=dc: s.dma_start(
            out=y_out[t_(dc, 128), :], in_=xf[:, t_(dc, T)]), inc="yd",
            wait=[("dve", None), ("te", None), ("act", None)])
    ydone = k.c("yd")
    k.E("sy", lambda s: None, wait=[("yd", ydone)])

    sems = {}
    with contextlib.ExitStack() as ses:
        for ch in KB.CHS:
            sems[ch] = ses.enter_context(nc.semaphore(f"s_{ch}"))
        with nc.Block() as block:
            block.tensor(k.runner("te", sems))
            block.scalar(k.runner("act", sems))
            block.vector(k.runner("dve", sems))
            block.gpsimd(k.runner("gp", sems))
            block.sync(k.runner("sy", sems))
        es.close()
    return nc


def _consts_host():
    c = np.zeros((128, 1664), np.float32)
    p = np.arange(128)
    c[:, IDC:IDC + 128][p, p] = 1.0
    c[64 + np.arange(64), LIC + np.arange(64)] = 1.0
    kp = p[:, None]
    key = np.arange(128)[None, :]
    c[:, MAC:MAC + 128] = np.where(kp <= key, -MS, 0.0)
    r = np.arange(896)[None, :] - 384
    c[:, MBW:MBW + 896] = np.where(kp > r, MS, 0.0)
    c[:, ONC:ONC + 128] = 1.0

    for e in range(8):
        c[:, OH8 + e * 8 + e] = 1.0
    c[0, LABC:LABC + 64] = 1.0
    c[32, LABC + 64:LABC + 128] = 1.0
    return c.astype(ml_dtypes.bfloat16)


def _constb2_host():
    c = np.zeros((8, 1024), np.float32)
    for e in range(8):
        c[e, e * 128:(e + 1) * 128] = 1.0
    return c.astype(ml_dtypes.bfloat16)


def stage_inputs(inp):
    n = 8
    x = np.asarray(inp["L0_attn_x"])
    w_attn = np.ascontiguousarray(np.asarray(inp["L0_attn_w_attn"]))
    w_proj = np.ascontiguousarray(np.asarray(inp["L0_attn_w_proj"]))
    s1t = np.ascontiguousarray(np.asarray(inp["L0_ffn_sw1"]).T)
    s2t = np.ascontiguousarray(np.asarray(inp["L0_ffn_sw2"]).T)
    s3t = np.ascontiguousarray(np.asarray(inp["L0_ffn_sw3"]).T)
    e1t = np.ascontiguousarray(np.asarray(inp["L0_ffn_ew1"]).transpose(0, 2, 1))
    e2t = np.ascontiguousarray(np.asarray(inp["L0_ffn_ew2"]).transpose(0, 2, 1))
    e3t = np.ascontiguousarray(np.asarray(inp["L0_ffn_ew3"]).transpose(0, 2, 1))
    gwt = np.zeros((D, 16), np.float32)
    gwt[:, 0] = np.asarray(inp["L0_ffn_sg_w"])[0]
    gwt[:, 1:9] = np.asarray(inp["L0_ffn_eg_w"]).T
    eb3_in = np.ascontiguousarray(np.asarray(inp["L0_ffn_eb3"]))

    smalls = np.zeros((128, 64), np.float32)
    for hc in range(4):
        smalls[:, SB1 + hc] = np.asarray(inp["L0_ffn_sb1"])[t_(hc, 128)]
        smalls[:, SB2 + hc] = np.asarray(inp["L0_ffn_sb2"])[t_(hc, 128)]
        smalls[:, SLG + hc] = np.asarray(inp["L0_ffn_s_ln_g"])[t_(hc, 128)]
        smalls[:, SLB + hc] = np.asarray(inp["L0_ffn_s_ln_b"])[t_(hc, 128)]
        smalls[:, SELB + hc] = np.asarray(inp["L0_ffn_sel_ln_b"])[
            t_(hc, 128)]
        smalls[:, SELG + hc] = np.asarray(inp["L0_ffn_sel_ln_g"])[
            t_(hc, 128)]
    for dc in range(8):
        smalls[:, SB3 + dc] = np.asarray(inp["L0_ffn_sb3"])[t_(dc, 128)]
    smalls[0:64, GAM] = np.asarray(inp["L0_attn_gamma"])

    smalls2 = np.zeros((128, 64), np.float32)
    eb1 = np.asarray(inp["L0_ffn_eb1"])
    eb2 = np.asarray(inp["L0_ffn_eb2"])
    for e in range(8):
        for hc in range(4):
            smalls2[:, e * 4 + hc] = eb1[e][t_(hc, 128)]
            smalls2[:, 32 + e * 4 + hc] = eb2[e][t_(hc, 128)]

    constb = _consts_host()
    constf = np.zeros((128, 168), np.float32)
    constf[:, 160] = EPS
    constf[:, 161] = 1e-12
    constf[:, IOC:IOC + 8] = np.arange(8)[None, :]
    constf[0, GBC] = np.asarray(inp["L0_ffn_sg_b"])[0]
    constf[0, GBC + 1:GBC + 9] = np.asarray(inp["L0_ffn_eg_b"])
    constf[:, EBC:EBC + 8] = np.asarray(inp["L0_ffn_exp_bias"])[None, :]
    constf[0, LQ1:LQ1 + 32] = np.asarray(inp["L0_attn_lambda_q1"])
    constf[0, LK1:LK1 + 32] = np.asarray(inp["L0_attn_lambda_k1"])
    constf[0, LQ2:LQ2 + 32] = np.asarray(inp["L0_attn_lambda_q2"])
    constf[0, LK2:LK2 + 32] = np.asarray(inp["L0_attn_lambda_k2"])
    constb2 = _constb2_host()

    index_b = np.asarray(inp["L0_ffn_index_b"]).astype(np.int64)
    index_t = np.asarray(inp["L0_ffn_index_t"]).astype(np.int64)
    gidxs_percore = []
    for c in range(n):
        gi = np.full((128, 16), T, np.uint32)
        for m in range(M):
            if index_b[m] == c:
                gi[m % 128, m // 128] = index_t[m]
        gidxs_percore.append(gi)
    rids = []
    for c in range(n):
        rev = np.full(T, T, np.int64)
        own = np.nonzero(index_b == c)[0]
        rev[index_t[own]] = own
        ri = np.zeros((128, 16), np.uint32)
        for t in range(T):
            ri[t % 128, t // 128] = rev[t]
        rids.append(ri)

    shared = dict(w_attn=w_attn, w_proj=w_proj, s1t=s1t, s2t=s2t, s3t=s3t,
                  e1t=e1t, e2t=e2t, e3t=e3t, gwt=gwt, eb3_in=eb3_in,
                  smalls=smalls, smalls2=smalls2, constb=constb,
                  constf=constf, constb2=constb2)
    in_maps = []
    for c in range(n):
        m = dict(shared)
        m["x_in"] = np.ascontiguousarray(x[c].T)
        m["ridx"] = rids[c]
        m["gidx"] = gidxs_percore[c]
        in_maps.append(m)
    return in_maps


def kernel(**inp):
    n = 8
    in_maps = stage_inputs(inp)
    nc = build_program()
    res = run_bass_kernel_spmd(nc, in_maps, core_ids=list(range(n)))
    out = np.stack([np.asarray(res.results[c]["y_out"]).T
                    for c in range(n)], axis=0)
    return out
